# version 06
# baseline (speedup 1.0000x reference)
from contextlib import ExitStack

import numpy as np
import ml_dtypes

import concourse.bass as bass
import concourse.mybir as mybir
from concourse.bass_utils import run_bass_kernel_spmd

F32 = mybir.dt.float32
BF16 = mybir.dt.bfloat16
U8 = mybir.dt.uint8
AF = mybir.ActivationFunctionType
ALU = mybir.AluOpType
AX = mybir.AxisListType
BF = ml_dtypes.bfloat16

B, L, D = 4, 4096, 2048
DH = 1024
NST = 16
TC = 256
NT = L // TC
TG = 256
NG = 8
SHLAG = 6

C_EPS, C_INVD, C_NEG1, C_ONE, C_NEGH, C_NEG128, C_HALF = range(7)
NCONST = 8


class Sem:
    def __init__(self, es, nc, name):
        self.h = es.enter_context(nc.semaphore(name))
        self.n = 0

    def inc(self, instr, k=1):
        instr.then_inc(self.h, k)
        self.n += k
        return self.n


class _MockInstr:
    def then_inc(self, h, k):
        return self


class _MockEng:
    def __getattr__(self, name):
        return lambda *a, **kw: _MockInstr()


_UNCHAINED = {"wait_ge", "sem_inc", "dma_start", "collective_compute",
              "then_inc", "nop", "engine_nop"}


class ChainEng:

    def __init__(self, eng, sem):
        object.__setattr__(self, "_e", eng)
        object.__setattr__(self, "_s", sem)
        object.__setattr__(self, "_lag", 1)

    def set_lag(self, k):
        object.__setattr__(self, "_lag", k)

    def __getattr__(self, name):
        f = getattr(self._e, name)
        if name in _UNCHAINED or name.startswith("_"):
            return f

        def g(*a, **kw):
            w = self._s.n - (self._lag - 1)
            if w > 0:
                self._e.wait_ge(self._s.h, w)
            r = f(*a, **kw)
            self._s.inc(r, 1)
            return r
        return g


def build_nc():
    nc = bass.Bass("TRN2", target_bir_lowering=False)

    xb = nc.dram_tensor("xb", [L, D], F32, kind="ExternalInput").ap()
    x0m = nc.dram_tensor("x0m", [NG, TG, D], F32, kind="ExternalInput").ap()
    wdt = nc.dram_tensor("wdt", [8, 128, 16, 128], F32, kind="ExternalInput").ap()
    wxp = nc.dram_tensor("wxp", [128, 16, 2 * NST], F32, kind="ExternalInput").ap()
    wout = nc.dram_tensor("wout", [8, 128, 8, 256], F32, kind="ExternalInput").ap()
    alog = nc.dram_tensor("alog", [128, 128], F32, kind="ExternalInput").ap()
    dtb = nc.dram_tensor("dtb", [128, 8], F32, kind="ExternalInput").ap()
    xpb = nc.dram_tensor("xpb", [2 * NST, 1], F32, kind="ExternalInput").ap()
    outb_bc = nc.dram_tensor("outb_bc", [128, D], F32, kind="ExternalInput").ap()
    wg8 = nc.dram_tensor("wg8", [64, 128, 16, 128], U8, kind="ExternalInput").ap()
    wu8 = nc.dram_tensor("wu8", [64, 128, 16, 128], U8, kind="ExternalInput").ap()
    wd8 = nc.dram_tensor("wd8", [4, 64, 128, 512], U8, kind="ExternalInput").ap()
    gsc = nc.dram_tensor("gsc", [128, 64], F32, kind="ExternalInput").ap()
    usc = nc.dram_tensor("usc", [128, 64], F32, kind="ExternalInput").ap()
    dsc_bc = nc.dram_tensor("dsc_bc", [128, D], F32, kind="ExternalInput").ap()
    ident = nc.dram_tensor("ident", [128, 128], BF16, kind="ExternalInput").ap()
    onesc = nc.dram_tensor("onesc", [128, 1], BF16, kind="ExternalInput").ap()
    onesr = nc.dram_tensor("onesr", [1, 512], BF16, kind="ExternalInput").ap()
    consts = nc.dram_tensor("consts", [128, NCONST], F32, kind="ExternalInput").ap()

    out = nc.dram_tensor("out", [NG, TG, D], F32, kind="ExternalOutput").ap()

    wg_bf = nc.dram_tensor("wg_bf", [64, 128, 16, 128], BF16)
    wu_bf = nc.dram_tensor("wu_bf", [64, 128, 16, 128], BF16)
    wd_bf = nc.dram_tensor("wd_bf", [4, 64, 128, 512], BF16)
    bc_d = nc.dram_tensor("bc_d", [2, 2 * NST, TC], BF16)
    mixp_d = nc.dram_tensor("mixp_d", [L, D], BF16)
    mixg_d = nc.dram_tensor("mixg_d", [NG, TG, D], BF16)

    es = ExitStack()
    sb = lambda name, shape, dt: es.enter_context(nc.sbuf_tensor(name, shape, dt))

    wxp_sb = sb("wxp_sb", [128, 16, 2 * NST], BF16)
    a_sb = sb("a_sb", [128, 128], F32)
    dtb_sb = sb("dtb_sb", [128, 8], F32)
    xpb_sb = sb("xpb_sb", [2 * NST, 1], F32)
    gsc_sb = sb("gsc_sb", [128, 64], F32)
    usc_sb = sb("usc_sb", [128, 64], F32)
    ident_sb = sb("ident_sb", [128, 128], BF16)
    onesc_sb = sb("onesc_sb", [128, 1], BF16)
    onesr_sb = sb("onesr_sb", [1, 512], BF16)
    ct = sb("ct", [128, NCONST], F32)
    sx = sb("sx", [128, 2], F32)
    sx2 = sb("sx2", [128, 2, 4], F32)
    smean = sb("smean", [128, 2], F32)
    snvar = sb("snvar", [128, 2], F32)
    srstd = sb("srstd", [128, 2], F32)
    snegmr = sb("snegmr", [128, 2], F32)
    scr = sb("scr", [128, 512], F32)
    outbh_sb = sb("outbh_sb", [128, D], BF16)
    dsc_sb = sb("dsc_sb", [128, D], BF16)
    rsn_sb = sb("rsn_sb", [1, TG], BF16)
    shn_sb = sb("shn_sb", [1, TG], BF16)

    x0c = sb("x0c", [128, D], BF16)
    ln1_tm = sb("ln1tm", [128, D], BF16)
    ln1_fm = sb("ln1fm", [128, 16, TC], BF16)
    wdtb = [sb(f"wdtb{i}", [128, 16, 128], BF16) for i in range(2)]
    dt_sb = sb("dt_sb", [128, 8, TC], BF16)
    u_sb = sb("u_sb", [128, 8, TC], BF16)
    bc_sb = [sb(f"bcs{i}", [2 * NST, TC], BF16) for i in range(2)]
    b_bc = sb("b_bc", [128, NST, TC], BF16)
    c_bc = sb("c_bc", [128, NST, TC], BF16)
    da_sb = [sb(f"das{i}", [128, TC], F32) for i in range(2)]
    dbx_sb = [sb(f"dbxs{i}", [128, TC], BF16) for i in range(2)]
    h_all = sb("h_all", [128, NST, TC], BF16)
    hstate = sb("hstate", [128, 128], F32)
    y32 = sb("y32", [128, TC], F32)
    y_bf = [sb(f"ybfs{i}", [128, 8, TC], BF16) for i in range(2)]
    woutb = [sb(f"woutbs{i}", [128, 8, 256], BF16) for i in range(2)]
    mixp_sb = sb("mixp_sb", [128, 2, D], BF16)
    mixg_t = sb("mixg_t", [128, D], BF16)
    x0m_t = sb("x0m_t", [128, D], BF16)
    x1_sb = sb("x1_sb", [128, 2, D], BF16)
    ln2_tm = sb("ln2_tm", [128, 2, D], BF16)
    ln2_fm = sb("ln2_fm", [128, 16, TG], BF16)
    wgb = [sb(f"wgbs{i}", [128, 16, 128], BF16) for i in range(2)]
    wub = [sb(f"wubs{i}", [128, 16, 128], BF16) for i in range(2)]
    wdb = [sb(f"wdbs{i}", [128, 4, 512], BF16) for i in range(2)]
    g_b = [sb(f"g_bs{i}", [128, TG], BF16) for i in range(2)]
    u_b = [sb(f"u_bs{i}", [128, TG], BF16) for i in range(2)]
    hh_sb = sb("hh_sb", [128, 64, TG], BF16)
    ffn32 = sb("ffn32", [128, 2, D], F32)

    ps = [es.enter_context(nc.psum_tensor(f"ps{i}", [128, 512], F32)) for i in range(6)]
    pt0 = es.enter_context(nc.psum_tensor("pt0", [128, 1024], BF16))
    psr = es.enter_context(nc.psum_tensor("psr", [128, 512], F32))

    sConst = Sem(es, nc, "sConst")
    sPre = Sem(es, nc, "sPre")
    sPc = Sem(es, nc, "sPc")
    sX0 = Sem(es, nc, "sX0")
    sWdt = [Sem(es, nc, f"sWdt{i}") for i in range(2)]
    sWo = [Sem(es, nc, f"sWo{i}") for i in range(2)]
    sMixp = Sem(es, nc, "sMixp")
    sBc = Sem(es, nc, "sBc")
    sBcb = Sem(es, nc, "sBcb")
    sMg = Sem(es, nc, "sMg")
    sX0m = Sem(es, nc, "sX0m")
    sGU = [Sem(es, nc, f"sGU{i}") for i in range(2)]
    sWd = [Sem(es, nc, f"sWd{i}") for i in range(2)]
    sX1s = [Sem(es, nc, f"sX1s{i}") for i in range(2)]
    sST = Sem(es, nc, "sST")
    sCC = Sem(es, nc, "sCC")
    sPE = Sem(es, nc, "sPE")
    sVE = Sem(es, nc, "sVE")
    sSC = Sem(es, nc, "sSC")
    ALL_SEMS = ([sConst, sPre, sPc, sMixp, sBc, sBcb, sX0m, sST, sCC,
                 sPE, sVE, sSC] + [sX0] + sWdt + sWo + [sMg] + sGU + sWd + sX1s)

    ev = {}
    sched = {"sync": [], "gpsimd": [], "scalar": [], "vector": [], "tensor": []}
    order = []

    def em(engine, fn):
        sched[engine].append(fn)
        order.append((engine, fn))

    def snap(key, sem):
        order.append(("__snap__", (key, sem)))

    def wait(eng, key):
        sem, n = ev[key]
        if n > 0:
            eng.wait_ge(sem.h, n)

    class Pool:

        def __init__(self):
            self.q = [(b, None) for b in range(6)]
            self.log = []
            self.replay = False
            self.ri = 0

        def take(self, eng):
            if self.replay:
                b, key = self.log[self.ri]
                self.ri += 1
            else:
                b, key = self.q.pop(0)
                self.log.append((b, key))
            if key is not None:
                wait(eng, key)
            return b

        def give(self, b, key):
            if not self.replay:
                self.q.append((b, key))

    pool = Pool()

    with es:
        blk = es.enter_context(nc.Block())

        def _loads(sync):
            for ap_in, tile in ((alog, a_sb), (dtb, dtb_sb), (xpb, xpb_sb),
                                (gsc, gsc_sb), (usc, usc_sb), (ident, ident_sb),
                                (onesc, onesc_sb), (onesr, onesr_sb), (consts, ct)):
                sConst.inc(sync.dma_start(out=tile[:], in_=ap_in[:]), 16)
        em("sync", _loads)
        snap("loads0", sConst)

        def _pre(gpsimd):
            sPre.inc(gpsimd.dma_start(out=wxp_sb[:].rearrange('p k c -> p (k c)'),
                                      in_=wxp[:].rearrange('p k c -> p (k c)')), 16)
            sPre.inc(gpsimd.dma_start(out=outbh_sb[:], in_=outb_bc[:]), 16)
            sPre.inc(gpsimd.dma_start(out=dsc_sb[:], in_=dsc_bc[:]), 16)
        em("gpsimd", _pre)
        snap("pre", sPre)

        def _precast(gpsimd):
            def flat(ap, cols=32768):
                return ap.rearrange(
                    " ".join(f"d{i}" for i in range(len(ap.shape)))
                    + " -> (" + " ".join(f"d{i}" for i in range(len(ap.shape)))
                    + ")").rearrange("(x y) -> x y", y=cols)
            for srcw, dst in ((wg8, wg_bf), (wu8, wu_bf)):
                for q in range(4):
                    sPc.inc(gpsimd.dma_start(
                        out=flat(dst.ap()[q * 16:(q + 1) * 16]),
                        in_=flat(srcw[q * 16:(q + 1) * 16])), 16)
            for ho in range(4):
                sPc.inc(gpsimd.dma_start(out=flat(wd_bf.ap()[ho]),
                                         in_=flat(wd8[ho])), 16)
        em("gpsimd", _precast)
        snap("precast", sPc)

        def _prep_sc(scalar):
            wait(scalar, "loads0")
            scalar.activation(out=a_sb[:], in_=a_sb[:], func=AF.Exp)
        em("scalar", _prep_sc)
        snap("aexp", sSC)

        def _prep_ve(vector):
            vector.set_lag(1)
            wait(vector, "aexp")
            wait(vector, "pre")
            vector.tensor_scalar(out=a_sb[:], in0=a_sb[:],
                                 scalar1=ct[:, C_NEG1:C_NEG1 + 1],
                                 scalar2=None, op0=ALU.mult)
            vector.tensor_scalar(out=outbh_sb[:], in0=outbh_sb[:],
                                 scalar1=ct[:, C_HALF:C_HALF + 1],
                                 scalar2=None, op0=ALU.mult)
        em("vector", _prep_ve)
        snap("prep", sVE)

        ln_ctr = [0]
        ln_norm_key = {}

        def emit_ln_stats(src_ap_fn):
            s = ln_ctr[0] % 2
            i = ln_ctr[0]
            ln_ctr[0] += 1

            def _v1(vector, i=i, s=s):
                vector.set_lag(1)
                if i >= 2:
                    wait(vector, f"lnstat_done_{i - 2}")
                vector.tensor_reduce(out=sx[:, s:s + 1], in_=src_ap_fn(),
                                     axis=AX.X, op=ALU.add)
            em("vector", _v1)

            def _s1(scalar, i=i, s=s):
                if i >= 2:
                    wait(scalar, f"lnstat_done_{i - 2}")
                src = src_ap_fn()
                for q in range(4):
                    scalar.activation(out=scr[:], in_=src[:, q * 512:(q + 1) * 512],
                                      func=AF.Square, accum_out=sx2[:, s, q:q + 1])
            em("scalar", _s1)
            snap(f"ln_q_{i}", sSC)

            def _v2(vector, i=i, s=s):
                vector.set_lag(1)
                wait(vector, f"ln_q_{i}")
                vector.tensor_scalar(out=smean[:, s:s + 1], in0=sx[:, s:s + 1],
                                     scalar1=ct[:, C_INVD:C_INVD + 1], scalar2=None,
                                     op0=ALU.mult)
                vector.tensor_reduce(out=snvar[:, s:s + 1], in_=sx2[:, s, :],
                                     axis=AX.X, op=ALU.add)
                vector.tensor_scalar(out=snvar[:, s:s + 1], in0=snvar[:, s:s + 1],
                                     scalar1=ct[:, C_INVD:C_INVD + 1], scalar2=None,
                                     op0=ALU.mult)
                vector.scalar_tensor_tensor(out=snvar[:, s:s + 1],
                                            in0=smean[:, s:s + 1],
                                            scalar=smean[:, s:s + 1],
                                            in1=snvar[:, s:s + 1],
                                            op0=ALU.mult, op1=ALU.subtract)
            em("vector", _v2)
            snap(f"ln_nvar_{i}", sVE)

            def _s2(scalar, i=i, s=s):
                wait(scalar, f"ln_nvar_{i}")
                scalar.activation(out=srstd[:, s:s + 1], in_=snvar[:, s:s + 1],
                                  func=AF.Ln, scale=ct[:, C_NEG1:C_NEG1 + 1],
                                  bias=ct[:, C_EPS:C_EPS + 1])
                scalar.activation(out=srstd[:, s:s + 1], in_=srstd[:, s:s + 1],
                                  func=AF.Exp, scale=ct[:, C_NEGH:C_NEGH + 1])
            em("scalar", _s2)
            snap(f"ln_rstd_{i}", sSC)

            def _v3(vector, i=i, s=s):
                vector.set_lag(1)
                wait(vector, f"ln_rstd_{i}")
                if i - 2 in ln_norm_key:
                    wait(vector, ln_norm_key[i - 2])
                vector.scalar_tensor_tensor(out=snegmr[:, s:s + 1],
                                            in0=smean[:, s:s + 1],
                                            scalar=ct[:, C_NEG1:C_NEG1 + 1],
                                            in1=srstd[:, s:s + 1],
                                            op0=ALU.mult, op1=ALU.mult)
            em("vector", _v3)
            snap(f"lnstat_done_{i}", sVE)
            return s, i

        tg_ctr = [0]

        def emit_transposes(src_fn, dst_fn, ready_key, extra_key=None):
            for grp in range(2):
                tg = tg_ctr[0]
                tg_ctr[0] += 1

                def _tr(tensor, grp=grp, tg=tg):
                    if tg == 0:
                        wait(tensor, "loads0")
                    wait(tensor, ready_key)
                    if extra_key is not None and grp == 0:
                        wait(tensor, extra_key)
                    if tg >= 1:
                        wait(tensor, f"trev_{tg - 1}")
                    last = None
                    for dcq in range(8):
                        last = tensor.transpose(pt0[:, dcq * 128:(dcq + 1) * 128],
                                                src_fn(grp * 8 + dcq), ident_sb[:])
                    sPE.inc(last)
                em("tensor", _tr)
                snap(f"trmm_{tg}", sPE)

                def _trev(scalar, grp=grp, tg=tg):
                    wait(scalar, f"trmm_{tg}")
                    scalar.copy(dst_fn(grp),
                                pt0[:].rearrange("p (dc t) -> p dc t", dc=8))
                em("scalar", _trev)
                snap(f"trev_{tg}", sSC)

        def emit_A_piece(c, piece):
            if c >= NT:
                return
            if piece in (0, 2):
                si = piece // 2
                sig = c * 2 + si

                def _ld(gpsimd, sig=sig, c=c, si=si):
                    if sig >= 1:
                        wait(gpsimd, f"ln1tm_{sig - 1}")
                    r0 = c * TC + si * 128
                    sX0.inc(gpsimd.dma_start(out=x0c[:],
                                             in_=xb[r0:r0 + 128, :]), 16)
                em("gpsimd", _ld)
                snap(f"x0ld_{sig}", sX0)

                def _wv(vector, k=f"x0ld_{sig}"):
                    wait(vector, k)
                em("vector", _wv)

                def _ws(scalar, k=f"x0ld_{sig}"):
                    wait(scalar, k)
                em("scalar", _ws)
                slot, li = emit_ln_stats(lambda: x0c[:])

                def _norm(scalar, slot=slot, li=li, sig=sig):
                    wait(scalar, f"lnstat_done_{li}")
                    if sig >= 1:
                        wait(scalar, f"ln1tr_{sig - 1}")
                    scalar.activation(out=ln1_tm[:], in_=x0c[:],
                                      func=AF.Identity,
                                      scale=srstd[:, slot:slot + 1],
                                      bias=snegmr[:, slot:slot + 1])
                em("scalar", _norm)
                snap(f"ln1tm_{sig}", sSC)
                ln_norm_key[li] = f"ln1tm_{sig}"

            elif piece in (1, 3):
                si = piece // 2
                sig = c * 2 + si
                emit_transposes(
                    (lambda dc: ln1_tm[:, dc * 128:(dc + 1) * 128]),
                    (lambda grp, si=si: ln1_fm[:, grp * 8:grp * 8 + 8,
                                               si * 128:(si + 1) * 128]),
                    f"ln1tm_{sig}",
                    f"udone_{c - 1}" if (si == 0 and c >= 1) else None)
                snap(f"ln1tr_{sig}", sPE)
                if si == 1:
                    snap(f"ln1fm_{c}", sSC)

            else:
                m0 = 2 * (piece - 4)
                for m in (m0, m0 + 1):
                    gdm = c * 8 + m

                    def _wdl(gpsimd, c=c, m=m, gdm=gdm):
                        if gdm >= 2:
                            wait(gpsimd, f"dtmm_{gdm - 2}")
                        sWdt[m % 2].inc(gpsimd.dma_start(
                            out=wdtb[m % 2][:].rearrange('p k c -> p (k c)'),
                            in_=wdt[m].rearrange('p k c -> p (k c)')), 16)
                    em("gpsimd", _wdl)
                    snap(f"wdtl_{gdm}", sWdt[m % 2])

                    hold = {}

                    def _mm(tensor, c=c, m=m, gdm=gdm, hold=hold):
                        wait(tensor, f"wdtl_{gdm}")
                        wait(tensor, f"ln1fm_{c}")
                        b = hold.setdefault("b", pool.take(tensor))
                        if pool.replay and "taken2" not in hold:
                            hold["taken2"] = True
                        last = None
                        for kk in range(16):
                            last = tensor.matmul(ps[b][:, 0:TC],
                                                 wdtb[m % 2][:, kk, :],
                                                 ln1_fm[:, kk, :],
                                                 start=(kk == 0), stop=(kk == 15))
                        sPE.inc(last)
                    em("tensor", _mm)
                    snap(f"dtmm_{gdm}", sPE)

                    def _sp(scalar, c=c, m=m, gdm=gdm, hold=hold):
                        wait(scalar, f"dtmm_{gdm}")
                        if c >= 1 and m == 0:
                            wait(scalar, f"udone_{c - 1}")
                        b = hold["b"]
                        e = scr[:, (m % 2) * 256:(m % 2) * 256 + 256]
                        scalar.activation(out=e, in_=ps[b][:, 0:TC], func=AF.Exp,
                                          bias=dtb_sb[:, m:m + 1])
                        scalar.activation(out=dt_sb[:, m, :], in_=e, func=AF.Ln,
                                          bias=ct[:, C_ONE:C_ONE + 1])
                    em("scalar", _sp)
                    snap(f"dtev_{gdm}", sSC)

                    def _give(hold=hold, key=f"dtev_{gdm}"):
                        pool.give(hold["b"], key)
                    order.append(("__call__", _give))

                if piece == 7:
                    holdx = {}

                    def _xp(tensor, c=c, hold=holdx):
                        wait(tensor, f"ln1fm_{c}")
                        b = hold.setdefault("b", pool.take(tensor))
                        last = None
                        for kk in range(16):
                            last = tensor.matmul(ps[b][0:2 * NST, 0:TC],
                                                 wxp_sb[:, kk, :], ln1_fm[:, kk, :],
                                                 start=(kk == 0), stop=(kk == 15))
                        sPE.inc(last)
                    em("tensor", _xp)
                    snap(f"xpmm_{c}", sPE)

                    def _xpe(scalar, c=c, hold=holdx):
                        wait(scalar, f"xpmm_{c}")
                        if c >= 2:
                            wait(scalar, f"bcst_{c - 2}")
                        scalar.activation(out=bc_sb[c % 2][:],
                                          in_=ps[hold["b"]][0:2 * NST, 0:TC],
                                          func=AF.Identity, bias=xpb_sb[:])
                    em("scalar", _xpe)
                    snap(f"bcev_{c}", sSC)

                    def _givexp(hold=holdx, key=f"bcev_{c}"):
                        pool.give(hold["b"], key)
                    order.append(("__call__", _givexp))

                    def _u(vector, c=c):
                        vector.set_lag(1)
                        wait(vector, f"dtev_{c * 8 + 7}")
                        for m in range(8):
                            vector.tensor_tensor(out=u_sb[:, m, :],
                                                 in0=dt_sb[:, m, :],
                                                 in1=ln1_fm[:, m, :], op=ALU.mult)
                    em("vector", _u)
                    snap(f"udone_{c}", sVE)

                    def _bcst(gpsimd, c=c):
                        wait(gpsimd, f"bcev_{c}")
                        sBc.inc(gpsimd.dma_start(out=bc_d.ap()[c % 2],
                                                 in_=bc_sb[c % 2][:]), 16)
                    em("gpsimd", _bcst)
                    snap(f"bcst_{c}", sBc)

                    def _bcb(gpsimd, c=c):
                        wait(gpsimd, f"bcst_{c}")
                        if c >= 1:
                            wait(gpsimd, f"scanchunk_{c - 1}")
                        base_el = (c % 2) * 2 * NST * TC
                        repb = bass.AP(bc_d.ap().tensor, base_el,
                                       [[0, 128], [TC, NST], [1, TC]])
                        repc = bass.AP(bc_d.ap().tensor, base_el + NST * TC,
                                       [[0, 128], [TC, NST], [1, TC]])
                        sBcb.inc(gpsimd.dma_start(out=b_bc[:], in_=repb), 16)
                        sBcb.inc(gpsimd.dma_start(out=c_bc[:], in_=repc), 16)
                    em("gpsimd", _bcb)
                    snap(f"bcast_{c}", sBcb)

        def emit_A(c):
            for piece in range(8):
                emit_A_piece(c, piece)

        def emit_scan_m(c, m):
            for np_ in range(NST // 2):
                n0 = 2 * np_
                n1 = n0 + 1
                i0 = c * 128 + m * 16 + n0
                i1 = i0 + 1

                def _da2(scalar, c=c, m=m, n0=n0, n1=n1, i0=i0, i1=i1):
                    if i0 == 0:
                        wait(scalar, "prep")
                    if i0 >= 2:
                        wait(scalar, f"scan_{i0 - 2}")
                    scalar.activation(
                        out=da_sb[i0 % 2][:], in_=dt_sb[:, m, :], func=AF.Exp,
                        scale=a_sb[:, m * 16 + n0:m * 16 + n0 + 1])
                    if i1 >= 2:
                        wait(scalar, f"scan_{i1 - 2}")
                    scalar.activation(
                        out=da_sb[i1 % 2][:], in_=dt_sb[:, m, :], func=AF.Exp,
                        scale=a_sb[:, m * 16 + n1:m * 16 + n1 + 1])
                em("scalar", _da2)
                snap(f"da_{i0}", sSC)
                snap(f"da_{i1}", sSC)

                def _sc2(vector, c=c, m=m, n0=n0, n1=n1, i0=i0, i1=i1):
                    vector.set_lag(2)
                    if n0 == 0 and m == 0:
                        wait(vector, f"bcast_{c}")
                        wait(vector, f"udone_{c}")
                    vector.tensor_tensor(out=dbx_sb[0][:], in0=u_sb[:, m, :],
                                         in1=b_bc[:, n0, :], op=ALU.mult)
                    vector.tensor_tensor(out=dbx_sb[1][:], in0=u_sb[:, m, :],
                                         in1=b_bc[:, n1, :], op=ALU.mult)
                    wait(vector, f"da_{i1}")
                    k0 = m * 16 + n0
                    k1 = m * 16 + n1
                    init0 = 0.0 if c == 0 else hstate[:, k0:k0 + 1]
                    init1 = 0.0 if c == 0 else hstate[:, k1:k1 + 1]
                    vector.tensor_tensor_scan(out=h_all[:, n0, :],
                                              data0=da_sb[i0 % 2][:],
                                              data1=dbx_sb[0][:], initial=init0,
                                              op0=ALU.mult, op1=ALU.add)
                    vector.tensor_tensor_scan(out=h_all[:, n1, :],
                                              data0=da_sb[i1 % 2][:],
                                              data1=dbx_sb[1][:], initial=init1,
                                              op0=ALU.mult, op1=ALU.add)
                    vector.tensor_copy(hstate[:, k0:k0 + 1], h_all[:, n0, TC - 1:TC])
                    vector.tensor_copy(hstate[:, k1:k1 + 1], h_all[:, n1, TC - 1:TC])
                    vector.tensor_tensor(out=h_all[:, n0, :], in0=h_all[:, n0, :],
                                         in1=c_bc[:, n0, :], op=ALU.mult)
                    vector.tensor_tensor(out=h_all[:, n1, :], in0=h_all[:, n1, :],
                                         in1=c_bc[:, n1, :], op=ALU.mult)
                em("vector", _sc2)
                snap(f"scan_{i0}", sVE)
                snap(f"scan_{i1}", sVE)

            gm = c * 8 + m

            def _red(vector, gm=gm):
                vector.set_lag(1)
                if gm >= 1:
                    wait(vector, f"ycst_{gm - 1}")
                vector.tensor_reduce(out=y32[:],
                                     in_=h_all[:].rearrange("p n t -> p t n"),
                                     axis=AX.X, op=ALU.add)
            em("vector", _red)
            snap(f"yred_{gm}", sVE)

            def _yc(scalar, c=c, m=m, gm=gm):
                wait(scalar, f"yred_{gm}")
                if c >= 2 and m == 0:
                    wait(scalar, f"opdone_{c - 2}")
                scalar.copy(y_bf[c % 2][:, m, :], y32[:])
            em("scalar", _yc)
            snap(f"ycst_{gm}", sSC)

        def emit_outproj(cc):
            for mo in range(8):
                def _wl(gpsimd, cc=cc, mo=mo):
                    gi4 = cc * 8 + mo
                    if gi4 >= 2:
                        wait(gpsimd, f"opmm_{cc}_{(mo - 2) * 2 + 1}" if mo >= 2
                             else f"opmm_{cc - 1}_{(mo + 6) * 2 + 1}")
                    sWo[mo % 2].inc(gpsimd.dma_start(
                        out=woutb[mo % 2][:].rearrange('p k c -> p (k c)'),
                        in_=wout[mo].rearrange('p k c -> p (k c)')), 16)
                em("gpsimd", _wl)
                snap(f"opwl_{cc}_{mo}", sWo[mo % 2])

                for tt in range(2):
                    gi = mo * 2 + tt

                    holdg = {}

                    def _g(tensor, cc=cc, tt=tt, mo=mo, gi=gi, hold=holdg):
                        if gi == 0:
                            wait(tensor, f"ycast_{cc}")
                            wait(tensor, "prep")
                        if tt == 0:
                            wait(tensor, f"opwl_{cc}_{mo}")
                        b = hold.setdefault("b", pool.take(tensor))
                        last = None
                        for kk in range(8):
                            last = tensor.matmul(
                                ps[b][:, 0:256],
                                y_bf[cc % 2][:, kk, tt * 128:(tt + 1) * 128],
                                woutb[mo % 2][:, kk, :],
                                start=(kk == 0), stop=(kk == 7))
                        sPE.inc(last)
                    em("tensor", _g)
                    snap(f"opmm_{cc}_{gi}", sPE)

                    def _e(vector, cc=cc, tt=tt, mo=mo, gi=gi, hold=holdg):
                        vector.set_lag(1)
                        wait(vector, f"opmm_{cc}_{gi}")
                        if gi == 0 and cc > 0:
                            wait(vector, f"mixdma_{cc - 1}")
                        vector.tensor_tensor(
                            out=mixp_sb[:, tt, mo * 256:(mo + 1) * 256],
                            in0=ps[hold["b"]][:, 0:256],
                            in1=outbh_sb[:, mo * 256:(mo + 1) * 256], op=ALU.add)
                    em("vector", _e)
                    snap(f"opev_{cc}_{gi}", sVE)

                    def _giveop(hold=holdg, key=f"opev_{cc}_{gi}"):
                        pool.give(hold["b"], key)
                    order.append(("__call__", _giveop))
            snap(f"mixevac_{cc}", sVE)
            snap(f"opdone_{cc}", sPE)

            def _d(sync, cc=cc):
                wait(sync, f"mixevac_{cc}")
                dst = mixp_d.ap()[cc * TC:(cc + 1) * TC, :].rearrange(
                    "(tt p) m -> p tt m", p=128)
                sMixp.inc(sync.dma_start(out=dst, in_=mixp_sb[:]), 16)
            em("sync", _d)
            snap(f"mixdma_{cc}", sMixp)

            if cc % 2 == 1:
                gg = cc // 2

                def _rs(gpsimd, cc=cc, gg=gg):
                    wait(gpsimd, f"mixdma_{cc}")
                    sCC.inc(gpsimd.collective_compute(
                        "ReduceScatter", ALU.add,
                        replica_groups=[[0, 1], [2, 3], [4, 5], [6, 7]],
                        ins=[mixp_d.ap()[gg * 512:(gg + 1) * 512, :]],
                        outs=[mixg_d.ap()[gg]]))
                em("gpsimd", _rs)
                snap(f"rs_{gg}", sCC)

        def emit_C(g):
            for tt in range(2):
                ti = g * 2 + tt

                def _gldx(gpsimd, g=g, tt=tt, ti=ti):
                    if ti >= 1:
                        wait(gpsimd, f"x1_{(ti - 1) // 2}_{(ti - 1) % 2}")
                    sX0m.inc(gpsimd.dma_start(
                        out=x0m_t[:],
                        in_=x0m[g][tt * 128:(tt + 1) * 128, :]), 16)
                em("gpsimd", _gldx)
                snap(f"gldx_{ti}", sX0m)

                def _gld(sync, g=g, tt=tt, ti=ti):
                    wait(sync, f"rs_{g}")
                    if ti >= 1:
                        wait(sync, f"mixused_{ti - 1}")
                    sMg.inc(sync.dma_start(
                        out=mixg_t[:],
                        in_=mixg_d.ap()[g][tt * 128:(tt + 1) * 128, :]), 16)
                em("sync", _gld)
                snap(f"gldm_{ti}", sMg)

                def _wv(vector, k1=f"gldm_{ti}"):
                    wait(vector, k1)
                em("vector", _wv)

                def _ws(scalar, k1=f"gldm_{ti}"):
                    wait(scalar, k1)
                em("scalar", _ws)
                slot, li = emit_ln_stats(lambda: mixg_t[:])

                for q in range(4):
                    def _nmQ(scalar, ti=ti, slot=slot, li=li, q=q):
                        if q == 0:
                            wait(scalar, f"lnstat_done_{li}")
                            if ti >= 1:
                                wait(scalar, f"x1q_{ti - 1}_3")
                        else:
                            wait(scalar, f"x1q_{ti}_{q - 1}")
                        scalar.activation(out=scr[:],
                                          in_=mixg_t[:, q * 512:(q + 1) * 512],
                                          func=AF.Identity,
                                          scale=srstd[:, slot:slot + 1],
                                          bias=snegmr[:, slot:slot + 1])
                    em("scalar", _nmQ)
                    snap(f"lnmQ_{ti}_{q}", sSC)
                    if q == 3:
                        ln_norm_key[li] = f"lnmQ_{ti}_{q}"

                    def _x1Q(vector, g=g, tt=tt, ti=ti, q=q):
                        vector.set_lag(1)
                        wait(vector, f"lnmQ_{ti}_{q}")
                        if q == 0:
                            wait(vector, f"gldx_{ti}")
                            if g >= 1:
                                wait(vector, f"ln2tm_{g - 1}_{tt}")
                                wait(vector, f"x1st_{g - 1}_{tt}")
                        vector.tensor_tensor(
                            out=x1_sb[:, tt, q * 512:(q + 1) * 512],
                            in0=scr[:], in1=x0m_t[:, q * 512:(q + 1) * 512],
                            op=ALU.add)
                    em("vector", _x1Q)
                    snap(f"x1q_{ti}_{q}", sVE)
                snap(f"x1_{g}_{tt}", sVE)

                def _x1st(gpsimd, g=g, tt=tt):
                    wait(gpsimd, f"x1_{g}_{tt}")
                    sX1s[tt].inc(gpsimd.dma_start(
                        out=out[g][tt * 128:(tt + 1) * 128, :],
                        in_=x1_sb[:, tt, :]), 16)
                em("gpsimd", _x1st)
                snap(f"x1st_{g}_{tt}", sX1s[tt])

                def _ws2(scalar, k=f"x1_{g}_{tt}"):
                    wait(scalar, k)
                em("scalar", _ws2)
                slot2, li2 = emit_ln_stats(lambda tt=tt: x1_sb[:, tt, :])

                def _nm2(scalar, g=g, tt=tt, slot=slot2, li=li2):
                    wait(scalar, f"lnstat_done_{li}")
                    if g >= 1:
                        wait(scalar, f"ln2tr_{g - 1}")
                    scalar.activation(out=ln2_tm[:, tt, :],
                                      in_=x1_sb[:, tt, :],
                                      func=AF.Identity,
                                      scale=srstd[:, slot:slot + 1],
                                      bias=snegmr[:, slot:slot + 1])
                em("scalar", _nm2)
                snap(f"ln2tm_{g}_{tt}", sSC)
                ln_norm_key[li2] = f"ln2tm_{g}_{tt}"
                snap(f"mixused_{ti}", sSC)

            for tt in range(2):
                emit_transposes(
                    (lambda dc, tt=tt: ln2_tm[:, tt, dc * 128:(dc + 1) * 128]),
                    (lambda grp, tt=tt: ln2_fm[:, grp * 8:grp * 8 + 8,
                                               tt * 128:(tt + 1) * 128]),
                    f"ln2tm_{g}_{tt}",
                    f"ffdone_{g - 1}" if (tt == 0 and g >= 1) else None)
            snap(f"ln2tr_{g}", sPE)
            snap(f"ln2fm_{g}", sSC)

            def _rsum(tensor, g=g):
                wait(tensor, f"ln2fm_{g}")
                if g >= 1:
                    wait(tensor, f"shnev_{g - 1}")
                last = None
                for kk in range(16):
                    last = tensor.matmul(psr[0:1, 0:TG], onesc_sb[:, 0:1],
                                         ln2_fm[:, kk, :],
                                         start=(kk == 0), stop=(kk == 15))
                sPE.inc(last)
            em("tensor", _rsum)
            snap(f"rsum_{g}", sPE)

            def _rsev(scalar, g=g):
                wait(scalar, f"rsum_{g}")
                scalar.activation(out=rsn_sb[:], in_=psr[0:1, 0:TG],
                                  func=AF.Identity,
                                  scale=ct[0:1, C_NEG128:C_NEG128 + 1])
            em("scalar", _rsev)
            snap(f"rsnev_{g}", sSC)

            for ic in range(64):
                gic = g * 64 + ic

                def _wld(sync, g=g, ic=ic, gic=gic):
                    if gic == 0:
                        wait(sync, "precast")
                    if gic >= 2:
                        wait(sync, f"gumm_{gic - 2}")
                    sGU[ic % 2].inc(sync.dma_start(
                        out=wgb[ic % 2][:].rearrange('p k c -> p (k c)'),
                        in_=wg_bf.ap()[ic].rearrange('p k c -> p (k c)')), 16)
                    sGU[ic % 2].inc(sync.dma_start(
                        out=wub[ic % 2][:].rearrange('p k c -> p (k c)'),
                        in_=wu_bf.ap()[ic].rearrange('p k c -> p (k c)')), 16)
                em("sync", _wld)
                snap(f"wld_{gic}", sGU[ic % 2])

                holdgu = {}

                def _gu(tensor, g=g, ic=ic, gic=gic, hold=holdgu):
                    wait(tensor, f"wld_{gic}")
                    if ic == 0:
                        wait(tensor, f"rsnev_{g}")
                    if "b" not in hold:
                        hold["b"] = (pool.take(tensor), pool.take(tensor))
                    else:
                        pool.take(tensor)
                        pool.take(tensor)
                    bg, bu = hold["b"]
                    if ic >= SHLAG:
                        wait(tensor, f"hh_{g}_{ic - SHLAG}")
                        tensor.matmul(psr[0:1, TG:2 * TG], onesc_sb[:, 0:1],
                                      hh_sb[:, ic - SHLAG, :],
                                      start=(ic - SHLAG == 0), stop=False)
                    for kk in range(16):
                        tensor.matmul(ps[bg][:, 0:TG], wgb[ic % 2][:, kk, :],
                                      ln2_fm[:, kk, :], start=(kk == 0), stop=False)
                    tensor.matmul(ps[bg][:, 0:TG], onesr_sb[0:1, 0:128],
                                  rsn_sb[0:1, :], start=False, stop=True)
                    for kk in range(16):
                        tensor.matmul(ps[bu][:, 0:TG], wub[ic % 2][:, kk, :],
                                      ln2_fm[:, kk, :], start=(kk == 0), stop=False)
                    sPE.inc(tensor.matmul(ps[bu][:, 0:TG], onesr_sb[0:1, 0:128],
                                          rsn_sb[0:1, :], start=False, stop=True))
                em("tensor", _gu)
                snap(f"gumm_{gic}", sPE)

                def _guev(scalar, g=g, ic=ic, gic=gic, hold=holdgu):
                    wait(scalar, f"gumm_{gic}")
                    if ic >= 2:
                        wait(scalar, f"hh_{g}_{ic - 2}")
                    elif g >= 1:
                        wait(scalar, f"hh_{g - 1}_{62 + ic}")
                    bg, bu = hold["b"]
                    scalar.activation(out=g_b[ic % 2][:], in_=ps[bg][:, 0:TG],
                                      func=AF.Silu, scale=gsc_sb[:, ic:ic + 1])
                    scalar.activation(out=u_b[ic % 2][:], in_=ps[bu][:, 0:TG],
                                      func=AF.Identity, scale=usc_sb[:, ic:ic + 1])
                em("scalar", _guev)
                snap(f"guev_{gic}", sSC)

                def _givegu(hold=holdgu, key=f"guev_{gic}"):
                    bg, bu = hold["b"]
                    pool.give(bg, key)
                    pool.give(bu, key)
                order.append(("__call__", _givegu))

                def _hh(vector, g=g, ic=ic, gic=gic):
                    vector.set_lag(1)
                    wait(vector, f"guev_{gic}")
                    if ic == 0 and g >= 1:
                        wait(vector, f"downdone_{g - 1}")
                    vector.tensor_tensor(out=hh_sb[:, ic, :], in0=g_b[ic % 2][:],
                                         in1=u_b[ic % 2][:], op=ALU.mult)
                em("vector", _hh)
                snap(f"hh_{g}_{ic}", sVE)

            def _shf(tensor, g=g):
                last = None
                for ic in range(64 - SHLAG, 64):
                    wait(tensor, f"hh_{g}_{ic}")
                    last = tensor.matmul(psr[0:1, TG:2 * TG], onesc_sb[:, 0:1],
                                         hh_sb[:, ic, :], start=False,
                                         stop=(ic == 63))
                sPE.inc(last)
            em("tensor", _shf)
            snap(f"shmm_{g}", sPE)
            snap(f"ffdone_{g}", sPE)

            def _shev(scalar, g=g):
                wait(scalar, f"shmm_{g}")
                scalar.activation(out=shn_sb[:], in_=psr[0:1, TG:2 * TG],
                                  func=AF.Identity,
                                  scale=ct[0:1, C_NEG128:C_NEG128 + 1])
            em("scalar", _shev)
            snap(f"shnev_{g}", sSC)

            bdown = []

            def _takedown(tensor, g=g, bd=bdown):
                if not pool.replay or len(bd) == 0:
                    bd.append(pool.take(tensor))
                    bd.append(pool.take(tensor))
                else:
                    pool.take(tensor)
                    pool.take(tensor)
            em("tensor", _takedown)
            for ho in range(4):
                gho = g * 4 + ho
                for icg in range(16):
                    gw = gho * 16 + icg

                    def _wdl(sync, ho=ho, icg=icg, gw=gw):
                        if gw >= 2:
                            wait(sync, f"wdmm_{gw - 2}")
                        if gw == 0:
                            wait(sync, "precast")
                        src = wd_bf.ap()[ho, icg * 4:(icg + 1) * 4].rearrange(
                            "u p c -> p u c")
                        sWd[icg % 2].inc(sync.dma_start(out=wdb[icg % 2][:],
                                                        in_=src), 16)
                    em("sync", _wdl)
                    snap(f"wdl_{gw}", sWd[icg % 2])

                    def _dmm(tensor, g=g, ho=ho, icg=icg, gw=gw, gho=gho, bd=bdown):
                        wait(tensor, f"wdl_{gw}")
                        if icg == 0:
                            wait(tensor, f"shnev_{g}")
                            wait(tensor, f"hh_{g}_{63}")
                            if ho >= 1:
                                wait(tensor, f"dnev_{gho - 1}")
                        last = None
                        for u in range(4):
                            ic = icg * 4 + u
                            for tt in range(2):
                                last = tensor.matmul(
                                    ps[bd[tt]][:],
                                    hh_sb[:, ic, tt * 128:(tt + 1) * 128],
                                    wdb[icg % 2][:, u, :],
                                    start=(ic == 0), stop=False)
                        if icg == 15:
                            for tt in range(2):
                                last = tensor.matmul(
                                    ps[bd[tt]][:],
                                    shn_sb[0:1, tt * 128:(tt + 1) * 128],
                                    onesr_sb[0:1, 0:512], start=False, stop=True)
                        sPE.inc(last)
                    em("tensor", _dmm)
                    snap(f"wdmm_{gw}", sPE)

                def _dnev(vector, g=g, ho=ho, gho=gho, bd=bdown):
                    vector.set_lag(1)
                    wait(vector, f"wdmm_{gho * 16 + 15}")
                    if g == 0 and ho == 0:
                        wait(vector, "pre")
                    if ho == 0 and g >= 1:
                        wait(vector, f"outdma_{g - 1}")
                    for tt in range(2):
                        vector.tensor_tensor(
                            out=ffn32[:, tt, ho * 512:(ho + 1) * 512],
                            in0=ps[bd[tt]][:],
                            in1=dsc_sb[:, ho * 512:(ho + 1) * 512], op=ALU.mult)
                em("vector", _dnev)
                snap(f"dnev_{gho}", sVE)

            def _givedown(bd=bdown, key=f"dnev_{g * 4 + 3}"):
                pool.give(bd[0], key)
                pool.give(bd[1], key)
            order.append(("__call__", _givedown))
            snap(f"downdone_{g}", sVE)
            snap(f"ffnout_{g}", sVE)

            for tt in range(2):
                def _ws3(scalar, k=f"ffnout_{g}"):
                    wait(scalar, k)
                em("scalar", _ws3)

                def _wv4(vector, k=f"ffnout_{g}"):
                    wait(vector, k)
                em("vector", _wv4)
                slot3, li3 = emit_ln_stats(lambda g=g, tt=tt: ffn32[:, tt, :])

                def _nm3(scalar, g=g, tt=tt, slot=slot3, li=li3):
                    wait(scalar, f"lnstat_done_{li}")
                    scalar.activation(out=ffn32[:, tt, :], in_=ffn32[:, tt, :],
                                      func=AF.Identity,
                                      scale=srstd[:, slot:slot + 1],
                                      bias=snegmr[:, slot:slot + 1])
                em("scalar", _nm3)
                snap(f"ln3_{g}_{tt}", sSC)
                ln_norm_key[li3] = f"ln3_{g}_{tt}"

            def _ost(gpsimd, g=g):
                wait(gpsimd, f"ln3_{g}_0")
                wait(gpsimd, f"ln3_{g}_1")
                wait(gpsimd, f"x1st_{g}_0")
                wait(gpsimd, f"x1st_{g}_1")
                dst = out[g].rearrange("(tt p) m -> p tt m", p=128)
                sST.inc(gpsimd.dma_start(out=dst, in_=ffn32[:],
                                         accum_op=ALU.add), 16)
            em("gpsimd", _ost)
            snap(f"outdma_{g}", sST)

        emit_A(0)
        for c in range(NT):
            for m in range(8):
                emit_scan_m(c, m)
                if m == 7:
                    snap(f"scanchunk_{c}", sVE)
                    snap(f"ycast_{c}", sSC)
                emit_A_piece(c + 1, m)
            if c >= 1:
                emit_outproj(c - 1)
            if c >= 3 and (c - 3) % 2 == 0:
                emit_C((c - 3) // 2)
        emit_outproj(NT - 1)
        emit_C(7)

        def _final(sync):
            sync.wait_ge(sST.h, sST.n)
        em("sync", _final)

        mock = _MockEng()
        mock_sc = ChainEng(mock, sSC)
        mock_ve = ChainEng(mock, sVE)
        for eng_name, payload in order:
            if eng_name == "__snap__":
                key, sem = payload
                ev[key] = (sem, sem.n)
            elif eng_name == "__call__":
                payload()
            elif eng_name == "scalar":
                payload(mock_sc)
            elif eng_name == "vector":
                payload(mock_ve)
            else:
                payload(mock)
        for s_ in ALL_SEMS:
            s_.n = 0
        pool.replay = True
        pool.ri = 0

        blk.sync(lambda eng: [f(eng) for f in sched["sync"]])
        blk.gpsimd(lambda eng: [f(eng) for f in sched["gpsimd"]])
        blk.scalar(lambda eng: [f(ChainEng(eng, sSC)) for f in sched["scalar"]])
        blk.vector(lambda eng: [f(ChainEng(eng, sVE)) for f in sched["vector"]])
        blk.tensor(lambda eng: [f(eng) for f in sched["tensor"]])

    return nc


def _prep_inputs(k, x, A_log, x_proj_w, dt_proj_w, dt_proj_b, out_proj_w, shared):
    b, j = k >> 1, k & 1
    d0, d1 = j * DH, (j + 1) * DH
    perm = np.concatenate([np.arange(d0, d1), np.arange((1 - j) * DH, (2 - j) * DH)])

    xbv = np.ascontiguousarray(x[b][:, perm], dtype=np.float32)
    x0mv = np.ascontiguousarray(x[b].reshape(NG, 2, TG, D)[:, j], dtype=np.float32)

    wdtT = dt_proj_w.T
    wdt_k = np.ascontiguousarray(
        wdtT[perm][:, d0:d1].reshape(16, 128, 8, 128).transpose(2, 1, 0, 3),
        dtype=np.float32)
    wxp_k = np.ascontiguousarray(
        x_proj_w.T[perm].reshape(16, 128, 2 * NST).transpose(1, 0, 2),
        dtype=np.float32)
    wout_k = np.ascontiguousarray(
        out_proj_w.T[d0:d1].reshape(8, 128, 8, 256).transpose(2, 1, 0, 3),
        dtype=np.float32)
    alog_k = np.ascontiguousarray(
        A_log[d0:d1].reshape(8, 128, NST).transpose(1, 0, 2).reshape(128, 128),
        dtype=np.float32)
    dtb_k = np.ascontiguousarray(dt_proj_b[d0:d1].reshape(8, 128).T,
                                 dtype=np.float32)

    return dict(xb=xbv, x0m=x0mv, wdt=wdt_k, wxp=wxp_k, wout=wout_k, alog=alog_k,
                dtb=dtb_k, **shared)


def kernel(L0_attn_x, L0_attn_A_log, L0_attn_x_proj_w, L0_attn_x_proj_b,
           L0_attn_dt_proj_w, L0_attn_dt_proj_b, L0_attn_out_proj_w,
           L0_attn_out_proj_b, L0_ffn_gate_w, L0_ffn_gate_scale, L0_ffn_up_w,
           L0_ffn_up_scale, L0_ffn_down_w, L0_ffn_down_scale):
    x = np.asarray(L0_attn_x, dtype=np.float32)
    A_log = np.asarray(L0_attn_A_log, dtype=np.float32)
    x_proj_w = np.asarray(L0_attn_x_proj_w, dtype=np.float32)
    x_proj_b = np.asarray(L0_attn_x_proj_b, dtype=np.float32)
    dt_proj_w = np.asarray(L0_attn_dt_proj_w, dtype=np.float32)
    dt_proj_b = np.asarray(L0_attn_dt_proj_b, dtype=np.float32)
    out_proj_w = np.asarray(L0_attn_out_proj_w, dtype=np.float32)
    out_proj_b = np.asarray(L0_attn_out_proj_b, dtype=np.float32)
    gate_w = np.asarray(L0_ffn_gate_w)
    up_w = np.asarray(L0_ffn_up_w)
    down_w = np.asarray(L0_ffn_down_w)
    gate_scale = np.asarray(L0_ffn_gate_scale, dtype=np.float32)
    up_scale = np.asarray(L0_ffn_up_scale, dtype=np.float32)
    down_scale = np.asarray(L0_ffn_down_scale, dtype=np.float32)

    wg8v = np.ascontiguousarray(
        gate_w.astype(np.uint8).T.reshape(16, 128, 64, 128).transpose(2, 1, 0, 3))
    wu8v = np.ascontiguousarray(
        up_w.astype(np.uint8).T.reshape(16, 128, 64, 128).transpose(2, 1, 0, 3))
    wd8v = np.ascontiguousarray(
        down_w.astype(np.uint8).T.reshape(64, 128, 4, 512).transpose(2, 0, 1, 3))
    shared = dict(
        xpb=np.ascontiguousarray(x_proj_b.reshape(2 * NST, 1)),
        outb_bc=np.ascontiguousarray(np.broadcast_to(out_proj_b, (128, D))),
        wg8=wg8v, wu8=wu8v, wd8=wd8v,
        gsc=np.ascontiguousarray(gate_scale.reshape(64, 128).T),
        usc=np.ascontiguousarray(up_scale.reshape(64, 128).T),
        dsc_bc=np.ascontiguousarray(np.broadcast_to(down_scale, (128, D))),
        ident=np.eye(128, dtype=BF),
        onesc=np.ones((128, 1), dtype=BF),
        onesr=np.ones((1, 512), dtype=BF),
        consts=np.ascontiguousarray(np.broadcast_to(np.array(
            [1e-5, 1.0 / D, -1.0, 1.0, -0.5, -128.0, 0.5, 0.0], np.float32),
            (128, NCONST))),
    )

    nc = build_nc()
    in_maps = [_prep_inputs(k, x, A_log, x_proj_w, dt_proj_w, dt_proj_b,
                            out_proj_w, shared) for k in range(8)]
    res = run_bass_kernel_spmd(nc, in_maps, core_ids=list(range(8)))

    outv = np.empty((B, L, D), dtype=np.float32)
    for k in range(8):
        b, j = k >> 1, k & 1
        outv[b].reshape(NG, 2, TG, D)[:, j] = res.results[k]["out"]
    return outv


if __name__ == "__main__":
    nc = build_nc()
    print("built; instructions:", nc.n_instructions(),
          "sbuf remaining:", nc.sbuf_bytes_remaining)


# version 07
# speedup vs baseline: 1.0340x; 1.0340x over previous
from contextlib import ExitStack

import numpy as np
import ml_dtypes

import concourse.bass as bass
import concourse.mybir as mybir
from concourse.bass_utils import run_bass_kernel_spmd

F32 = mybir.dt.float32
BF16 = mybir.dt.bfloat16
U8 = mybir.dt.uint8
AF = mybir.ActivationFunctionType
ALU = mybir.AluOpType
AX = mybir.AxisListType
BF = ml_dtypes.bfloat16

B, L, D = 4, 4096, 2048
DH = 1024
NST = 16
TC = 256
NT = L // TC
TG = 256
NG = 8
SHLAG = 6

C_EPS, C_INVD, C_NEG1, C_ONE, C_NEGH, C_NEG128, C_HALF = range(7)
NCONST = 8


class Sem:
    def __init__(self, es, nc, name):
        self.h = es.enter_context(nc.semaphore(name))
        self.n = 0

    def inc(self, instr, k=1):
        instr.then_inc(self.h, k)
        self.n += k
        return self.n


class _MockInstr:
    def then_inc(self, h, k):
        return self


class _MockEng:
    def __getattr__(self, name):
        return lambda *a, **kw: _MockInstr()


_UNCHAINED = {"wait_ge", "sem_inc", "dma_start", "collective_compute",
              "then_inc", "nop", "engine_nop"}


class ChainEng:

    def __init__(self, eng, sem):
        object.__setattr__(self, "_e", eng)
        object.__setattr__(self, "_s", sem)
        object.__setattr__(self, "_lag", 1)

    def set_lag(self, k):
        object.__setattr__(self, "_lag", k)

    def __getattr__(self, name):
        f = getattr(self._e, name)
        if name in _UNCHAINED or name.startswith("_"):
            return f

        def g(*a, **kw):
            w = self._s.n - (self._lag - 1)
            if w > 0:
                self._e.wait_ge(self._s.h, w)
            r = f(*a, **kw)
            self._s.inc(r, 1)
            return r
        return g


def build_nc():
    nc = bass.Bass("TRN2", target_bir_lowering=False)

    xb = nc.dram_tensor("xb", [L, D], F32, kind="ExternalInput").ap()
    x0m = nc.dram_tensor("x0m", [NG, TG, D], F32, kind="ExternalInput").ap()
    wdt = nc.dram_tensor("wdt", [8, 128, 16, 128], F32, kind="ExternalInput").ap()
    wxp = nc.dram_tensor("wxp", [128, 16, 2 * NST], F32, kind="ExternalInput").ap()
    wout = nc.dram_tensor("wout", [8, 128, 8, 256], F32, kind="ExternalInput").ap()
    alog = nc.dram_tensor("alog", [128, 128], F32, kind="ExternalInput").ap()
    dtb = nc.dram_tensor("dtb", [128, 8], F32, kind="ExternalInput").ap()
    xpb = nc.dram_tensor("xpb", [2 * NST, 1], F32, kind="ExternalInput").ap()
    outb_bc = nc.dram_tensor("outb_bc", [128, D], F32, kind="ExternalInput").ap()
    wg8 = nc.dram_tensor("wg8", [64, 128, 16, 128], U8, kind="ExternalInput").ap()
    wu8 = nc.dram_tensor("wu8", [64, 128, 16, 128], U8, kind="ExternalInput").ap()
    wd8 = nc.dram_tensor("wd8", [4, 64, 128, 512], U8, kind="ExternalInput").ap()
    gsc = nc.dram_tensor("gsc", [128, 64], F32, kind="ExternalInput").ap()
    usc = nc.dram_tensor("usc", [128, 64], F32, kind="ExternalInput").ap()
    dsc_bc = nc.dram_tensor("dsc_bc", [128, D], F32, kind="ExternalInput").ap()
    ident = nc.dram_tensor("ident", [128, 128], BF16, kind="ExternalInput").ap()
    onesc = nc.dram_tensor("onesc", [128, 1], BF16, kind="ExternalInput").ap()
    onesr = nc.dram_tensor("onesr", [1, 512], BF16, kind="ExternalInput").ap()
    consts = nc.dram_tensor("consts", [128, NCONST], F32, kind="ExternalInput").ap()

    out = nc.dram_tensor("out", [NG, TG, D], F32, kind="ExternalOutput").ap()

    wg_bf = nc.dram_tensor("wg_bf", [64, 128, 16, 128], BF16)
    wu_bf = nc.dram_tensor("wu_bf", [64, 128, 16, 128], BF16)
    wd_bf = nc.dram_tensor("wd_bf", [4, 64, 128, 512], BF16)
    bc_d = nc.dram_tensor("bc_d", [2, 2 * NST, TC], BF16)
    mixp_d = nc.dram_tensor("mixp_d", [L, D], BF16)
    mixg_d = nc.dram_tensor("mixg_d", [NG, TG, D], BF16)

    es = ExitStack()
    sb = lambda name, shape, dt: es.enter_context(nc.sbuf_tensor(name, shape, dt))

    wxp_sb = sb("wxp_sb", [128, 16, 2 * NST], BF16)
    a_sb = sb("a_sb", [128, 128], F32)
    dtb_sb = sb("dtb_sb", [128, 8], F32)
    xpb_sb = sb("xpb_sb", [2 * NST, 1], F32)
    gsc_sb = sb("gsc_sb", [128, 64], F32)
    usc_sb = sb("usc_sb", [128, 64], F32)
    ident_sb = sb("ident_sb", [128, 128], BF16)
    onesc_sb = sb("onesc_sb", [128, 1], BF16)
    onesr_sb = sb("onesr_sb", [1, 512], BF16)
    ct = sb("ct", [128, NCONST], F32)
    sx = sb("sx", [128, 4], F32)
    sx2 = sb("sx2", [128, 4, 4], F32)
    smean = sb("smean", [128, 4], F32)
    snvar = sb("snvar", [128, 4], F32)
    srstd = sb("srstd", [128, 4], F32)
    snegmr = sb("snegmr", [128, 4], F32)
    scr = sb("scr", [128, 512], F32)
    outbh_sb = sb("outbh_sb", [128, D], BF16)
    dsc_sb = sb("dsc_sb", [128, D], BF16)
    rsn_sb = sb("rsn_sb", [1, TG], BF16)
    shn_sb = sb("shn_sb", [1, TG], BF16)

    x0c = sb("x0c", [128, D], BF16)
    ln1_tm = sb("ln1tm", [128, D], BF16)
    ln1_fm = sb("ln1fm", [128, 16, TC], BF16)
    wdtb = [sb(f"wdtb{i}", [128, 16, 128], BF16) for i in range(2)]
    dt_sb = sb("dt_sb", [128, 8, TC], BF16)
    u_sb = sb("u_sb", [128, 8, TC], BF16)
    bc_sb = [sb(f"bcs{i}", [2 * NST, TC], BF16) for i in range(2)]
    b_bc = sb("b_bc", [128, NST, TC], BF16)
    c_bc = sb("c_bc", [128, NST, TC], BF16)
    da_sb = [sb(f"das{i}", [128, TC], F32) for i in range(2)]
    dbx_sb = [sb(f"dbxs{i}", [128, TC], BF16) for i in range(2)]
    h_all = sb("h_all", [128, NST, TC], BF16)
    hstate = sb("hstate", [128, 128], F32)
    y32 = sb("y32", [128, TC], F32)
    y_bf = [sb(f"ybfs{i}", [128, 8, TC], BF16) for i in range(2)]
    woutb = [sb(f"woutbs{i}", [128, 8, 256], BF16) for i in range(2)]
    mixp_sb = sb("mixp_sb", [128, 2, D], BF16)
    mixg_t = sb("mixg_t", [128, D], BF16)
    x0m_t = sb("x0m_t", [128, D], BF16)
    x1_sb = sb("x1_sb", [128, 2, D], BF16)
    ln2_tm = sb("ln2_tm", [128, 2, D], BF16)
    ln2_fm = sb("ln2_fm", [128, 16, TG], BF16)
    wgb = [sb(f"wgbs{i}", [128, 16, 128], BF16) for i in range(2)]
    wub = [sb(f"wubs{i}", [128, 16, 128], BF16) for i in range(2)]
    wdb = [sb(f"wdbs{i}", [128, 4, 512], BF16) for i in range(2)]
    g_b = [sb(f"g_bs{i}", [128, TG], BF16) for i in range(2)]
    u_b = [sb(f"u_bs{i}", [128, TG], BF16) for i in range(2)]
    hh_sb = sb("hh_sb", [128, 64, TG], BF16)
    ffn32 = sb("ffn32", [128, 2, D], F32)

    ps = [es.enter_context(nc.psum_tensor(f"ps{i}", [128, 512], F32)) for i in range(5)]
    pt0 = es.enter_context(nc.psum_tensor("pt0", [128, 1024], BF16))
    pt1 = es.enter_context(nc.psum_tensor("pt1", [128, 1024], BF16))
    psr = es.enter_context(nc.psum_tensor("psr", [128, 512], F32))

    sConst = Sem(es, nc, "sConst")
    sPre = Sem(es, nc, "sPre")
    sPc = Sem(es, nc, "sPc")
    sX0 = Sem(es, nc, "sX0")
    sWdt = [Sem(es, nc, f"sWdt{i}") for i in range(2)]
    sWo = [Sem(es, nc, f"sWo{i}") for i in range(2)]
    sMixp = Sem(es, nc, "sMixp")
    sBc = Sem(es, nc, "sBc")
    sBcb = Sem(es, nc, "sBcb")
    sMg = Sem(es, nc, "sMg")
    sX0m = Sem(es, nc, "sX0m")
    sGU = [Sem(es, nc, f"sGU{i}") for i in range(2)]
    sWd = [Sem(es, nc, f"sWd{i}") for i in range(2)]
    sX1s = [Sem(es, nc, f"sX1s{i}") for i in range(2)]
    sST = Sem(es, nc, "sST")
    sCC = Sem(es, nc, "sCC")
    sPE = Sem(es, nc, "sPE")
    sVE = Sem(es, nc, "sVE")
    sSC = Sem(es, nc, "sSC")
    ALL_SEMS = ([sConst, sPre, sPc, sMixp, sBc, sBcb, sX0m, sST, sCC,
                 sPE, sVE, sSC] + [sX0] + sWdt + sWo + [sMg] + sGU + sWd + sX1s)

    ev = {}
    sched = {"sync": [], "gpsimd": [], "scalar": [], "vector": [], "tensor": []}
    order = []
    rec = [None]

    def _emit(kind, payload):
        if rec[0] is not None:
            rec[0][-1].append((kind, payload))
        elif kind in ("__snap__", "__call__"):
            order.append((kind, payload))
        else:
            sched[kind].append(payload)
            order.append((kind, payload))

    def em(engine, fn):
        _emit(engine, fn)

    def snap(key, sem):
        _emit("__snap__", (key, sem))

    def emc(fn):
        _emit("__call__", fn)

    def piece_break():
        rec[0].append([])

    def record_pieces(emit_fn):
        buf = [[]]
        rec[0] = buf
        emit_fn()
        rec[0] = None
        pieces = []
        for plist in buf:
            if not plist:
                continue

            def piece(plist=plist):
                for kind, payload in plist:
                    _emit(kind, payload)
            pieces.append(piece)
        return pieces

    def wait(eng, key):
        sem, n = ev[key]
        if n > 0:
            eng.wait_ge(sem.h, n)

    class Pool:

        def __init__(self):
            self.q = [(b, None) for b in range(5)]
            self.log = []
            self.replay = False
            self.ri = 0

        def take(self, eng):
            if self.replay:
                b, key = self.log[self.ri]
                self.ri += 1
            else:
                b, key = self.q.pop(0)
                self.log.append((b, key))
            if key is not None:
                wait(eng, key)
            return b

        def give(self, b, key):
            if not self.replay:
                self.q.append((b, key))

    pool = Pool()

    with es:
        blk = es.enter_context(nc.Block())

        def _loads(sync):
            for ap_in, tile in ((alog, a_sb), (dtb, dtb_sb), (xpb, xpb_sb),
                                (gsc, gsc_sb), (usc, usc_sb), (ident, ident_sb),
                                (onesc, onesc_sb), (onesr, onesr_sb), (consts, ct)):
                sConst.inc(sync.dma_start(out=tile[:], in_=ap_in[:]), 16)
        em("sync", _loads)
        snap("loads0", sConst)

        def _pre(gpsimd):
            sPre.inc(gpsimd.dma_start(out=wxp_sb[:].rearrange('p k c -> p (k c)'),
                                      in_=wxp[:].rearrange('p k c -> p (k c)')), 16)
            sPre.inc(gpsimd.dma_start(out=outbh_sb[:], in_=outb_bc[:]), 16)
            sPre.inc(gpsimd.dma_start(out=dsc_sb[:], in_=dsc_bc[:]), 16)
        em("gpsimd", _pre)
        snap("pre", sPre)

        def _precast(gpsimd):
            def flat(ap, cols=32768):
                return ap.rearrange(
                    " ".join(f"d{i}" for i in range(len(ap.shape)))
                    + " -> (" + " ".join(f"d{i}" for i in range(len(ap.shape)))
                    + ")").rearrange("(x y) -> x y", y=cols)
            for srcw, dst in ((wg8, wg_bf), (wu8, wu_bf)):
                for q in range(4):
                    sPc.inc(gpsimd.dma_start(
                        out=flat(dst.ap()[q * 16:(q + 1) * 16]),
                        in_=flat(srcw[q * 16:(q + 1) * 16])), 16)
            for ho in range(4):
                sPc.inc(gpsimd.dma_start(out=flat(wd_bf.ap()[ho]),
                                         in_=flat(wd8[ho])), 16)
        em("gpsimd", _precast)
        snap("precast", sPc)

        def _prep_sc(scalar):
            wait(scalar, "loads0")
            scalar.activation(out=a_sb[:], in_=a_sb[:], func=AF.Exp)
        em("scalar", _prep_sc)
        snap("aexp", sSC)

        def _prep_ve(vector):
            vector.set_lag(1)
            wait(vector, "aexp")
            wait(vector, "pre")
            vector.tensor_scalar(out=a_sb[:], in0=a_sb[:],
                                 scalar1=ct[:, C_NEG1:C_NEG1 + 1],
                                 scalar2=None, op0=ALU.mult)
            vector.tensor_scalar(out=outbh_sb[:], in0=outbh_sb[:],
                                 scalar1=ct[:, C_HALF:C_HALF + 1],
                                 scalar2=None, op0=ALU.mult)
        em("vector", _prep_ve)
        snap("prep", sVE)

        ln_ctr = {0: 0, 1: 0}
        ln_norm_key = {}

        def emit_ln_stats(src_ap_fn, fam=0):
            ctrv = ln_ctr[fam]
            ln_ctr[fam] += 1
            s = 2 * fam + ctrv % 2
            i = fam * 1000 + ctrv
            first = ctrv < 2

            def _v1(vector, i=i, s=s, first=first):
                vector.set_lag(1)
                if not first:
                    wait(vector, f"lnstat_done_{i - 2}")
                vector.tensor_reduce(out=sx[:, s:s + 1], in_=src_ap_fn(),
                                     axis=AX.X, op=ALU.add)
            em("vector", _v1)

            def _s1(scalar, i=i, s=s, first=first):
                if not first:
                    wait(scalar, f"lnstat_done_{i - 2}")
                src = src_ap_fn()
                for q in range(4):
                    scalar.activation(out=scr[:], in_=src[:, q * 512:(q + 1) * 512],
                                      func=AF.Square, accum_out=sx2[:, s, q:q + 1])
            em("scalar", _s1)
            snap(f"ln_q_{i}", sSC)

            def _v2(vector, i=i, s=s):
                vector.set_lag(1)
                wait(vector, f"ln_q_{i}")
                vector.tensor_scalar(out=smean[:, s:s + 1], in0=sx[:, s:s + 1],
                                     scalar1=ct[:, C_INVD:C_INVD + 1], scalar2=None,
                                     op0=ALU.mult)
                vector.tensor_reduce(out=snvar[:, s:s + 1], in_=sx2[:, s, :],
                                     axis=AX.X, op=ALU.add)
                vector.tensor_scalar(out=snvar[:, s:s + 1], in0=snvar[:, s:s + 1],
                                     scalar1=ct[:, C_INVD:C_INVD + 1], scalar2=None,
                                     op0=ALU.mult)
                vector.scalar_tensor_tensor(out=snvar[:, s:s + 1],
                                            in0=smean[:, s:s + 1],
                                            scalar=smean[:, s:s + 1],
                                            in1=snvar[:, s:s + 1],
                                            op0=ALU.mult, op1=ALU.subtract)
            em("vector", _v2)
            snap(f"ln_nvar_{i}", sVE)

            def _s2(scalar, i=i, s=s):
                wait(scalar, f"ln_nvar_{i}")
                scalar.activation(out=srstd[:, s:s + 1], in_=snvar[:, s:s + 1],
                                  func=AF.Ln, scale=ct[:, C_NEG1:C_NEG1 + 1],
                                  bias=ct[:, C_EPS:C_EPS + 1])
                scalar.activation(out=srstd[:, s:s + 1], in_=srstd[:, s:s + 1],
                                  func=AF.Exp, scale=ct[:, C_NEGH:C_NEGH + 1])
            em("scalar", _s2)
            snap(f"ln_rstd_{i}", sSC)

            def _v3(vector, i=i, s=s):
                vector.set_lag(1)
                wait(vector, f"ln_rstd_{i}")
                if i - 2 in ln_norm_key:
                    wait(vector, ln_norm_key[i - 2])
                vector.scalar_tensor_tensor(out=snegmr[:, s:s + 1],
                                            in0=smean[:, s:s + 1],
                                            scalar=ct[:, C_NEG1:C_NEG1 + 1],
                                            in1=srstd[:, s:s + 1],
                                            op0=ALU.mult, op1=ALU.mult)
            em("vector", _v3)
            snap(f"lnstat_done_{i}", sVE)
            return s, i

        tg_ctr = {0: 0, 1: 0}

        def emit_transposes(src_fn, dst_fn, ready_key, extra_key=None, fam=0):
            pt = pt0 if fam == 0 else pt1
            for grp in range(2):
                ctrv = tg_ctr[fam]
                tg_ctr[fam] += 1
                tg = fam * 1000 + ctrv

                def _tr(tensor, grp=grp, tg=tg, ctrv=ctrv, pt=pt):
                    if ctrv == 0:
                        wait(tensor, "loads0")
                    wait(tensor, ready_key)
                    if extra_key is not None and grp == 0:
                        wait(tensor, extra_key)
                    if ctrv >= 1:
                        wait(tensor, f"trev_{tg - 1}")
                    last = None
                    for dcq in range(8):
                        last = tensor.transpose(pt[:, dcq * 128:(dcq + 1) * 128],
                                                src_fn(grp * 8 + dcq), ident_sb[:])
                    sPE.inc(last)
                em("tensor", _tr)
                snap(f"trmm_{tg}", sPE)

                def _trev(scalar, grp=grp, tg=tg, pt=pt):
                    wait(scalar, f"trmm_{tg}")
                    scalar.copy(dst_fn(grp),
                                pt[:].rearrange("p (dc t) -> p dc t", dc=8))
                em("scalar", _trev)
                snap(f"trev_{tg}", sSC)

        def emit_A_piece(c, piece):
            if c >= NT:
                return
            if piece in (0, 2):
                si = piece // 2
                sig = c * 2 + si

                def _ld(gpsimd, sig=sig, c=c, si=si):
                    if sig >= 1:
                        wait(gpsimd, f"ln1tm_{sig - 1}")
                    r0 = c * TC + si * 128
                    sX0.inc(gpsimd.dma_start(out=x0c[:],
                                             in_=xb[r0:r0 + 128, :]), 16)
                em("gpsimd", _ld)
                snap(f"x0ld_{sig}", sX0)

                def _wv(vector, k=f"x0ld_{sig}"):
                    wait(vector, k)
                em("vector", _wv)

                def _ws(scalar, k=f"x0ld_{sig}"):
                    wait(scalar, k)
                em("scalar", _ws)
                slot, li = emit_ln_stats(lambda: x0c[:])

                def _norm(scalar, slot=slot, li=li, sig=sig):
                    wait(scalar, f"lnstat_done_{li}")
                    if sig >= 1:
                        wait(scalar, f"ln1tr_{sig - 1}")
                    scalar.activation(out=ln1_tm[:], in_=x0c[:],
                                      func=AF.Identity,
                                      scale=srstd[:, slot:slot + 1],
                                      bias=snegmr[:, slot:slot + 1])
                em("scalar", _norm)
                snap(f"ln1tm_{sig}", sSC)
                ln_norm_key[li] = f"ln1tm_{sig}"

            elif piece in (1, 3):
                si = piece // 2
                sig = c * 2 + si
                emit_transposes(
                    (lambda dc: ln1_tm[:, dc * 128:(dc + 1) * 128]),
                    (lambda grp, si=si: ln1_fm[:, grp * 8:grp * 8 + 8,
                                               si * 128:(si + 1) * 128]),
                    f"ln1tm_{sig}",
                    f"udone_{c - 1}" if (si == 0 and c >= 1) else None)
                snap(f"ln1tr_{sig}", sPE)
                if si == 1:
                    snap(f"ln1fm_{c}", sSC)

            else:
                m0 = 2 * (piece - 4)
                for m in (m0, m0 + 1):
                    gdm = c * 8 + m

                    def _wdl(gpsimd, c=c, m=m, gdm=gdm):
                        if gdm >= 2:
                            wait(gpsimd, f"dtmm_{gdm - 2}")
                        sWdt[m % 2].inc(gpsimd.dma_start(
                            out=wdtb[m % 2][:].rearrange('p k c -> p (k c)'),
                            in_=wdt[m].rearrange('p k c -> p (k c)')), 16)
                    em("gpsimd", _wdl)
                    snap(f"wdtl_{gdm}", sWdt[m % 2])

                    hold = {}

                    def _mm(tensor, c=c, m=m, gdm=gdm, hold=hold):
                        wait(tensor, f"wdtl_{gdm}")
                        wait(tensor, f"ln1fm_{c}")
                        b = hold.setdefault("b", pool.take(tensor))
                        if pool.replay and "taken2" not in hold:
                            hold["taken2"] = True
                        last = None
                        for kk in range(16):
                            last = tensor.matmul(ps[b][:, 0:TC],
                                                 wdtb[m % 2][:, kk, :],
                                                 ln1_fm[:, kk, :],
                                                 start=(kk == 0), stop=(kk == 15))
                        sPE.inc(last)
                    em("tensor", _mm)
                    snap(f"dtmm_{gdm}", sPE)

                    def _sp(scalar, c=c, m=m, gdm=gdm, hold=hold):
                        wait(scalar, f"dtmm_{gdm}")
                        if c >= 1 and m == 0:
                            wait(scalar, f"udone_{c - 1}")
                        b = hold["b"]
                        e = scr[:, (m % 2) * 256:(m % 2) * 256 + 256]
                        scalar.activation(out=e, in_=ps[b][:, 0:TC], func=AF.Exp,
                                          bias=dtb_sb[:, m:m + 1])
                        scalar.activation(out=dt_sb[:, m, :], in_=e, func=AF.Ln,
                                          bias=ct[:, C_ONE:C_ONE + 1])
                    em("scalar", _sp)
                    snap(f"dtev_{gdm}", sSC)

                    def _give(hold=hold, key=f"dtev_{gdm}"):
                        pool.give(hold["b"], key)
                    emc(_give)

                if piece == 7:
                    holdx = {}

                    def _xp(tensor, c=c, hold=holdx):
                        wait(tensor, f"ln1fm_{c}")
                        b = hold.setdefault("b", pool.take(tensor))
                        last = None
                        for kk in range(16):
                            last = tensor.matmul(ps[b][0:2 * NST, 0:TC],
                                                 wxp_sb[:, kk, :], ln1_fm[:, kk, :],
                                                 start=(kk == 0), stop=(kk == 15))
                        sPE.inc(last)
                    em("tensor", _xp)
                    snap(f"xpmm_{c}", sPE)

                    def _xpe(scalar, c=c, hold=holdx):
                        wait(scalar, f"xpmm_{c}")
                        if c >= 2:
                            wait(scalar, f"bcst_{c - 2}")
                        scalar.activation(out=bc_sb[c % 2][:],
                                          in_=ps[hold["b"]][0:2 * NST, 0:TC],
                                          func=AF.Identity, bias=xpb_sb[:])
                    em("scalar", _xpe)
                    snap(f"bcev_{c}", sSC)

                    def _givexp(hold=holdx, key=f"bcev_{c}"):
                        pool.give(hold["b"], key)
                    emc(_givexp)

                    def _u(vector, c=c):
                        vector.set_lag(1)
                        wait(vector, f"dtev_{c * 8 + 7}")
                        for m in range(8):
                            vector.tensor_tensor(out=u_sb[:, m, :],
                                                 in0=dt_sb[:, m, :],
                                                 in1=ln1_fm[:, m, :], op=ALU.mult)
                    em("vector", _u)
                    snap(f"udone_{c}", sVE)

                    def _bcst(gpsimd, c=c):
                        wait(gpsimd, f"bcev_{c}")
                        sBc.inc(gpsimd.dma_start(out=bc_d.ap()[c % 2],
                                                 in_=bc_sb[c % 2][:]), 16)
                    em("gpsimd", _bcst)
                    snap(f"bcst_{c}", sBc)

                    def _bcb(gpsimd, c=c):
                        wait(gpsimd, f"bcst_{c}")
                        if c >= 1:
                            wait(gpsimd, f"scanchunk_{c - 1}")
                        base_el = (c % 2) * 2 * NST * TC
                        repb = bass.AP(bc_d.ap().tensor, base_el,
                                       [[0, 128], [TC, NST], [1, TC]])
                        repc = bass.AP(bc_d.ap().tensor, base_el + NST * TC,
                                       [[0, 128], [TC, NST], [1, TC]])
                        sBcb.inc(gpsimd.dma_start(out=b_bc[:], in_=repb), 16)
                        sBcb.inc(gpsimd.dma_start(out=c_bc[:], in_=repc), 16)
                    em("gpsimd", _bcb)
                    snap(f"bcast_{c}", sBcb)

        def emit_A(c):
            for piece in range(8):
                emit_A_piece(c, piece)

        def emit_scan_m(c, m):
            for np_ in range(NST // 2):
                n0 = 2 * np_
                n1 = n0 + 1
                i0 = c * 128 + m * 16 + n0
                i1 = i0 + 1

                def _da2(scalar, c=c, m=m, n0=n0, n1=n1, i0=i0, i1=i1):
                    if i0 == 0:
                        wait(scalar, "prep")
                    if i0 >= 2:
                        wait(scalar, f"scan_{i0 - 2}")
                    scalar.activation(
                        out=da_sb[i0 % 2][:], in_=dt_sb[:, m, :], func=AF.Exp,
                        scale=a_sb[:, m * 16 + n0:m * 16 + n0 + 1])
                    if i1 >= 2:
                        wait(scalar, f"scan_{i1 - 2}")
                    scalar.activation(
                        out=da_sb[i1 % 2][:], in_=dt_sb[:, m, :], func=AF.Exp,
                        scale=a_sb[:, m * 16 + n1:m * 16 + n1 + 1])
                em("scalar", _da2)
                snap(f"da_{i0}", sSC)
                snap(f"da_{i1}", sSC)

                def _sc2(vector, c=c, m=m, n0=n0, n1=n1, i0=i0, i1=i1):
                    vector.set_lag(2)
                    if n0 == 0 and m == 0:
                        wait(vector, f"bcast_{c}")
                        wait(vector, f"udone_{c}")
                    vector.tensor_tensor(out=dbx_sb[0][:], in0=u_sb[:, m, :],
                                         in1=b_bc[:, n0, :], op=ALU.mult)
                    vector.tensor_tensor(out=dbx_sb[1][:], in0=u_sb[:, m, :],
                                         in1=b_bc[:, n1, :], op=ALU.mult)
                    wait(vector, f"da_{i1}")
                    k0 = m * 16 + n0
                    k1 = m * 16 + n1
                    init0 = 0.0 if c == 0 else hstate[:, k0:k0 + 1]
                    init1 = 0.0 if c == 0 else hstate[:, k1:k1 + 1]
                    vector.tensor_tensor_scan(out=h_all[:, n0, :],
                                              data0=da_sb[i0 % 2][:],
                                              data1=dbx_sb[0][:], initial=init0,
                                              op0=ALU.mult, op1=ALU.add)
                    vector.tensor_tensor_scan(out=h_all[:, n1, :],
                                              data0=da_sb[i1 % 2][:],
                                              data1=dbx_sb[1][:], initial=init1,
                                              op0=ALU.mult, op1=ALU.add)
                    vector.tensor_copy(hstate[:, k0:k0 + 1], h_all[:, n0, TC - 1:TC])
                    vector.tensor_copy(hstate[:, k1:k1 + 1], h_all[:, n1, TC - 1:TC])
                    vector.tensor_tensor(out=h_all[:, n0, :], in0=h_all[:, n0, :],
                                         in1=c_bc[:, n0, :], op=ALU.mult)
                    vector.tensor_tensor(out=h_all[:, n1, :], in0=h_all[:, n1, :],
                                         in1=c_bc[:, n1, :], op=ALU.mult)
                em("vector", _sc2)
                snap(f"scan_{i0}", sVE)
                snap(f"scan_{i1}", sVE)

            gm = c * 8 + m

            def _red(vector, gm=gm):
                vector.set_lag(1)
                if gm >= 1:
                    wait(vector, f"ycst_{gm - 1}")
                vector.tensor_reduce(out=y32[:],
                                     in_=h_all[:].rearrange("p n t -> p t n"),
                                     axis=AX.X, op=ALU.add)
            em("vector", _red)
            snap(f"yred_{gm}", sVE)

            def _yc(scalar, c=c, m=m, gm=gm):
                wait(scalar, f"yred_{gm}")
                if c >= 2 and m == 0:
                    wait(scalar, f"opdone_{c - 2}")
                scalar.copy(y_bf[c % 2][:, m, :], y32[:])
            em("scalar", _yc)
            snap(f"ycst_{gm}", sSC)

        def emit_outproj(cc):
            for mo in range(8):
                def _wl(gpsimd, cc=cc, mo=mo):
                    gi4 = cc * 8 + mo
                    if gi4 >= 2:
                        wait(gpsimd, f"opmm_{cc}_{(mo - 2) * 2 + 1}" if mo >= 2
                             else f"opmm_{cc - 1}_{(mo + 6) * 2 + 1}")
                    sWo[mo % 2].inc(gpsimd.dma_start(
                        out=woutb[mo % 2][:].rearrange('p k c -> p (k c)'),
                        in_=wout[mo].rearrange('p k c -> p (k c)')), 16)
                em("gpsimd", _wl)
                snap(f"opwl_{cc}_{mo}", sWo[mo % 2])

                for tt in range(2):
                    gi = mo * 2 + tt

                    holdg = {}

                    def _g(tensor, cc=cc, tt=tt, mo=mo, gi=gi, hold=holdg):
                        if gi == 0:
                            wait(tensor, f"ycast_{cc}")
                            wait(tensor, "prep")
                        if tt == 0:
                            wait(tensor, f"opwl_{cc}_{mo}")
                        b = hold.setdefault("b", pool.take(tensor))
                        last = None
                        for kk in range(8):
                            last = tensor.matmul(
                                ps[b][:, 0:256],
                                y_bf[cc % 2][:, kk, tt * 128:(tt + 1) * 128],
                                woutb[mo % 2][:, kk, :],
                                start=(kk == 0), stop=(kk == 7))
                        sPE.inc(last)
                    em("tensor", _g)
                    snap(f"opmm_{cc}_{gi}", sPE)

                    def _e(vector, cc=cc, tt=tt, mo=mo, gi=gi, hold=holdg):
                        vector.set_lag(1)
                        wait(vector, f"opmm_{cc}_{gi}")
                        if gi == 0 and cc > 0:
                            wait(vector, f"mixdma_{cc - 1}")
                        vector.tensor_tensor(
                            out=mixp_sb[:, tt, mo * 256:(mo + 1) * 256],
                            in0=ps[hold["b"]][:, 0:256],
                            in1=outbh_sb[:, mo * 256:(mo + 1) * 256], op=ALU.add)
                    em("vector", _e)
                    snap(f"opev_{cc}_{gi}", sVE)

                    def _giveop(hold=holdg, key=f"opev_{cc}_{gi}"):
                        pool.give(hold["b"], key)
                    emc(_giveop)
                if mo < 7:
                    piece_break()
            snap(f"mixevac_{cc}", sVE)
            snap(f"opdone_{cc}", sPE)

            def _d(sync, cc=cc):
                wait(sync, f"mixevac_{cc}")
                dst = mixp_d.ap()[cc * TC:(cc + 1) * TC, :].rearrange(
                    "(tt p) m -> p tt m", p=128)
                sMixp.inc(sync.dma_start(out=dst, in_=mixp_sb[:]), 16)
            em("sync", _d)
            snap(f"mixdma_{cc}", sMixp)

            if cc % 2 == 1:
                gg = cc // 2

                def _rs(gpsimd, cc=cc, gg=gg):
                    wait(gpsimd, f"mixdma_{cc}")
                    sCC.inc(gpsimd.collective_compute(
                        "ReduceScatter", ALU.add,
                        replica_groups=[[0, 1], [2, 3], [4, 5], [6, 7]],
                        ins=[mixp_d.ap()[gg * 512:(gg + 1) * 512, :]],
                        outs=[mixg_d.ap()[gg]]))
                em("gpsimd", _rs)
                snap(f"rs_{gg}", sCC)

        def emit_C(g):
            for tt in range(2):
                ti = g * 2 + tt

                def _gldx(gpsimd, g=g, tt=tt, ti=ti):
                    if ti >= 1:
                        wait(gpsimd, f"x1_{(ti - 1) // 2}_{(ti - 1) % 2}")
                    sX0m.inc(gpsimd.dma_start(
                        out=x0m_t[:],
                        in_=x0m[g][tt * 128:(tt + 1) * 128, :]), 16)
                em("gpsimd", _gldx)
                snap(f"gldx_{ti}", sX0m)

                def _gld(sync, g=g, tt=tt, ti=ti):
                    wait(sync, f"rs_{g}")
                    if ti >= 1:
                        wait(sync, f"mixused_{ti - 1}")
                    sMg.inc(sync.dma_start(
                        out=mixg_t[:],
                        in_=mixg_d.ap()[g][tt * 128:(tt + 1) * 128, :]), 16)
                em("sync", _gld)
                snap(f"gldm_{ti}", sMg)

                def _wv(vector, k1=f"gldm_{ti}"):
                    wait(vector, k1)
                em("vector", _wv)

                def _ws(scalar, k1=f"gldm_{ti}"):
                    wait(scalar, k1)
                em("scalar", _ws)
                slot, li = emit_ln_stats(lambda: mixg_t[:], fam=1)
                piece_break()

                for q in range(4):
                    def _nmQ(scalar, ti=ti, slot=slot, li=li, q=q):
                        if q == 0:
                            wait(scalar, f"lnstat_done_{li}")
                            if ti >= 1:
                                wait(scalar, f"x1q_{ti - 1}_3")
                        else:
                            wait(scalar, f"x1q_{ti}_{q - 1}")
                        scalar.activation(out=scr[:],
                                          in_=mixg_t[:, q * 512:(q + 1) * 512],
                                          func=AF.Identity,
                                          scale=srstd[:, slot:slot + 1],
                                          bias=snegmr[:, slot:slot + 1])
                    em("scalar", _nmQ)
                    snap(f"lnmQ_{ti}_{q}", sSC)
                    if q == 3:
                        ln_norm_key[li] = f"lnmQ_{ti}_{q}"

                    def _x1Q(vector, g=g, tt=tt, ti=ti, q=q):
                        vector.set_lag(1)
                        wait(vector, f"lnmQ_{ti}_{q}")
                        if q == 0:
                            wait(vector, f"gldx_{ti}")
                            if g >= 1:
                                wait(vector, f"ln2tm_{g - 1}_{tt}")
                                wait(vector, f"x1st_{g - 1}_{tt}")
                        vector.tensor_tensor(
                            out=x1_sb[:, tt, q * 512:(q + 1) * 512],
                            in0=scr[:], in1=x0m_t[:, q * 512:(q + 1) * 512],
                            op=ALU.add)
                    em("vector", _x1Q)
                    snap(f"x1q_{ti}_{q}", sVE)
                snap(f"x1_{g}_{tt}", sVE)

                def _x1st(gpsimd, g=g, tt=tt):
                    wait(gpsimd, f"x1_{g}_{tt}")
                    sX1s[tt].inc(gpsimd.dma_start(
                        out=out[g][tt * 128:(tt + 1) * 128, :],
                        in_=x1_sb[:, tt, :]), 16)
                em("gpsimd", _x1st)
                snap(f"x1st_{g}_{tt}", sX1s[tt])

                def _ws2(scalar, k=f"x1_{g}_{tt}"):
                    wait(scalar, k)
                em("scalar", _ws2)
                slot2, li2 = emit_ln_stats(lambda tt=tt: x1_sb[:, tt, :], fam=1)

                def _nm2(scalar, g=g, tt=tt, slot=slot2, li=li2):
                    wait(scalar, f"lnstat_done_{li}")
                    if g >= 1:
                        wait(scalar, f"ln2tr_{g - 1}")
                    scalar.activation(out=ln2_tm[:, tt, :],
                                      in_=x1_sb[:, tt, :],
                                      func=AF.Identity,
                                      scale=srstd[:, slot:slot + 1],
                                      bias=snegmr[:, slot:slot + 1])
                em("scalar", _nm2)
                snap(f"ln2tm_{g}_{tt}", sSC)
                ln_norm_key[li2] = f"ln2tm_{g}_{tt}"
                snap(f"mixused_{ti}", sSC)
                piece_break()

            for tt in range(2):
                emit_transposes(
                    (lambda dc, tt=tt: ln2_tm[:, tt, dc * 128:(dc + 1) * 128]),
                    (lambda grp, tt=tt: ln2_fm[:, grp * 8:grp * 8 + 8,
                                               tt * 128:(tt + 1) * 128]),
                    f"ln2tm_{g}_{tt}",
                    f"ffdone_{g - 1}" if (tt == 0 and g >= 1) else None,
                    fam=1)
            snap(f"ln2tr_{g}", sPE)
            snap(f"ln2fm_{g}", sSC)

            def _rsum(tensor, g=g):
                wait(tensor, f"ln2fm_{g}")
                if g >= 1:
                    wait(tensor, f"shnev_{g - 1}")
                last = None
                for kk in range(16):
                    last = tensor.matmul(psr[0:1, 0:TG], onesc_sb[:, 0:1],
                                         ln2_fm[:, kk, :],
                                         start=(kk == 0), stop=(kk == 15))
                sPE.inc(last)
            em("tensor", _rsum)
            snap(f"rsum_{g}", sPE)

            def _rsev(scalar, g=g):
                wait(scalar, f"rsum_{g}")
                scalar.activation(out=rsn_sb[:], in_=psr[0:1, 0:TG],
                                  func=AF.Identity,
                                  scale=ct[0:1, C_NEG128:C_NEG128 + 1])
            em("scalar", _rsev)
            snap(f"rsnev_{g}", sSC)
            piece_break()

            for ic in range(64):
                gic = g * 64 + ic

                def _wld(sync, g=g, ic=ic, gic=gic):
                    if gic == 0:
                        wait(sync, "precast")
                    if gic >= 2:
                        wait(sync, f"gumm_{gic - 2}")
                    sGU[ic % 2].inc(sync.dma_start(
                        out=wgb[ic % 2][:].rearrange('p k c -> p (k c)'),
                        in_=wg_bf.ap()[ic].rearrange('p k c -> p (k c)')), 16)
                    sGU[ic % 2].inc(sync.dma_start(
                        out=wub[ic % 2][:].rearrange('p k c -> p (k c)'),
                        in_=wu_bf.ap()[ic].rearrange('p k c -> p (k c)')), 16)
                em("sync", _wld)
                snap(f"wld_{gic}", sGU[ic % 2])

                holdgu = {}

                def _gu(tensor, g=g, ic=ic, gic=gic, hold=holdgu):
                    wait(tensor, f"wld_{gic}")
                    if ic == 0:
                        wait(tensor, f"rsnev_{g}")
                    if "b" not in hold:
                        hold["b"] = (pool.take(tensor), pool.take(tensor))
                    else:
                        pool.take(tensor)
                        pool.take(tensor)
                    bg, bu = hold["b"]
                    if ic >= SHLAG:
                        wait(tensor, f"hh_{g}_{ic - SHLAG}")
                        tensor.matmul(psr[0:1, TG:2 * TG], onesc_sb[:, 0:1],
                                      hh_sb[:, ic - SHLAG, :],
                                      start=(ic - SHLAG == 0), stop=False)
                    for kk in range(16):
                        tensor.matmul(ps[bg][:, 0:TG], wgb[ic % 2][:, kk, :],
                                      ln2_fm[:, kk, :], start=(kk == 0), stop=False)
                    tensor.matmul(ps[bg][:, 0:TG], onesr_sb[0:1, 0:128],
                                  rsn_sb[0:1, :], start=False, stop=True)
                    for kk in range(16):
                        tensor.matmul(ps[bu][:, 0:TG], wub[ic % 2][:, kk, :],
                                      ln2_fm[:, kk, :], start=(kk == 0), stop=False)
                    sPE.inc(tensor.matmul(ps[bu][:, 0:TG], onesr_sb[0:1, 0:128],
                                          rsn_sb[0:1, :], start=False, stop=True))
                em("tensor", _gu)
                snap(f"gumm_{gic}", sPE)

                def _guev(scalar, g=g, ic=ic, gic=gic, hold=holdgu):
                    wait(scalar, f"gumm_{gic}")
                    if ic >= 2:
                        wait(scalar, f"hh_{g}_{ic - 2}")
                    elif g >= 1:
                        wait(scalar, f"hh_{g - 1}_{62 + ic}")
                    bg, bu = hold["b"]
                    scalar.activation(out=g_b[ic % 2][:], in_=ps[bg][:, 0:TG],
                                      func=AF.Silu, scale=gsc_sb[:, ic:ic + 1])
                    scalar.activation(out=u_b[ic % 2][:], in_=ps[bu][:, 0:TG],
                                      func=AF.Identity, scale=usc_sb[:, ic:ic + 1])
                em("scalar", _guev)
                snap(f"guev_{gic}", sSC)

                def _givegu(hold=holdgu, key=f"guev_{gic}"):
                    bg, bu = hold["b"]
                    pool.give(bg, key)
                    pool.give(bu, key)
                emc(_givegu)

                def _hh(vector, g=g, ic=ic, gic=gic):
                    vector.set_lag(1)
                    wait(vector, f"guev_{gic}")
                    if ic == 0 and g >= 1:
                        wait(vector, f"downdone_{g - 1}")
                    vector.tensor_tensor(out=hh_sb[:, ic, :], in0=g_b[ic % 2][:],
                                         in1=u_b[ic % 2][:], op=ALU.mult)
                em("vector", _hh)
                snap(f"hh_{g}_{ic}", sVE)
                if ic % 8 == 7:
                    piece_break()

            def _shf(tensor, g=g):
                last = None
                for ic in range(64 - SHLAG, 64):
                    wait(tensor, f"hh_{g}_{ic}")
                    last = tensor.matmul(psr[0:1, TG:2 * TG], onesc_sb[:, 0:1],
                                         hh_sb[:, ic, :], start=False,
                                         stop=(ic == 63))
                sPE.inc(last)
            em("tensor", _shf)
            snap(f"shmm_{g}", sPE)
            snap(f"ffdone_{g}", sPE)

            def _shev(scalar, g=g):
                wait(scalar, f"shmm_{g}")
                scalar.activation(out=shn_sb[:], in_=psr[0:1, TG:2 * TG],
                                  func=AF.Identity,
                                  scale=ct[0:1, C_NEG128:C_NEG128 + 1])
            em("scalar", _shev)
            snap(f"shnev_{g}", sSC)

            bdown = []

            def _takedown(tensor, g=g, bd=bdown):
                if not pool.replay or len(bd) == 0:
                    bd.append(pool.take(tensor))
                    bd.append(pool.take(tensor))
                else:
                    pool.take(tensor)
                    pool.take(tensor)
            em("tensor", _takedown)
            for ho in range(4):
                gho = g * 4 + ho
                for icg in range(16):
                    gw = gho * 16 + icg

                    def _wdl(sync, ho=ho, icg=icg, gw=gw):
                        if gw >= 2:
                            wait(sync, f"wdmm_{gw - 2}")
                        if gw == 0:
                            wait(sync, "precast")
                        src = wd_bf.ap()[ho, icg * 4:(icg + 1) * 4].rearrange(
                            "u p c -> p u c")
                        sWd[icg % 2].inc(sync.dma_start(out=wdb[icg % 2][:],
                                                        in_=src), 16)
                    em("sync", _wdl)
                    snap(f"wdl_{gw}", sWd[icg % 2])

                    def _dmm(tensor, g=g, ho=ho, icg=icg, gw=gw, gho=gho, bd=bdown):
                        wait(tensor, f"wdl_{gw}")
                        if icg == 0:
                            wait(tensor, f"shnev_{g}")
                            wait(tensor, f"hh_{g}_{63}")
                            if ho >= 1:
                                wait(tensor, f"dnev_{gho - 1}")
                        last = None
                        for u in range(4):
                            ic = icg * 4 + u
                            for tt in range(2):
                                last = tensor.matmul(
                                    ps[bd[tt]][:],
                                    hh_sb[:, ic, tt * 128:(tt + 1) * 128],
                                    wdb[icg % 2][:, u, :],
                                    start=(ic == 0), stop=False)
                        if icg == 15:
                            for tt in range(2):
                                last = tensor.matmul(
                                    ps[bd[tt]][:],
                                    shn_sb[0:1, tt * 128:(tt + 1) * 128],
                                    onesr_sb[0:1, 0:512], start=False, stop=True)
                        sPE.inc(last)
                    em("tensor", _dmm)
                    snap(f"wdmm_{gw}", sPE)

                def _dnev(vector, g=g, ho=ho, gho=gho, bd=bdown):
                    vector.set_lag(1)
                    wait(vector, f"wdmm_{gho * 16 + 15}")
                    if g == 0 and ho == 0:
                        wait(vector, "pre")
                    if ho == 0 and g >= 1:
                        wait(vector, f"outdma_{g - 1}")
                    for tt in range(2):
                        vector.tensor_tensor(
                            out=ffn32[:, tt, ho * 512:(ho + 1) * 512],
                            in0=ps[bd[tt]][:],
                            in1=dsc_sb[:, ho * 512:(ho + 1) * 512], op=ALU.mult)
                em("vector", _dnev)
                snap(f"dnev_{gho}", sVE)
                if ho == 1:
                    piece_break()

            def _givedown(bd=bdown, key=f"dnev_{g * 4 + 3}"):
                pool.give(bd[0], key)
                pool.give(bd[1], key)
            emc(_givedown)
            snap(f"downdone_{g}", sVE)
            snap(f"ffnout_{g}", sVE)
            piece_break()

            for tt in range(2):
                def _ws3(scalar, k=f"ffnout_{g}"):
                    wait(scalar, k)
                em("scalar", _ws3)

                def _wv4(vector, k=f"ffnout_{g}"):
                    wait(vector, k)
                em("vector", _wv4)
                slot3, li3 = emit_ln_stats(lambda g=g, tt=tt: ffn32[:, tt, :],
                                           fam=1)

                def _nm3(scalar, g=g, tt=tt, slot=slot3, li=li3):
                    wait(scalar, f"lnstat_done_{li}")
                    scalar.activation(out=ffn32[:, tt, :], in_=ffn32[:, tt, :],
                                      func=AF.Identity,
                                      scale=srstd[:, slot:slot + 1],
                                      bias=snegmr[:, slot:slot + 1])
                em("scalar", _nm3)
                snap(f"ln3_{g}_{tt}", sSC)
                ln_norm_key[li3] = f"ln3_{g}_{tt}"

            def _ost(gpsimd, g=g):
                wait(gpsimd, f"ln3_{g}_0")
                wait(gpsimd, f"ln3_{g}_1")
                wait(gpsimd, f"x1st_{g}_0")
                wait(gpsimd, f"x1st_{g}_1")
                dst = out[g].rearrange("(tt p) m -> p tt m", p=128)
                sST.inc(gpsimd.dma_start(out=dst, in_=ffn32[:],
                                         accum_op=ALU.add), 16)
            em("gpsimd", _ost)
            snap(f"outdma_{g}", sST)

        emit_A(0)
        cq = []
        for c in range(NT):
            opp = (record_pieces(lambda cc=c - 1: emit_outproj(cc))
                   if c >= 1 else None)
            if c >= 3 and (c - 3) % 2 == 0:
                cq.extend(record_pieces(lambda g=(c - 3) // 2: emit_C(g)))
            for m in range(8):
                emit_scan_m(c, m)
                if m == 7:
                    snap(f"scanchunk_{c}", sVE)
                    snap(f"ycast_{c}", sSC)
                emit_A_piece(c + 1, m)
                if opp is not None:
                    opp[m]()
                if cq:
                    cq.pop(0)()
        opp = record_pieces(lambda: emit_outproj(NT - 1))
        for m in range(8):
            opp[m]()
            if cq:
                cq.pop(0)()
        cq.extend(record_pieces(lambda: emit_C(7)))
        while cq:
            cq.pop(0)()

        def _final(sync):
            sync.wait_ge(sST.h, sST.n)
        em("sync", _final)

        mock = _MockEng()
        mock_sc = ChainEng(mock, sSC)
        mock_ve = ChainEng(mock, sVE)
        for eng_name, payload in order:
            if eng_name == "__snap__":
                key, sem = payload
                ev[key] = (sem, sem.n)
            elif eng_name == "__call__":
                payload()
            elif eng_name == "scalar":
                payload(mock_sc)
            elif eng_name == "vector":
                payload(mock_ve)
            else:
                payload(mock)
        for s_ in ALL_SEMS:
            s_.n = 0
        pool.replay = True
        pool.ri = 0

        blk.sync(lambda eng: [f(eng) for f in sched["sync"]])
        blk.gpsimd(lambda eng: [f(eng) for f in sched["gpsimd"]])
        blk.scalar(lambda eng: [f(ChainEng(eng, sSC)) for f in sched["scalar"]])
        blk.vector(lambda eng: [f(ChainEng(eng, sVE)) for f in sched["vector"]])
        blk.tensor(lambda eng: [f(eng) for f in sched["tensor"]])

    return nc


def _prep_inputs(k, x, A_log, x_proj_w, dt_proj_w, dt_proj_b, out_proj_w, shared):
    b, j = k >> 1, k & 1
    d0, d1 = j * DH, (j + 1) * DH
    perm = np.concatenate([np.arange(d0, d1), np.arange((1 - j) * DH, (2 - j) * DH)])

    xbv = np.ascontiguousarray(x[b][:, perm], dtype=np.float32)
    x0mv = np.ascontiguousarray(x[b].reshape(NG, 2, TG, D)[:, j], dtype=np.float32)

    wdtT = dt_proj_w.T
    wdt_k = np.ascontiguousarray(
        wdtT[perm][:, d0:d1].reshape(16, 128, 8, 128).transpose(2, 1, 0, 3),
        dtype=np.float32)
    wxp_k = np.ascontiguousarray(
        x_proj_w.T[perm].reshape(16, 128, 2 * NST).transpose(1, 0, 2),
        dtype=np.float32)
    wout_k = np.ascontiguousarray(
        out_proj_w.T[d0:d1].reshape(8, 128, 8, 256).transpose(2, 1, 0, 3),
        dtype=np.float32)
    alog_k = np.ascontiguousarray(
        A_log[d0:d1].reshape(8, 128, NST).transpose(1, 0, 2).reshape(128, 128),
        dtype=np.float32)
    dtb_k = np.ascontiguousarray(dt_proj_b[d0:d1].reshape(8, 128).T,
                                 dtype=np.float32)

    return dict(xb=xbv, x0m=x0mv, wdt=wdt_k, wxp=wxp_k, wout=wout_k, alog=alog_k,
                dtb=dtb_k, **shared)


def kernel(L0_attn_x, L0_attn_A_log, L0_attn_x_proj_w, L0_attn_x_proj_b,
           L0_attn_dt_proj_w, L0_attn_dt_proj_b, L0_attn_out_proj_w,
           L0_attn_out_proj_b, L0_ffn_gate_w, L0_ffn_gate_scale, L0_ffn_up_w,
           L0_ffn_up_scale, L0_ffn_down_w, L0_ffn_down_scale):
    x = np.asarray(L0_attn_x, dtype=np.float32)
    A_log = np.asarray(L0_attn_A_log, dtype=np.float32)
    x_proj_w = np.asarray(L0_attn_x_proj_w, dtype=np.float32)
    x_proj_b = np.asarray(L0_attn_x_proj_b, dtype=np.float32)
    dt_proj_w = np.asarray(L0_attn_dt_proj_w, dtype=np.float32)
    dt_proj_b = np.asarray(L0_attn_dt_proj_b, dtype=np.float32)
    out_proj_w = np.asarray(L0_attn_out_proj_w, dtype=np.float32)
    out_proj_b = np.asarray(L0_attn_out_proj_b, dtype=np.float32)
    gate_w = np.asarray(L0_ffn_gate_w)
    up_w = np.asarray(L0_ffn_up_w)
    down_w = np.asarray(L0_ffn_down_w)
    gate_scale = np.asarray(L0_ffn_gate_scale, dtype=np.float32)
    up_scale = np.asarray(L0_ffn_up_scale, dtype=np.float32)
    down_scale = np.asarray(L0_ffn_down_scale, dtype=np.float32)

    wg8v = np.ascontiguousarray(
        gate_w.astype(np.uint8).T.reshape(16, 128, 64, 128).transpose(2, 1, 0, 3))
    wu8v = np.ascontiguousarray(
        up_w.astype(np.uint8).T.reshape(16, 128, 64, 128).transpose(2, 1, 0, 3))
    wd8v = np.ascontiguousarray(
        down_w.astype(np.uint8).T.reshape(64, 128, 4, 512).transpose(2, 0, 1, 3))
    shared = dict(
        xpb=np.ascontiguousarray(x_proj_b.reshape(2 * NST, 1)),
        outb_bc=np.ascontiguousarray(np.broadcast_to(out_proj_b, (128, D))),
        wg8=wg8v, wu8=wu8v, wd8=wd8v,
        gsc=np.ascontiguousarray(gate_scale.reshape(64, 128).T),
        usc=np.ascontiguousarray(up_scale.reshape(64, 128).T),
        dsc_bc=np.ascontiguousarray(np.broadcast_to(down_scale, (128, D))),
        ident=np.eye(128, dtype=BF),
        onesc=np.ones((128, 1), dtype=BF),
        onesr=np.ones((1, 512), dtype=BF),
        consts=np.ascontiguousarray(np.broadcast_to(np.array(
            [1e-5, 1.0 / D, -1.0, 1.0, -0.5, -128.0, 0.5, 0.0], np.float32),
            (128, NCONST))),
    )

    nc = build_nc()
    in_maps = [_prep_inputs(k, x, A_log, x_proj_w, dt_proj_w, dt_proj_b,
                            out_proj_w, shared) for k in range(8)]
    res = run_bass_kernel_spmd(nc, in_maps, core_ids=list(range(8)))

    outv = np.empty((B, L, D), dtype=np.float32)
    for k in range(8):
        b, j = k >> 1, k & 1
        outv[b].reshape(NG, 2, TG, D)[:, j] = res.results[k]["out"]
    return outv


if __name__ == "__main__":
    nc = build_nc()
    print("built; instructions:", nc.n_instructions(),
          "sbuf remaining:", nc.sbuf_bytes_remaining)


# version 08
# speedup vs baseline: 1.2573x; 1.2160x over previous
from contextlib import ExitStack

import numpy as np
import ml_dtypes

import concourse.bass as bass
import concourse.mybir as mybir
from concourse.bass_utils import run_bass_kernel_spmd

F32 = mybir.dt.float32
BF16 = mybir.dt.bfloat16
U8 = mybir.dt.uint8
AF = mybir.ActivationFunctionType
ALU = mybir.AluOpType
AX = mybir.AxisListType
BF = ml_dtypes.bfloat16

B, L, D = 4, 4096, 2048
DH = 1024
NST = 16
TC = 256
NT = L // TC
TG = 256
NG = 8
SHLAG = 6

C_EPS, C_INVD, C_NEG1, C_ONE, C_NEGH, C_NEG128, C_HALF = range(7)
NCONST = 8


class Sem:
    def __init__(self, es, nc, name):
        self.h = es.enter_context(nc.semaphore(name))
        self.n = 0

    def inc(self, instr, k=1):
        instr.then_inc(self.h, k)
        self.n += k
        return self.n


class _MockInstr:
    def then_inc(self, h, k):
        return self


class _MockEng:
    def __getattr__(self, name):
        return lambda *a, **kw: _MockInstr()


_UNCHAINED = {"wait_ge", "sem_inc", "dma_start", "collective_compute",
              "then_inc", "nop", "engine_nop"}


class ChainEng:

    def __init__(self, eng, sem):
        object.__setattr__(self, "_e", eng)
        object.__setattr__(self, "_s", sem)
        object.__setattr__(self, "_lag", 1)

    def set_lag(self, k):
        object.__setattr__(self, "_lag", k)

    def __getattr__(self, name):
        f = getattr(self._e, name)
        if name in _UNCHAINED or name.startswith("_"):
            return f

        def g(*a, **kw):
            w = self._s.n - (self._lag - 1)
            if w > 0:
                self._e.wait_ge(self._s.h, w)
            r = f(*a, **kw)
            self._s.inc(r, 1)
            return r
        return g


def build_nc():
    nc = bass.Bass("TRN2", target_bir_lowering=False)

    xb = nc.dram_tensor("xb", [L, D], F32, kind="ExternalInput").ap()
    x0m = nc.dram_tensor("x0m", [NG, TG, D], F32, kind="ExternalInput").ap()
    wdt = nc.dram_tensor("wdt", [8, 128, 16, 128], F32, kind="ExternalInput").ap()
    wxp = nc.dram_tensor("wxp", [128, 16, 2 * NST], F32, kind="ExternalInput").ap()
    wout = nc.dram_tensor("wout", [8, 128, 8, 256], F32, kind="ExternalInput").ap()
    alog = nc.dram_tensor("alog", [128, 128], F32, kind="ExternalInput").ap()
    dtb = nc.dram_tensor("dtb", [128, 8], F32, kind="ExternalInput").ap()
    xpb = nc.dram_tensor("xpb", [2 * NST, 1], F32, kind="ExternalInput").ap()
    outb_bc = nc.dram_tensor("outb_bc", [128, D], F32, kind="ExternalInput").ap()
    wg8 = nc.dram_tensor("wg8", [64, 128, 16, 128], U8, kind="ExternalInput").ap()
    wu8 = nc.dram_tensor("wu8", [64, 128, 16, 128], U8, kind="ExternalInput").ap()
    wd8 = nc.dram_tensor("wd8", [4, 64, 128, 512], U8, kind="ExternalInput").ap()
    gsc = nc.dram_tensor("gsc", [128, 64], F32, kind="ExternalInput").ap()
    usc = nc.dram_tensor("usc", [128, 64], F32, kind="ExternalInput").ap()
    dsc_bc = nc.dram_tensor("dsc_bc", [128, D], F32, kind="ExternalInput").ap()
    ident = nc.dram_tensor("ident", [128, 128], BF16, kind="ExternalInput").ap()
    onesc = nc.dram_tensor("onesc", [128, 1], BF16, kind="ExternalInput").ap()
    onesr = nc.dram_tensor("onesr", [1, 512], BF16, kind="ExternalInput").ap()
    consts = nc.dram_tensor("consts", [128, NCONST], F32, kind="ExternalInput").ap()

    out = nc.dram_tensor("out", [NG, TG, D], F32, kind="ExternalOutput").ap()

    wg_bf = nc.dram_tensor("wg_bf", [64, 128, 16, 128], BF16)
    wu_bf = nc.dram_tensor("wu_bf", [64, 128, 16, 128], BF16)
    wd_bf = nc.dram_tensor("wd_bf", [4, 64, 128, 512], BF16)
    bc_d = nc.dram_tensor("bc_d", [2, 2 * NST, TC], BF16)
    mixp_d = nc.dram_tensor("mixp_d", [L, D], BF16)
    mixg_d = nc.dram_tensor("mixg_d", [NG, TG, D], BF16)

    es = ExitStack()
    sb = lambda name, shape, dt: es.enter_context(nc.sbuf_tensor(name, shape, dt))

    wxp_sb = sb("wxp_sb", [128, 16, 2 * NST], BF16)
    a_sb = sb("a_sb", [128, 128], F32)
    dtb_sb = sb("dtb_sb", [128, 8], F32)
    xpb_sb = sb("xpb_sb", [2 * NST, 1], F32)
    gsc_sb = sb("gsc_sb", [128, 64], F32)
    usc_sb = sb("usc_sb", [128, 64], F32)
    ident_sb = sb("ident_sb", [128, 128], BF16)
    onesc_sb = sb("onesc_sb", [128, 1], BF16)
    onesr_sb = sb("onesr_sb", [1, 512], BF16)
    ct = sb("ct", [128, NCONST], F32)
    sx = sb("sx", [128, 4, 4], F32)
    sx2 = sb("sx2", [128, 4, 4], F32)
    smean = sb("smean", [128, 4], F32)
    snvar = sb("snvar", [128, 4], F32)
    sm2 = sb("sm2", [128, 4], F32)
    srstd = sb("srstd", [128, 4], F32)
    snegmr = sb("snegmr", [128, 4], F32)
    scr = sb("scr", [128, 512], F32)
    outbh_sb = sb("outbh_sb", [128, D], BF16)
    dsc_sb = sb("dsc_sb", [128, D], BF16)
    rsn_sb = sb("rsn_sb", [1, TG], BF16)
    shn_sb = sb("shn_sb", [1, TG], BF16)

    x0c = sb("x0c", [128, D], BF16)
    ln1_tm = sb("ln1tm", [128, D], BF16)
    ln1_fm = sb("ln1fm", [128, 16, TC], BF16)
    wdtb = [sb(f"wdtb{i}", [128, 16, 128], BF16) for i in range(2)]
    dt_sb = sb("dt_sb", [128, 8, TC], BF16)
    u_sb = sb("u_sb", [128, 8, TC], BF16)
    bc_sb = [sb(f"bcs{i}", [2 * NST, TC], BF16) for i in range(2)]
    b_bc = sb("b_bc", [128, NST, TC], BF16)
    c_bc = sb("c_bc", [128, NST, TC], BF16)
    da_sb = [sb(f"das{i}", [128, TC], F32) for i in range(6)]
    h_all = sb("h_all", [128, NST, TC], BF16)
    hstate = sb("hstate", [128, 128], F32)
    y32 = sb("y32", [128, TC], F32)
    y_bf = [sb(f"ybfs{i}", [128, 8, TC], BF16) for i in range(2)]
    woutb = [sb(f"woutbs{i}", [128, 8, 256], BF16) for i in range(2)]
    mixp_sb = sb("mixp_sb", [128, 2, D], BF16)
    mixg_t = sb("mixg_t", [128, D], BF16)
    x0m_t = sb("x0m_t", [128, D], BF16)
    x1_sb = sb("x1_sb", [128, D], BF16)
    ln2_tm = sb("ln2_tm", [128, 2, D], BF16)
    ln2_fm = sb("ln2_fm", [128, 16, TG], BF16)
    wgb = [sb(f"wgbs{i}", [128, 16, 128], BF16) for i in range(2)]
    wub = [sb(f"wubs{i}", [128, 16, 128], BF16) for i in range(2)]
    wdb = [sb(f"wdbs{i}", [128, 4, 512], BF16) for i in range(2)]
    g_b = [sb(f"g_bs{i}", [128, TG], BF16) for i in range(2)]
    u_b = [sb(f"u_bs{i}", [128, TG], BF16) for i in range(2)]
    hh_sb = sb("hh_sb", [128, 64, TG], BF16)
    ffn32 = sb("ffn32", [128, 2, D], F32)

    ps = [es.enter_context(nc.psum_tensor(f"ps{i}", [128, 512], F32)) for i in range(5)]
    pt0 = es.enter_context(nc.psum_tensor("pt0", [128, 1024], BF16))
    pt1 = es.enter_context(nc.psum_tensor("pt1", [128, 1024], BF16))
    psr = es.enter_context(nc.psum_tensor("psr", [128, 512], F32))

    sConst = Sem(es, nc, "sConst")
    sPre = Sem(es, nc, "sPre")
    sPc = Sem(es, nc, "sPc")
    sX0 = Sem(es, nc, "sX0")
    sWdt = [Sem(es, nc, f"sWdt{i}") for i in range(2)]
    sWo = [Sem(es, nc, f"sWo{i}") for i in range(2)]
    sMixp = Sem(es, nc, "sMixp")
    sBc = Sem(es, nc, "sBc")
    sBcb = Sem(es, nc, "sBcb")
    sMg = Sem(es, nc, "sMg")
    sX0m = Sem(es, nc, "sX0m")
    sGU = [Sem(es, nc, f"sGU{i}") for i in range(2)]
    sWd = [Sem(es, nc, f"sWd{i}") for i in range(2)]
    sX1s = [Sem(es, nc, f"sX1s{i}") for i in range(2)]
    sST = Sem(es, nc, "sST")
    sCC = Sem(es, nc, "sCC")
    sPE = Sem(es, nc, "sPE")
    sVE = Sem(es, nc, "sVE")
    sSC = Sem(es, nc, "sSC")
    ALL_SEMS = ([sConst, sPre, sPc, sMixp, sBc, sBcb, sX0m, sST, sCC,
                 sPE, sVE, sSC] + [sX0] + sWdt + sWo + [sMg] + sGU + sWd + sX1s)

    ev = {}
    sched = {"sync": [], "gpsimd": [], "scalar": [], "vector": [], "tensor": []}
    order = []
    rec = [None]

    def _emit(kind, payload):
        if rec[0] is not None:
            rec[0][-1].append((kind, payload))
        elif kind in ("__snap__", "__call__"):
            order.append((kind, payload))
        else:
            sched[kind].append(payload)
            order.append((kind, payload))

    def em(engine, fn):
        _emit(engine, fn)

    def snap(key, sem):
        _emit("__snap__", (key, sem))

    def emc(fn):
        _emit("__call__", fn)

    def piece_break():
        rec[0].append([])

    def record_pieces(emit_fn):
        buf = [[]]
        rec[0] = buf
        emit_fn()
        rec[0] = None
        pieces = []
        for plist in buf:
            if not plist:
                continue

            def piece(plist=plist):
                for kind, payload in plist:
                    _emit(kind, payload)
            pieces.append(piece)
        return pieces

    def wait(eng, key):
        sem, n = ev[key]
        if n > 0:
            eng.wait_ge(sem.h, n)

    class Pool:

        def __init__(self):
            self.q = [(b, None) for b in range(5)]
            self.log = []
            self.replay = False
            self.ri = 0

        def take(self, eng):
            if self.replay:
                b, key = self.log[self.ri]
                self.ri += 1
            else:
                b, key = self.q.pop(0)
                self.log.append((b, key))
            if key is not None:
                wait(eng, key)
            return b

        def give(self, b, key):
            if not self.replay:
                self.q.append((b, key))

    pool = Pool()

    with es:
        blk = es.enter_context(nc.Block())

        def _loads(sync):
            for ap_in, tile in ((alog, a_sb), (dtb, dtb_sb), (xpb, xpb_sb),
                                (gsc, gsc_sb), (usc, usc_sb), (ident, ident_sb),
                                (onesc, onesc_sb), (onesr, onesr_sb), (consts, ct)):
                sConst.inc(sync.dma_start(out=tile[:], in_=ap_in[:]), 16)
        em("sync", _loads)
        snap("loads0", sConst)

        def _pre(gpsimd):
            sPre.inc(gpsimd.dma_start(out=wxp_sb[:].rearrange('p k c -> p (k c)'),
                                      in_=wxp[:].rearrange('p k c -> p (k c)')), 16)
            sPre.inc(gpsimd.dma_start(out=outbh_sb[:], in_=outb_bc[:]), 16)
            sPre.inc(gpsimd.dma_start(out=dsc_sb[:], in_=dsc_bc[:]), 16)
        em("gpsimd", _pre)
        snap("pre", sPre)

        def _precast(gpsimd):
            def flat(ap, cols=32768):
                return ap.rearrange(
                    " ".join(f"d{i}" for i in range(len(ap.shape)))
                    + " -> (" + " ".join(f"d{i}" for i in range(len(ap.shape)))
                    + ")").rearrange("(x y) -> x y", y=cols)
            for srcw, dst in ((wg8, wg_bf), (wu8, wu_bf)):
                for q in range(4):
                    sPc.inc(gpsimd.dma_start(
                        out=flat(dst.ap()[q * 16:(q + 1) * 16]),
                        in_=flat(srcw[q * 16:(q + 1) * 16])), 16)
            for ho in range(4):
                sPc.inc(gpsimd.dma_start(out=flat(wd_bf.ap()[ho]),
                                         in_=flat(wd8[ho])), 16)
        em("gpsimd", _precast)
        snap("precast", sPc)

        def _prep_sc(scalar):
            wait(scalar, "loads0")
            scalar.activation(out=a_sb[:], in_=a_sb[:], func=AF.Exp)
        em("scalar", _prep_sc)
        snap("aexp", sSC)

        def _prep_ve(vector):
            vector.set_lag(1)
            wait(vector, "aexp")
            wait(vector, "pre")
            vector.tensor_scalar(out=a_sb[:], in0=a_sb[:],
                                 scalar1=ct[:, C_NEG1:C_NEG1 + 1],
                                 scalar2=None, op0=ALU.mult)
            vector.tensor_scalar(out=outbh_sb[:], in0=outbh_sb[:],
                                 scalar1=ct[:, C_HALF:C_HALF + 1],
                                 scalar2=None, op0=ALU.mult)
        em("vector", _prep_ve)
        snap("prep", sVE)

        ln_ctr = {0: 0, 1: 0}
        ln_norm_key = {}

        def emit_ln_stats(src_ap_fn, fam=0):
            ctrv = ln_ctr[fam]
            ln_ctr[fam] += 1
            s = 2 * fam + ctrv % 2
            i = fam * 1000 + ctrv

            def _s1(scalar, i=i, s=s):
                src = src_ap_fn()
                for q in range(4):
                    scalar.activation(out=scr[:], in_=src[:, q * 512:(q + 1) * 512],
                                      func=AF.Identity,
                                      accum_out=sx[:, s, q:q + 1])
                for q in range(4):
                    scalar.activation(out=scr[:], in_=src[:, q * 512:(q + 1) * 512],
                                      func=AF.Square, accum_out=sx2[:, s, q:q + 1])
                scalar.activation(out=scr[:, 0:4], in_=sx[:, s, :],
                                  func=AF.Identity, scale=ct[:, C_INVD:C_INVD + 1],
                                  accum_out=smean[:, s:s + 1])
                scalar.activation(out=scr[:, 0:4], in_=sx2[:, s, :],
                                  func=AF.Identity, scale=ct[:, C_INVD:C_INVD + 1],
                                  accum_out=snvar[:, s:s + 1])
                scalar.activation(out=sm2[:, s:s + 1], in_=smean[:, s:s + 1],
                                  func=AF.Square)
                scalar.activation(out=sm2[:, s:s + 1], in_=sm2[:, s:s + 1],
                                  func=AF.Identity, scale=ct[:, C_NEG1:C_NEG1 + 1])
                scalar.activation(out=snvar[:, s:s + 1], in_=snvar[:, s:s + 1],
                                  func=AF.Identity, bias=sm2[:, s:s + 1])
                scalar.activation(out=snvar[:, s:s + 1], in_=snvar[:, s:s + 1],
                                  func=AF.Ln, bias=ct[:, C_EPS:C_EPS + 1])
                scalar.activation(out=srstd[:, s:s + 1], in_=snvar[:, s:s + 1],
                                  func=AF.Exp, scale=ct[:, C_NEGH:C_NEGH + 1])
                scalar.activation(out=snegmr[:, s:s + 1], in_=smean[:, s:s + 1],
                                  func=AF.Identity, scale=srstd[:, s:s + 1])
                scalar.activation(out=snegmr[:, s:s + 1], in_=snegmr[:, s:s + 1],
                                  func=AF.Identity, scale=ct[:, C_NEG1:C_NEG1 + 1])
            em("scalar", _s1)
            snap(f"lnstat_done_{i}", sSC)
            return s, i

        tg_ctr = {0: 0, 1: 0}

        def emit_transposes(src_fn, dst_fn, ready_key, extra_key=None, fam=0):
            pt = pt0 if fam == 0 else pt1
            for grp in range(2):
                ctrv = tg_ctr[fam]
                tg_ctr[fam] += 1
                tg = fam * 1000 + ctrv

                def _tr(tensor, grp=grp, tg=tg, ctrv=ctrv, pt=pt):
                    if ctrv == 0:
                        wait(tensor, "loads0")
                    wait(tensor, ready_key)
                    if extra_key is not None and grp == 0:
                        wait(tensor, extra_key)
                    if ctrv >= 1:
                        wait(tensor, f"trev_{tg - 1}")
                    last = None
                    for dcq in range(8):
                        last = tensor.transpose(pt[:, dcq * 128:(dcq + 1) * 128],
                                                src_fn(grp * 8 + dcq), ident_sb[:])
                    sPE.inc(last)
                em("tensor", _tr)
                snap(f"trmm_{tg}", sPE)

                def _trev(scalar, grp=grp, tg=tg, pt=pt):
                    wait(scalar, f"trmm_{tg}")
                    scalar.copy(dst_fn(grp),
                                pt[:].rearrange("p (dc t) -> p dc t", dc=8))
                em("scalar", _trev)
                snap(f"trev_{tg}", sSC)

        def emit_A_piece(c, piece):
            if c >= NT:
                return
            if piece in (0, 2):
                si = piece // 2
                sig = c * 2 + si

                def _ld(gpsimd, sig=sig, c=c, si=si):
                    if sig >= 1:
                        wait(gpsimd, f"ln1tm_{sig - 1}")
                    r0 = c * TC + si * 128
                    sX0.inc(gpsimd.dma_start(out=x0c[:],
                                             in_=xb[r0:r0 + 128, :]), 16)
                em("gpsimd", _ld)
                snap(f"x0ld_{sig}", sX0)

                def _ws(scalar, k=f"x0ld_{sig}"):
                    wait(scalar, k)
                em("scalar", _ws)
                slot, li = emit_ln_stats(lambda: x0c[:])

                def _norm(scalar, slot=slot, li=li, sig=sig):
                    wait(scalar, f"lnstat_done_{li}")
                    if sig >= 1:
                        wait(scalar, f"ln1tr_{sig - 1}")
                    scalar.activation(out=ln1_tm[:], in_=x0c[:],
                                      func=AF.Identity,
                                      scale=srstd[:, slot:slot + 1],
                                      bias=snegmr[:, slot:slot + 1])
                em("scalar", _norm)
                snap(f"ln1tm_{sig}", sSC)
                ln_norm_key[li] = f"ln1tm_{sig}"

            elif piece in (1, 3):
                si = piece // 2
                sig = c * 2 + si
                emit_transposes(
                    (lambda dc: ln1_tm[:, dc * 128:(dc + 1) * 128]),
                    (lambda grp, si=si: ln1_fm[:, grp * 8:grp * 8 + 8,
                                               si * 128:(si + 1) * 128]),
                    f"ln1tm_{sig}",
                    f"udone_{c - 1}" if (si == 0 and c >= 1) else None)
                snap(f"ln1tr_{sig}", sPE)
                if si == 1:
                    snap(f"ln1fm_{c}", sSC)

            else:
                m0 = 2 * (piece - 4)
                for m in (m0, m0 + 1):
                    gdm = c * 8 + m

                    def _wdl(gpsimd, c=c, m=m, gdm=gdm):
                        if gdm >= 2:
                            wait(gpsimd, f"dtmm_{gdm - 2}")
                        sWdt[m % 2].inc(gpsimd.dma_start(
                            out=wdtb[m % 2][:].rearrange('p k c -> p (k c)'),
                            in_=wdt[m].rearrange('p k c -> p (k c)')), 16)
                    em("gpsimd", _wdl)
                    snap(f"wdtl_{gdm}", sWdt[m % 2])

                    hold = {}

                    def _mm(tensor, c=c, m=m, gdm=gdm, hold=hold):
                        wait(tensor, f"wdtl_{gdm}")
                        wait(tensor, f"ln1fm_{c}")
                        b = hold.setdefault("b", pool.take(tensor))
                        if pool.replay and "taken2" not in hold:
                            hold["taken2"] = True
                        last = None
                        for kk in range(16):
                            last = tensor.matmul(ps[b][:, 0:TC],
                                                 wdtb[m % 2][:, kk, :],
                                                 ln1_fm[:, kk, :],
                                                 start=(kk == 0), stop=(kk == 15))
                        sPE.inc(last)
                    em("tensor", _mm)
                    snap(f"dtmm_{gdm}", sPE)

                    def _sp(scalar, c=c, m=m, gdm=gdm, hold=hold):
                        wait(scalar, f"dtmm_{gdm}")
                        if c >= 1 and m == 0:
                            wait(scalar, f"udone_{c - 1}")
                        b = hold["b"]
                        e = scr[:, (m % 2) * 256:(m % 2) * 256 + 256]
                        scalar.activation(out=e, in_=ps[b][:, 0:TC], func=AF.Exp,
                                          bias=dtb_sb[:, m:m + 1])
                        scalar.activation(out=dt_sb[:, m, :], in_=e, func=AF.Ln,
                                          bias=ct[:, C_ONE:C_ONE + 1])
                    em("scalar", _sp)
                    snap(f"dtev_{gdm}", sSC)

                    def _give(hold=hold, key=f"dtev_{gdm}"):
                        pool.give(hold["b"], key)
                    emc(_give)

                if piece == 7:
                    holdx = {}

                    def _xp(tensor, c=c, hold=holdx):
                        if c == 0:
                            wait(tensor, "pre")
                        wait(tensor, f"ln1fm_{c}")
                        b = hold.setdefault("b", pool.take(tensor))
                        last = None
                        for kk in range(16):
                            last = tensor.matmul(ps[b][0:2 * NST, 0:TC],
                                                 wxp_sb[:, kk, :], ln1_fm[:, kk, :],
                                                 start=(kk == 0), stop=(kk == 15))
                        sPE.inc(last)
                    em("tensor", _xp)
                    snap(f"xpmm_{c}", sPE)

                    def _xpe(scalar, c=c, hold=holdx):
                        wait(scalar, f"xpmm_{c}")
                        if c >= 2:
                            wait(scalar, f"bcst_{c - 2}")
                        scalar.activation(out=bc_sb[c % 2][:],
                                          in_=ps[hold["b"]][0:2 * NST, 0:TC],
                                          func=AF.Identity, bias=xpb_sb[:])
                    em("scalar", _xpe)
                    snap(f"bcev_{c}", sSC)

                    def _givexp(hold=holdx, key=f"bcev_{c}"):
                        pool.give(hold["b"], key)
                    emc(_givexp)

                    def _u(vector, c=c):
                        vector.set_lag(1)
                        wait(vector, f"dtev_{c * 8 + 7}")
                        vector.tensor_tensor(
                            out=u_sb[:].rearrange("p m t -> p (m t)"),
                            in0=dt_sb[:].rearrange("p m t -> p (m t)"),
                            in1=ln1_fm[:, 0:8, :].rearrange("p m t -> p (m t)"),
                            op=ALU.mult)
                    em("vector", _u)
                    snap(f"udone_{c}", sVE)

                    def _bcst(gpsimd, c=c):
                        wait(gpsimd, f"bcev_{c}")
                        sBc.inc(gpsimd.dma_start(out=bc_d.ap()[c % 2],
                                                 in_=bc_sb[c % 2][:]), 16)
                    em("gpsimd", _bcst)
                    snap(f"bcst_{c}", sBc)

                    def _bcb(gpsimd, c=c):
                        wait(gpsimd, f"bcst_{c}")
                        if c >= 1:
                            wait(gpsimd, f"scanchunk_{c - 1}")
                        base_el = (c % 2) * 2 * NST * TC
                        repb = bass.AP(bc_d.ap().tensor, base_el,
                                       [[0, 128], [TC, NST], [1, TC]])
                        repc = bass.AP(bc_d.ap().tensor, base_el + NST * TC,
                                       [[0, 128], [TC, NST], [1, TC]])
                        sBcb.inc(gpsimd.dma_start(out=b_bc[:], in_=repb), 16)
                        sBcb.inc(gpsimd.dma_start(out=c_bc[:], in_=repc), 16)
                    em("gpsimd", _bcb)
                    snap(f"bcast_{c}", sBcb)

        def emit_A(c):
            for piece in range(8):
                emit_A_piece(c, piece)

        def emit_scan_m(c, m):
            gm = c * 8 + m

            def mk_da2(np_):
                n0 = 2 * np_
                i0 = c * 128 + m * 16 + n0

                def _da2(scalar, c=c, m=m, n0=n0, i0=i0):
                    if i0 == 0:
                        wait(scalar, "prep")
                    for j in range(2):
                        i = i0 + j
                        n = n0 + j
                        if i >= 6:
                            wait(scalar, f"scan_{i - 6}")
                        scalar.activation(
                            out=da_sb[i % 6][:], in_=dt_sb[:, m, :], func=AF.Exp,
                            scale=a_sb[:, m * 16 + n:m * 16 + n + 1])
                em("scalar", _da2)
                snap(f"da_{i0}", sSC)
                snap(f"da_{i0 + 1}", sSC)

            def mk_sc2(np_):
                n0 = 2 * np_
                i0 = c * 128 + m * 16 + n0

                def _sc2(vector, c=c, m=m, n0=n0, i0=i0):
                    vector.set_lag(1 if n0 == 0 else 2)
                    wait(vector, f"da_{i0 + 1}")
                    for j in range(2):
                        n = n0 + j
                        k = m * 16 + n
                        init = 0.0 if c == 0 else hstate[:, k:k + 1]
                        vector.tensor_tensor_scan(
                            out=h_all[:, n, :], data0=da_sb[(i0 + j) % 6][:],
                            data1=h_all[:, n, :], initial=init,
                            op0=ALU.mult, op1=ALU.add)
                em("vector", _sc2)
                snap(f"scan_{i0}", sVE)
                snap(f"scan_{i0 + 1}", sVE)

            for p in range(3):
                mk_da2(p)

            def _dbx(vector, c=c, m=m):
                vector.set_lag(1)
                if m == 0:
                    wait(vector, f"bcast_{c}")
                    wait(vector, f"udone_{c}")
                urep = bass.AP(u_sb[:].tensor, m * TC,
                               [[8 * TC, 128], [0, NST], [1, TC]])
                vector.tensor_tensor(out=h_all[:], in0=b_bc[:], in1=urep,
                                     op=ALU.mult)
            em("vector", _dbx)

            for p in range(8):
                mk_sc2(p)
                if p + 3 <= 7:
                    mk_da2(p + 3)

            def _tail(vector, c=c, m=m, gm=gm):
                vector.set_lag(1)
                vector.tensor_copy(
                    hstate[:, m * 16:(m + 1) * 16],
                    h_all[:, :, TC - 1:TC].rearrange("p n o -> p (n o)"))
                vector.tensor_tensor(out=h_all[:], in0=h_all[:], in1=c_bc[:],
                                     op=ALU.mult)
                if gm >= 1:
                    wait(vector, f"ycst_{gm - 1}")
                vector.tensor_tensor(out=h_all[:, 0:8, :], in0=h_all[:, 0:8, :],
                                     in1=h_all[:, 8:16, :], op=ALU.add)
                vector.tensor_tensor(out=h_all[:, 0:4, :], in0=h_all[:, 0:4, :],
                                     in1=h_all[:, 4:8, :], op=ALU.add)
                vector.tensor_tensor(out=h_all[:, 0:2, :], in0=h_all[:, 0:2, :],
                                     in1=h_all[:, 2:4, :], op=ALU.add)
                vector.tensor_tensor(out=y32[:], in0=h_all[:, 0, :],
                                     in1=h_all[:, 1, :], op=ALU.add)
            em("vector", _tail)
            snap(f"yred_{gm}", sVE)

            def _yc(scalar, c=c, m=m, gm=gm):
                wait(scalar, f"yred_{gm}")
                if c >= 2 and m == 0:
                    wait(scalar, f"opdone_{c - 2}")
                scalar.copy(y_bf[c % 2][:, m, :], y32[:])
            em("scalar", _yc)
            snap(f"ycst_{gm}", sSC)

        def emit_outproj(cc):
            for mo in range(8):
                def _wl(gpsimd, cc=cc, mo=mo):
                    gi4 = cc * 8 + mo
                    if gi4 >= 2:
                        wait(gpsimd, f"opmm_{cc}_{(mo - 2) * 2 + 1}" if mo >= 2
                             else f"opmm_{cc - 1}_{(mo + 6) * 2 + 1}")
                    sWo[mo % 2].inc(gpsimd.dma_start(
                        out=woutb[mo % 2][:].rearrange('p k c -> p (k c)'),
                        in_=wout[mo].rearrange('p k c -> p (k c)')), 16)
                em("gpsimd", _wl)
                snap(f"opwl_{cc}_{mo}", sWo[mo % 2])

                for tt in range(2):
                    gi = mo * 2 + tt

                    holdg = {}

                    def _g(tensor, cc=cc, tt=tt, mo=mo, gi=gi, hold=holdg):
                        if gi == 0:
                            wait(tensor, f"ycast_{cc}")
                            wait(tensor, "prep")
                        if tt == 0:
                            wait(tensor, f"opwl_{cc}_{mo}")
                        b = hold.setdefault("b", pool.take(tensor))
                        last = None
                        for kk in range(8):
                            last = tensor.matmul(
                                ps[b][:, 0:256],
                                y_bf[cc % 2][:, kk, tt * 128:(tt + 1) * 128],
                                woutb[mo % 2][:, kk, :],
                                start=(kk == 0), stop=(kk == 7))
                        sPE.inc(last)
                    em("tensor", _g)
                    snap(f"opmm_{cc}_{gi}", sPE)

                    def _e(vector, cc=cc, tt=tt, mo=mo, gi=gi, hold=holdg):
                        vector.set_lag(1)
                        wait(vector, f"opmm_{cc}_{gi}")
                        if gi == 0 and cc > 0:
                            wait(vector, f"mixdma_{cc - 1}")
                        vector.tensor_tensor(
                            out=mixp_sb[:, tt, mo * 256:(mo + 1) * 256],
                            in0=ps[hold["b"]][:, 0:256],
                            in1=outbh_sb[:, mo * 256:(mo + 1) * 256], op=ALU.add)
                    em("vector", _e)
                    snap(f"opev_{cc}_{gi}", sVE)

                    def _giveop(hold=holdg, key=f"opev_{cc}_{gi}"):
                        pool.give(hold["b"], key)
                    emc(_giveop)
                if mo < 7:
                    piece_break()
            snap(f"mixevac_{cc}", sVE)
            snap(f"opdone_{cc}", sPE)

            def _d(sync, cc=cc):
                wait(sync, f"mixevac_{cc}")
                dst = mixp_d.ap()[cc * TC:(cc + 1) * TC, :].rearrange(
                    "(tt p) m -> p tt m", p=128)
                sMixp.inc(sync.dma_start(out=dst, in_=mixp_sb[:]), 16)
            em("sync", _d)
            snap(f"mixdma_{cc}", sMixp)

            if cc % 2 == 1:
                gg = cc // 2

                def _rs(gpsimd, cc=cc, gg=gg):
                    wait(gpsimd, f"mixdma_{cc}")
                    sCC.inc(gpsimd.collective_compute(
                        "ReduceScatter", ALU.add,
                        replica_groups=[[0, 1], [2, 3], [4, 5], [6, 7]],
                        ins=[mixp_d.ap()[gg * 512:(gg + 1) * 512, :]],
                        outs=[mixg_d.ap()[gg]]))
                em("gpsimd", _rs)
                snap(f"rs_{gg}", sCC)

        def emit_C(g):
            for tt in range(2):
                ti = g * 2 + tt

                def _gldx(gpsimd, g=g, tt=tt, ti=ti):
                    if ti >= 1:
                        wait(gpsimd, f"x1_{(ti - 1) // 2}_{(ti - 1) % 2}")
                    sX0m.inc(gpsimd.dma_start(
                        out=x0m_t[:],
                        in_=x0m[g][tt * 128:(tt + 1) * 128, :]), 16)
                em("gpsimd", _gldx)
                snap(f"gldx_{ti}", sX0m)

                def _gld(sync, g=g, tt=tt, ti=ti):
                    wait(sync, f"rs_{g}")
                    if ti >= 1:
                        wait(sync, f"mixused_{ti - 1}")
                    sMg.inc(sync.dma_start(
                        out=mixg_t[:],
                        in_=mixg_d.ap()[g][tt * 128:(tt + 1) * 128, :]), 16)
                em("sync", _gld)
                snap(f"gldm_{ti}", sMg)

                def _ws(scalar, k1=f"gldm_{ti}"):
                    wait(scalar, k1)
                em("scalar", _ws)
                slot, li = emit_ln_stats(lambda: mixg_t[:], fam=1)
                piece_break()

                for q in range(4):
                    def _nmQ(scalar, ti=ti, slot=slot, li=li, q=q):
                        if q == 0:
                            wait(scalar, f"lnstat_done_{li}")
                            if ti >= 1:
                                wait(scalar, f"x1q_{ti - 1}_3")
                        else:
                            wait(scalar, f"x1q_{ti}_{q - 1}")
                        scalar.activation(out=scr[:],
                                          in_=mixg_t[:, q * 512:(q + 1) * 512],
                                          func=AF.Identity,
                                          scale=srstd[:, slot:slot + 1],
                                          bias=snegmr[:, slot:slot + 1])
                    em("scalar", _nmQ)
                    snap(f"lnmQ_{ti}_{q}", sSC)
                    if q == 3:
                        ln_norm_key[li] = f"lnmQ_{ti}_{q}"

                    def _x1Q(vector, g=g, tt=tt, ti=ti, q=q):
                        vector.set_lag(1)
                        wait(vector, f"lnmQ_{ti}_{q}")
                        if q == 0:
                            wait(vector, f"gldx_{ti}")
                            if ti >= 1:
                                pg, pt = (ti - 1) // 2, (ti - 1) % 2
                                wait(vector, f"ln2tm_{pg}_{pt}")
                                wait(vector, f"x1st_{pg}_{pt}")
                        vector.tensor_tensor(
                            out=x1_sb[:, q * 512:(q + 1) * 512],
                            in0=scr[:], in1=x0m_t[:, q * 512:(q + 1) * 512],
                            op=ALU.add)
                    em("vector", _x1Q)
                    snap(f"x1q_{ti}_{q}", sVE)
                snap(f"x1_{g}_{tt}", sVE)

                def _x1st(gpsimd, g=g, tt=tt):
                    wait(gpsimd, f"x1_{g}_{tt}")
                    sX1s[tt].inc(gpsimd.dma_start(
                        out=out[g][tt * 128:(tt + 1) * 128, :],
                        in_=x1_sb[:]), 16)
                em("gpsimd", _x1st)
                snap(f"x1st_{g}_{tt}", sX1s[tt])

                def _ws2(scalar, k=f"x1_{g}_{tt}"):
                    wait(scalar, k)
                em("scalar", _ws2)
                slot2, li2 = emit_ln_stats(lambda: x1_sb[:], fam=1)

                def _nm2(scalar, g=g, tt=tt, slot=slot2, li=li2):
                    wait(scalar, f"lnstat_done_{li}")
                    if g >= 1:
                        wait(scalar, f"ln2tr_{g - 1}")
                    scalar.activation(out=ln2_tm[:, tt, :],
                                      in_=x1_sb[:],
                                      func=AF.Identity,
                                      scale=srstd[:, slot:slot + 1],
                                      bias=snegmr[:, slot:slot + 1])
                em("scalar", _nm2)
                snap(f"ln2tm_{g}_{tt}", sSC)
                ln_norm_key[li2] = f"ln2tm_{g}_{tt}"
                snap(f"mixused_{ti}", sSC)
                piece_break()

            for tt in range(2):
                emit_transposes(
                    (lambda dc, tt=tt: ln2_tm[:, tt, dc * 128:(dc + 1) * 128]),
                    (lambda grp, tt=tt: ln2_fm[:, grp * 8:grp * 8 + 8,
                                               tt * 128:(tt + 1) * 128]),
                    f"ln2tm_{g}_{tt}",
                    f"ffdone_{g - 1}" if (tt == 0 and g >= 1) else None,
                    fam=1)
            snap(f"ln2tr_{g}", sPE)
            snap(f"ln2fm_{g}", sSC)

            def _rsum(tensor, g=g):
                wait(tensor, f"ln2fm_{g}")
                if g >= 1:
                    wait(tensor, f"shnev_{g - 1}")
                last = None
                for kk in range(16):
                    last = tensor.matmul(psr[0:1, 0:TG], onesc_sb[:, 0:1],
                                         ln2_fm[:, kk, :],
                                         start=(kk == 0), stop=(kk == 15))
                sPE.inc(last)
            em("tensor", _rsum)
            snap(f"rsum_{g}", sPE)

            def _rsev(scalar, g=g):
                wait(scalar, f"rsum_{g}")
                scalar.activation(out=rsn_sb[:], in_=psr[0:1, 0:TG],
                                  func=AF.Identity,
                                  scale=ct[0:1, C_NEG128:C_NEG128 + 1])
            em("scalar", _rsev)
            snap(f"rsnev_{g}", sSC)
            piece_break()

            for ic in range(64):
                gic = g * 64 + ic

                def _wld(sync, g=g, ic=ic, gic=gic):
                    if gic == 0:
                        wait(sync, "precast")
                    if gic >= 2:
                        wait(sync, f"gumm_{gic - 2}")
                    sGU[ic % 2].inc(sync.dma_start(
                        out=wgb[ic % 2][:].rearrange('p k c -> p (k c)'),
                        in_=wg_bf.ap()[ic].rearrange('p k c -> p (k c)')), 16)
                    sGU[ic % 2].inc(sync.dma_start(
                        out=wub[ic % 2][:].rearrange('p k c -> p (k c)'),
                        in_=wu_bf.ap()[ic].rearrange('p k c -> p (k c)')), 16)
                em("sync", _wld)
                snap(f"wld_{gic}", sGU[ic % 2])

                holdgu = {}

                def _gu(tensor, g=g, ic=ic, gic=gic, hold=holdgu):
                    wait(tensor, f"wld_{gic}")
                    if ic == 0:
                        wait(tensor, f"rsnev_{g}")
                    if "b" not in hold:
                        hold["b"] = (pool.take(tensor), pool.take(tensor))
                    else:
                        pool.take(tensor)
                        pool.take(tensor)
                    bg, bu = hold["b"]
                    if ic >= SHLAG:
                        wait(tensor, f"hh_{g}_{ic - SHLAG}")
                        tensor.matmul(psr[0:1, TG:2 * TG], onesc_sb[:, 0:1],
                                      hh_sb[:, ic - SHLAG, :],
                                      start=(ic - SHLAG == 0), stop=False)
                    for kk in range(16):
                        tensor.matmul(ps[bg][:, 0:TG], wgb[ic % 2][:, kk, :],
                                      ln2_fm[:, kk, :], start=(kk == 0), stop=False)
                    tensor.matmul(ps[bg][:, 0:TG], onesr_sb[0:1, 0:128],
                                  rsn_sb[0:1, :], start=False, stop=True)
                    for kk in range(16):
                        tensor.matmul(ps[bu][:, 0:TG], wub[ic % 2][:, kk, :],
                                      ln2_fm[:, kk, :], start=(kk == 0), stop=False)
                    sPE.inc(tensor.matmul(ps[bu][:, 0:TG], onesr_sb[0:1, 0:128],
                                          rsn_sb[0:1, :], start=False, stop=True))
                em("tensor", _gu)
                snap(f"gumm_{gic}", sPE)

                def _guev(scalar, g=g, ic=ic, gic=gic, hold=holdgu):
                    wait(scalar, f"gumm_{gic}")
                    if ic >= 2:
                        wait(scalar, f"hh_{g}_{ic - 2}")
                    elif g >= 1:
                        wait(scalar, f"hh_{g - 1}_{62 + ic}")
                    bg, bu = hold["b"]
                    scalar.activation(out=g_b[ic % 2][:], in_=ps[bg][:, 0:TG],
                                      func=AF.Silu, scale=gsc_sb[:, ic:ic + 1])
                    scalar.activation(out=u_b[ic % 2][:], in_=ps[bu][:, 0:TG],
                                      func=AF.Identity, scale=usc_sb[:, ic:ic + 1])
                em("scalar", _guev)
                snap(f"guev_{gic}", sSC)

                def _givegu(hold=holdgu, key=f"guev_{gic}"):
                    bg, bu = hold["b"]
                    pool.give(bg, key)
                    pool.give(bu, key)
                emc(_givegu)

                def _hh(vector, g=g, ic=ic, gic=gic):
                    vector.set_lag(1)
                    wait(vector, f"guev_{gic}")
                    if ic == 0 and g >= 1:
                        wait(vector, f"downdone_{g - 1}")
                    vector.tensor_tensor(out=hh_sb[:, ic, :], in0=g_b[ic % 2][:],
                                         in1=u_b[ic % 2][:], op=ALU.mult)
                em("vector", _hh)
                snap(f"hh_{g}_{ic}", sVE)
                if ic % 8 == 7:
                    piece_break()

            def _shf(tensor, g=g):
                last = None
                for ic in range(64 - SHLAG, 64):
                    wait(tensor, f"hh_{g}_{ic}")
                    last = tensor.matmul(psr[0:1, TG:2 * TG], onesc_sb[:, 0:1],
                                         hh_sb[:, ic, :], start=False,
                                         stop=(ic == 63))
                sPE.inc(last)
            em("tensor", _shf)
            snap(f"shmm_{g}", sPE)
            snap(f"ffdone_{g}", sPE)

            def _shev(scalar, g=g):
                wait(scalar, f"shmm_{g}")
                scalar.activation(out=shn_sb[:], in_=psr[0:1, TG:2 * TG],
                                  func=AF.Identity,
                                  scale=ct[0:1, C_NEG128:C_NEG128 + 1])
            em("scalar", _shev)
            snap(f"shnev_{g}", sSC)

            bdown = []

            def _takedown(tensor, g=g, bd=bdown):
                if not pool.replay or len(bd) == 0:
                    bd.append(pool.take(tensor))
                    bd.append(pool.take(tensor))
                else:
                    pool.take(tensor)
                    pool.take(tensor)
            em("tensor", _takedown)
            for ho in range(4):
                gho = g * 4 + ho
                for icg in range(16):
                    gw = gho * 16 + icg

                    def _wdl(sync, ho=ho, icg=icg, gw=gw):
                        if gw >= 2:
                            wait(sync, f"wdmm_{gw - 2}")
                        if gw == 0:
                            wait(sync, "precast")
                        src = wd_bf.ap()[ho, icg * 4:(icg + 1) * 4].rearrange(
                            "u p c -> p u c")
                        sWd[icg % 2].inc(sync.dma_start(out=wdb[icg % 2][:],
                                                        in_=src), 16)
                    em("sync", _wdl)
                    snap(f"wdl_{gw}", sWd[icg % 2])

                    def _dmm(tensor, g=g, ho=ho, icg=icg, gw=gw, gho=gho, bd=bdown):
                        wait(tensor, f"wdl_{gw}")
                        if icg == 0:
                            wait(tensor, f"shnev_{g}")
                            wait(tensor, f"hh_{g}_{63}")
                            if ho >= 1:
                                wait(tensor, f"dnev_{gho - 1}")
                        last = None
                        for u in range(4):
                            ic = icg * 4 + u
                            for tt in range(2):
                                last = tensor.matmul(
                                    ps[bd[tt]][:],
                                    hh_sb[:, ic, tt * 128:(tt + 1) * 128],
                                    wdb[icg % 2][:, u, :],
                                    start=(ic == 0), stop=False)
                        if icg == 15:
                            for tt in range(2):
                                last = tensor.matmul(
                                    ps[bd[tt]][:],
                                    shn_sb[0:1, tt * 128:(tt + 1) * 128],
                                    onesr_sb[0:1, 0:512], start=False, stop=True)
                        sPE.inc(last)
                    em("tensor", _dmm)
                    snap(f"wdmm_{gw}", sPE)

                def _dnev(vector, g=g, ho=ho, gho=gho, bd=bdown):
                    vector.set_lag(1)
                    wait(vector, f"wdmm_{gho * 16 + 15}")
                    if g == 0 and ho == 0:
                        wait(vector, "pre")
                    if ho == 0 and g >= 1:
                        wait(vector, f"outdma_{g - 1}")
                    for tt in range(2):
                        vector.tensor_tensor(
                            out=ffn32[:, tt, ho * 512:(ho + 1) * 512],
                            in0=ps[bd[tt]][:],
                            in1=dsc_sb[:, ho * 512:(ho + 1) * 512], op=ALU.mult)
                em("vector", _dnev)
                snap(f"dnev_{gho}", sVE)
                if ho == 1:
                    piece_break()

            def _givedown(bd=bdown, key=f"dnev_{g * 4 + 3}"):
                pool.give(bd[0], key)
                pool.give(bd[1], key)
            emc(_givedown)
            snap(f"downdone_{g}", sVE)
            snap(f"ffnout_{g}", sVE)
            piece_break()

            for tt in range(2):
                def _ws3(scalar, k=f"ffnout_{g}"):
                    wait(scalar, k)
                em("scalar", _ws3)
                slot3, li3 = emit_ln_stats(lambda g=g, tt=tt: ffn32[:, tt, :],
                                           fam=1)

                def _nm3(scalar, g=g, tt=tt, slot=slot3, li=li3):
                    wait(scalar, f"lnstat_done_{li}")
                    scalar.activation(out=ffn32[:, tt, :], in_=ffn32[:, tt, :],
                                      func=AF.Identity,
                                      scale=srstd[:, slot:slot + 1],
                                      bias=snegmr[:, slot:slot + 1])
                em("scalar", _nm3)
                snap(f"ln3_{g}_{tt}", sSC)
                ln_norm_key[li3] = f"ln3_{g}_{tt}"

            def _ost(gpsimd, g=g):
                wait(gpsimd, f"ln3_{g}_0")
                wait(gpsimd, f"ln3_{g}_1")
                wait(gpsimd, f"x1st_{g}_0")
                wait(gpsimd, f"x1st_{g}_1")
                dst = out[g].rearrange("(tt p) m -> p tt m", p=128)
                sST.inc(gpsimd.dma_start(out=dst, in_=ffn32[:],
                                         accum_op=ALU.add), 16)
            em("gpsimd", _ost)
            snap(f"outdma_{g}", sST)

        emit_A(0)
        cq = []
        for c in range(NT):
            opp = (record_pieces(lambda cc=c - 1: emit_outproj(cc))
                   if c >= 1 else None)
            if c >= 3 and (c - 3) % 2 == 0:
                cq.extend(record_pieces(lambda g=(c - 3) // 2: emit_C(g)))
            for m in range(8):
                emit_scan_m(c, m)
                if m == 7:
                    snap(f"scanchunk_{c}", sVE)
                    snap(f"ycast_{c}", sSC)
                emit_A_piece(c + 1, m)
                if opp is not None:
                    opp[m]()
                if cq:
                    cq.pop(0)()
        opp = record_pieces(lambda: emit_outproj(NT - 1))
        for m in range(8):
            opp[m]()
            if cq:
                cq.pop(0)()
        cq.extend(record_pieces(lambda: emit_C(7)))
        while cq:
            cq.pop(0)()

        def _final(sync):
            sync.wait_ge(sST.h, sST.n)
        em("sync", _final)

        mock = _MockEng()
        mock_sc = ChainEng(mock, sSC)
        mock_ve = ChainEng(mock, sVE)
        for eng_name, payload in order:
            if eng_name == "__snap__":
                key, sem = payload
                ev[key] = (sem, sem.n)
            elif eng_name == "__call__":
                payload()
            elif eng_name == "scalar":
                payload(mock_sc)
            elif eng_name == "vector":
                payload(mock_ve)
            else:
                payload(mock)
        for s_ in ALL_SEMS:
            s_.n = 0
        pool.replay = True
        pool.ri = 0

        blk.sync(lambda eng: [f(eng) for f in sched["sync"]])
        blk.gpsimd(lambda eng: [f(eng) for f in sched["gpsimd"]])
        blk.scalar(lambda eng: [f(ChainEng(eng, sSC)) for f in sched["scalar"]])
        blk.vector(lambda eng: [f(ChainEng(eng, sVE)) for f in sched["vector"]])
        blk.tensor(lambda eng: [f(eng) for f in sched["tensor"]])

    return nc


def _prep_inputs(k, x, A_log, x_proj_w, dt_proj_w, dt_proj_b, out_proj_w, shared):
    b, j = k >> 1, k & 1
    d0, d1 = j * DH, (j + 1) * DH
    perm = np.concatenate([np.arange(d0, d1), np.arange((1 - j) * DH, (2 - j) * DH)])

    xbv = np.ascontiguousarray(x[b][:, perm], dtype=np.float32)
    x0mv = np.ascontiguousarray(x[b].reshape(NG, 2, TG, D)[:, j], dtype=np.float32)

    wdtT = dt_proj_w.T
    wdt_k = np.ascontiguousarray(
        wdtT[perm][:, d0:d1].reshape(16, 128, 8, 128).transpose(2, 1, 0, 3),
        dtype=np.float32)
    wxp_k = np.ascontiguousarray(
        x_proj_w.T[perm].reshape(16, 128, 2 * NST).transpose(1, 0, 2),
        dtype=np.float32)
    wout_k = np.ascontiguousarray(
        out_proj_w.T[d0:d1].reshape(8, 128, 8, 256).transpose(2, 1, 0, 3),
        dtype=np.float32)
    alog_k = np.ascontiguousarray(
        A_log[d0:d1].reshape(8, 128, NST).transpose(1, 0, 2).reshape(128, 128),
        dtype=np.float32)
    dtb_k = np.ascontiguousarray(dt_proj_b[d0:d1].reshape(8, 128).T,
                                 dtype=np.float32)

    return dict(xb=xbv, x0m=x0mv, wdt=wdt_k, wxp=wxp_k, wout=wout_k, alog=alog_k,
                dtb=dtb_k, **shared)


def kernel(L0_attn_x, L0_attn_A_log, L0_attn_x_proj_w, L0_attn_x_proj_b,
           L0_attn_dt_proj_w, L0_attn_dt_proj_b, L0_attn_out_proj_w,
           L0_attn_out_proj_b, L0_ffn_gate_w, L0_ffn_gate_scale, L0_ffn_up_w,
           L0_ffn_up_scale, L0_ffn_down_w, L0_ffn_down_scale):
    x = np.asarray(L0_attn_x, dtype=np.float32)
    A_log = np.asarray(L0_attn_A_log, dtype=np.float32)
    x_proj_w = np.asarray(L0_attn_x_proj_w, dtype=np.float32)
    x_proj_b = np.asarray(L0_attn_x_proj_b, dtype=np.float32)
    dt_proj_w = np.asarray(L0_attn_dt_proj_w, dtype=np.float32)
    dt_proj_b = np.asarray(L0_attn_dt_proj_b, dtype=np.float32)
    out_proj_w = np.asarray(L0_attn_out_proj_w, dtype=np.float32)
    out_proj_b = np.asarray(L0_attn_out_proj_b, dtype=np.float32)
    gate_w = np.asarray(L0_ffn_gate_w)
    up_w = np.asarray(L0_ffn_up_w)
    down_w = np.asarray(L0_ffn_down_w)
    gate_scale = np.asarray(L0_ffn_gate_scale, dtype=np.float32)
    up_scale = np.asarray(L0_ffn_up_scale, dtype=np.float32)
    down_scale = np.asarray(L0_ffn_down_scale, dtype=np.float32)

    wg8v = np.ascontiguousarray(
        gate_w.astype(np.uint8).T.reshape(16, 128, 64, 128).transpose(2, 1, 0, 3))
    wu8v = np.ascontiguousarray(
        up_w.astype(np.uint8).T.reshape(16, 128, 64, 128).transpose(2, 1, 0, 3))
    wd8v = np.ascontiguousarray(
        down_w.astype(np.uint8).T.reshape(64, 128, 4, 512).transpose(2, 0, 1, 3))
    shared = dict(
        xpb=np.ascontiguousarray(x_proj_b.reshape(2 * NST, 1)),
        outb_bc=np.ascontiguousarray(np.broadcast_to(out_proj_b, (128, D))),
        wg8=wg8v, wu8=wu8v, wd8=wd8v,
        gsc=np.ascontiguousarray(gate_scale.reshape(64, 128).T),
        usc=np.ascontiguousarray(up_scale.reshape(64, 128).T),
        dsc_bc=np.ascontiguousarray(np.broadcast_to(down_scale, (128, D))),
        ident=np.eye(128, dtype=BF),
        onesc=np.ones((128, 1), dtype=BF),
        onesr=np.ones((1, 512), dtype=BF),
        consts=np.ascontiguousarray(np.broadcast_to(np.array(
            [1e-5, 1.0 / D, -1.0, 1.0, -0.5, -128.0, 0.5, 0.0], np.float32),
            (128, NCONST))),
    )

    nc = build_nc()
    in_maps = [_prep_inputs(k, x, A_log, x_proj_w, dt_proj_w, dt_proj_b,
                            out_proj_w, shared) for k in range(8)]
    res = run_bass_kernel_spmd(nc, in_maps, core_ids=list(range(8)))

    outv = np.empty((B, L, D), dtype=np.float32)
    for k in range(8):
        b, j = k >> 1, k & 1
        outv[b].reshape(NG, 2, TG, D)[:, j] = res.results[k]["out"]
    return outv


if __name__ == "__main__":
    nc = build_nc()
    print("built; instructions:", nc.n_instructions(),
          "sbuf remaining:", nc.sbuf_bytes_remaining)


# version 09
# speedup vs baseline: 1.2976x; 1.0320x over previous
from contextlib import ExitStack

import numpy as np
import ml_dtypes

import concourse.bass as bass
import concourse.mybir as mybir
from concourse.bass_utils import run_bass_kernel_spmd

F32 = mybir.dt.float32
BF16 = mybir.dt.bfloat16
U8 = mybir.dt.uint8
AF = mybir.ActivationFunctionType
ALU = mybir.AluOpType
AX = mybir.AxisListType
BF = ml_dtypes.bfloat16

B, L, D = 4, 4096, 2048
DH = 1024
NST = 16
TC = 256
NT = L // TC
TG = 256
NG = 8
SHLAG = 6

C_EPS, C_INVD, C_NEG1, C_ONE, C_NEGH, C_NEG128, C_HALF = range(7)
NCONST = 8


class Sem:
    def __init__(self, es, nc, name):
        self.h = es.enter_context(nc.semaphore(name))
        self.n = 0

    def inc(self, instr, k=1):
        instr.then_inc(self.h, k)
        self.n += k
        return self.n


class _MockInstr:
    def then_inc(self, h, k):
        return self


class _MockEng:
    def __getattr__(self, name):
        return lambda *a, **kw: _MockInstr()


_UNCHAINED = {"wait_ge", "sem_inc", "dma_start", "collective_compute",
              "then_inc", "nop", "engine_nop"}


class ChainEng:

    def __init__(self, eng, sem):
        object.__setattr__(self, "_e", eng)
        object.__setattr__(self, "_s", sem)
        object.__setattr__(self, "_lag", 1)

    def set_lag(self, k):
        object.__setattr__(self, "_lag", k)

    def __getattr__(self, name):
        f = getattr(self._e, name)
        if name in _UNCHAINED or name.startswith("_"):
            return f

        def g(*a, **kw):
            w = self._s.n - (self._lag - 1)
            if w > 0:
                self._e.wait_ge(self._s.h, w)
            r = f(*a, **kw)
            self._s.inc(r, 1)
            return r
        return g


def build_nc():
    nc = bass.Bass("TRN2", target_bir_lowering=False)

    xb = nc.dram_tensor("xb", [L, D], F32, kind="ExternalInput").ap()
    x0m = nc.dram_tensor("x0m", [NG, TG, D], F32, kind="ExternalInput").ap()
    wdt = nc.dram_tensor("wdt", [8, 128, 16, 128], F32, kind="ExternalInput").ap()
    wxp = nc.dram_tensor("wxp", [128, 16, 2 * NST], F32, kind="ExternalInput").ap()
    wout = nc.dram_tensor("wout", [8, 128, 8, 256], F32, kind="ExternalInput").ap()
    alog = nc.dram_tensor("alog", [128, 128], F32, kind="ExternalInput").ap()
    dtb = nc.dram_tensor("dtb", [128, 8], F32, kind="ExternalInput").ap()
    xpb = nc.dram_tensor("xpb", [2 * NST, 1], F32, kind="ExternalInput").ap()
    outb_bc = nc.dram_tensor("outb_bc", [128, D], F32, kind="ExternalInput").ap()
    wg8 = nc.dram_tensor("wg8", [64, 128, 16, 128], U8, kind="ExternalInput").ap()
    wu8 = nc.dram_tensor("wu8", [64, 128, 16, 128], U8, kind="ExternalInput").ap()
    wd8 = nc.dram_tensor("wd8", [4, 64, 128, 512], U8, kind="ExternalInput").ap()
    gsc = nc.dram_tensor("gsc", [128, 64], F32, kind="ExternalInput").ap()
    usc = nc.dram_tensor("usc", [128, 64], F32, kind="ExternalInput").ap()
    dsc_bc = nc.dram_tensor("dsc_bc", [128, D], F32, kind="ExternalInput").ap()
    ident = nc.dram_tensor("ident", [128, 128], BF16, kind="ExternalInput").ap()
    onesc = nc.dram_tensor("onesc", [128, 1], BF16, kind="ExternalInput").ap()
    onesr = nc.dram_tensor("onesr", [1, 512], BF16, kind="ExternalInput").ap()
    consts = nc.dram_tensor("consts", [128, NCONST], F32, kind="ExternalInput").ap()

    out = nc.dram_tensor("out", [NG, TG, D], F32, kind="ExternalOutput").ap()

    wg_bf = nc.dram_tensor("wg_bf", [64, 128, 16, 128], BF16)
    wu_bf = nc.dram_tensor("wu_bf", [64, 128, 16, 128], BF16)
    wd_bf = nc.dram_tensor("wd_bf", [4, 64, 128, 512], BF16)
    bc_d = nc.dram_tensor("bc_d", [2, 2 * NST, TC], BF16)
    mixp_d = nc.dram_tensor("mixp_d", [L, D], BF16)
    mixg_d = nc.dram_tensor("mixg_d", [NG, TG, D], BF16)

    es = ExitStack()
    sb = lambda name, shape, dt: es.enter_context(nc.sbuf_tensor(name, shape, dt))

    wxp_sb = sb("wxp_sb", [128, 16, 2 * NST], BF16)
    a_sb = sb("a_sb", [128, 128], F32)
    dtb_sb = sb("dtb_sb", [128, 8], F32)
    xpb_sb = sb("xpb_sb", [2 * NST, 1], F32)
    gsc_sb = sb("gsc_sb", [128, 64], F32)
    usc_sb = sb("usc_sb", [128, 64], F32)
    ident_sb = sb("ident_sb", [128, 128], BF16)
    onesc_sb = sb("onesc_sb", [128, 1], BF16)
    onesr_sb = sb("onesr_sb", [1, 512], BF16)
    ct = sb("ct", [128, NCONST], F32)
    sx = sb("sx", [128, 6, 4], F32)
    sx2 = sb("sx2", [128, 6, 4], F32)
    smean = sb("smean", [128, 6], F32)
    snvar = sb("snvar", [128, 6], F32)
    sm2 = sb("sm2", [128, 6], F32)
    srstd = sb("srstd", [128, 6], F32)
    snegmr = sb("snegmr", [128, 6], F32)
    scr = sb("scr", [128, 512], F32)
    outbh_sb = sb("outbh_sb", [128, D], BF16)
    dsc_sb = sb("dsc_sb", [128, D], BF16)
    rsn_sb = sb("rsn_sb", [1, TG], BF16)
    shn_sb = sb("shn_sb", [1, TG], BF16)

    x0c = sb("x0c", [128, D], BF16)
    ln1_tm = sb("ln1tm", [128, D], BF16)
    ln1_fm = sb("ln1fm", [128, 16, TC], BF16)
    wdtb = [sb(f"wdtb{i}", [128, 16, 128], BF16) for i in range(2)]
    dt_sb = sb("dt_sb", [128, 8, TC], BF16)
    u_sb = sb("u_sb", [128, 8, TC], BF16)
    bc_sb = [sb(f"bcs{i}", [2 * NST, TC], BF16) for i in range(2)]
    b_bc = sb("b_bc", [128, NST, TC], BF16)
    c_bc = sb("c_bc", [128, NST, TC], BF16)
    da_sb = [sb(f"das{i}", [128, TC], F32) for i in range(6)]
    h_all = sb("h_all", [128, NST, TC], BF16)
    hstate = sb("hstate", [128, 128], F32)
    y32 = sb("y32", [128, TC], F32)
    y_bf = [sb(f"ybfs{i}", [128, 8, TC], BF16) for i in range(2)]
    woutb = [sb(f"woutbs{i}", [128, 8, 256], BF16) for i in range(2)]
    mixp_sb = sb("mixp_sb", [128, 2, D], BF16)
    mixg_t = sb("mixg_t", [128, D], BF16)
    x0m_t = sb("x0m_t", [128, D], BF16)
    x1_sb = sb("x1_sb", [128, D], BF16)
    ln2_tm = sb("ln2_tm", [128, 2, D], BF16)
    ln2_fm = sb("ln2_fm", [128, 16, TG], BF16)
    wgb = [sb(f"wgbs{i}", [128, 16, 128], BF16) for i in range(2)]
    wub = [sb(f"wubs{i}", [128, 16, 128], BF16) for i in range(2)]
    wdb = [sb(f"wdbs{i}", [128, 4, 512], BF16) for i in range(2)]
    g_b = [sb(f"g_bs{i}", [128, TG], BF16) for i in range(2)]
    hh_sb = sb("hh_sb", [128, 64, TG], BF16)
    ffn32 = sb("ffn32", [128, 2, D], F32)

    ps = [es.enter_context(nc.psum_tensor(f"ps{i}", [128, 512], F32)) for i in range(5)]
    pt0 = es.enter_context(nc.psum_tensor("pt0", [128, 1024], BF16))
    pt1 = es.enter_context(nc.psum_tensor("pt1", [128, 1024], BF16))
    psr = es.enter_context(nc.psum_tensor("psr", [128, 512], F32))

    sConst = Sem(es, nc, "sConst")
    sPre = Sem(es, nc, "sPre")
    sPc = Sem(es, nc, "sPc")
    sX0 = Sem(es, nc, "sX0")
    sWdt = [Sem(es, nc, f"sWdt{i}") for i in range(2)]
    sWo = [Sem(es, nc, f"sWo{i}") for i in range(2)]
    sMixp = Sem(es, nc, "sMixp")
    sBc = Sem(es, nc, "sBc")
    sBcb = Sem(es, nc, "sBcb")
    sMg = Sem(es, nc, "sMg")
    sX0m = Sem(es, nc, "sX0m")
    sGU = [Sem(es, nc, f"sGU{i}") for i in range(2)]
    sWd = [Sem(es, nc, f"sWd{i}") for i in range(2)]
    sX1s = [Sem(es, nc, f"sX1s{i}") for i in range(2)]
    sST = Sem(es, nc, "sST")
    sCC = Sem(es, nc, "sCC")
    sPE = Sem(es, nc, "sPE")
    sVE = Sem(es, nc, "sVE")
    sSC = Sem(es, nc, "sSC")
    ALL_SEMS = ([sConst, sPre, sPc, sMixp, sBc, sBcb, sX0m, sST, sCC,
                 sPE, sVE, sSC] + [sX0] + sWdt + sWo + [sMg] + sGU + sWd + sX1s)

    ev = {}
    sched = {"sync": [], "gpsimd": [], "scalar": [], "vector": [], "tensor": []}
    order = []
    rec = [None]

    def _emit(kind, payload):
        if rec[0] is not None:
            rec[0][-1].append((kind, payload))
        elif kind in ("__snap__", "__call__"):
            order.append((kind, payload))
        else:
            sched[kind].append(payload)
            order.append((kind, payload))

    def em(engine, fn):
        _emit(engine, fn)

    def snap(key, sem):
        _emit("__snap__", (key, sem))

    def emc(fn):
        _emit("__call__", fn)

    def piece_break():
        rec[0].append([])

    def record_pieces(emit_fn):
        buf = [[]]
        rec[0] = buf
        emit_fn()
        rec[0] = None
        pieces = []
        for plist in buf:
            if not plist:
                continue

            def piece(plist=plist):
                for kind, payload in plist:
                    _emit(kind, payload)
            pieces.append(piece)
        return pieces

    def wait(eng, key):
        sem, n = ev[key]
        if n > 0:
            eng.wait_ge(sem.h, n)

    class Pool:

        def __init__(self):
            self.q = [(b, None) for b in range(5)]
            self.log = []
            self.replay = False
            self.ri = 0

        def take(self, eng):
            if self.replay:
                b, key = self.log[self.ri]
                self.ri += 1
            else:
                b, key = self.q.pop(0)
                self.log.append((b, key))
            if key is not None:
                wait(eng, key)
            return b

        def give(self, b, key):
            if not self.replay:
                self.q.append((b, key))

    pool = Pool()

    with es:
        blk = es.enter_context(nc.Block())

        def _loads(sync):
            for ap_in, tile in ((alog, a_sb), (dtb, dtb_sb), (xpb, xpb_sb),
                                (gsc, gsc_sb), (usc, usc_sb), (ident, ident_sb),
                                (onesc, onesc_sb), (onesr, onesr_sb), (consts, ct)):
                sConst.inc(sync.dma_start(out=tile[:], in_=ap_in[:]), 16)
        em("sync", _loads)
        snap("loads0", sConst)

        def _pre(gpsimd):
            sPre.inc(gpsimd.dma_start(out=wxp_sb[:].rearrange('p k c -> p (k c)'),
                                      in_=wxp[:].rearrange('p k c -> p (k c)')), 16)
            sPre.inc(gpsimd.dma_start(out=outbh_sb[:], in_=outb_bc[:]), 16)
            sPre.inc(gpsimd.dma_start(out=dsc_sb[:], in_=dsc_bc[:]), 16)
        em("gpsimd", _pre)
        snap("pre", sPre)

        def _precast(gpsimd):
            def flat(ap, cols=32768):
                return ap.rearrange(
                    " ".join(f"d{i}" for i in range(len(ap.shape)))
                    + " -> (" + " ".join(f"d{i}" for i in range(len(ap.shape)))
                    + ")").rearrange("(x y) -> x y", y=cols)
            for srcw, dst in ((wg8, wg_bf), (wu8, wu_bf)):
                for q in range(4):
                    sPc.inc(gpsimd.dma_start(
                        out=flat(dst.ap()[q * 16:(q + 1) * 16]),
                        in_=flat(srcw[q * 16:(q + 1) * 16])), 16)
            for ho in range(4):
                sPc.inc(gpsimd.dma_start(out=flat(wd_bf.ap()[ho]),
                                         in_=flat(wd8[ho])), 16)
        em("gpsimd", _precast)
        snap("precast", sPc)

        def _prep_sc(scalar):
            wait(scalar, "loads0")
            scalar.activation(out=a_sb[:], in_=a_sb[:], func=AF.Exp)
        em("scalar", _prep_sc)
        snap("aexp", sSC)

        def _prep_ve(vector):
            vector.set_lag(1)
            wait(vector, "aexp")
            wait(vector, "pre")
            vector.tensor_scalar(out=a_sb[:], in0=a_sb[:],
                                 scalar1=ct[:, C_NEG1:C_NEG1 + 1],
                                 scalar2=None, op0=ALU.mult)
            vector.tensor_scalar(out=outbh_sb[:], in0=outbh_sb[:],
                                 scalar1=ct[:, C_HALF:C_HALF + 1],
                                 scalar2=None, op0=ALU.mult)
        em("vector", _prep_ve)
        snap("prep", sVE)

        ln_ctr = {0: 0, 1: 0, 2: 0}
        ln_norm_key = {}

        def emit_ln_stats(src_ap_fn, fam=0):
            ctrv = ln_ctr[fam]
            ln_ctr[fam] += 1
            s = 2 * fam + ctrv % 2
            i = fam * 1000 + ctrv

            def _s1(scalar, i=i, s=s):
                src = src_ap_fn()
                for q in range(4):
                    scalar.activation(out=scr[:], in_=src[:, q * 512:(q + 1) * 512],
                                      func=AF.Identity,
                                      accum_out=sx[:, s, q:q + 1])
                for q in range(4):
                    scalar.activation(out=scr[:], in_=src[:, q * 512:(q + 1) * 512],
                                      func=AF.Square, accum_out=sx2[:, s, q:q + 1])
                scalar.activation(out=scr[:, 0:4], in_=sx[:, s, :],
                                  func=AF.Identity, scale=ct[:, C_INVD:C_INVD + 1],
                                  accum_out=smean[:, s:s + 1])
                scalar.activation(out=scr[:, 0:4], in_=sx2[:, s, :],
                                  func=AF.Identity, scale=ct[:, C_INVD:C_INVD + 1],
                                  accum_out=snvar[:, s:s + 1])
                scalar.activation(out=sm2[:, s:s + 1], in_=smean[:, s:s + 1],
                                  func=AF.Square)
                scalar.activation(out=sm2[:, s:s + 1], in_=sm2[:, s:s + 1],
                                  func=AF.Identity, scale=ct[:, C_NEG1:C_NEG1 + 1])
                scalar.activation(out=snvar[:, s:s + 1], in_=snvar[:, s:s + 1],
                                  func=AF.Identity, bias=sm2[:, s:s + 1])
                scalar.activation(out=snvar[:, s:s + 1], in_=snvar[:, s:s + 1],
                                  func=AF.Ln, bias=ct[:, C_EPS:C_EPS + 1])
                scalar.activation(out=srstd[:, s:s + 1], in_=snvar[:, s:s + 1],
                                  func=AF.Exp, scale=ct[:, C_NEGH:C_NEGH + 1])
                scalar.activation(out=snegmr[:, s:s + 1], in_=smean[:, s:s + 1],
                                  func=AF.Identity, scale=srstd[:, s:s + 1])
                scalar.activation(out=snegmr[:, s:s + 1], in_=snegmr[:, s:s + 1],
                                  func=AF.Identity, scale=ct[:, C_NEG1:C_NEG1 + 1])
            em("scalar", _s1)
            snap(f"lnstat_done_{i}", sSC)
            return s, i

        tg_ctr = {0: 0, 1: 0}

        def emit_transposes(src_fn, dst_fn, ready_key, extra_key=None, fam=0):
            pt = pt0 if fam == 0 else pt1
            for grp in range(2):
                ctrv = tg_ctr[fam]
                tg_ctr[fam] += 1
                tg = fam * 1000 + ctrv

                def _tr(tensor, grp=grp, tg=tg, ctrv=ctrv, pt=pt):
                    if ctrv == 0:
                        wait(tensor, "loads0")
                    wait(tensor, ready_key)
                    if extra_key is not None and grp == 0:
                        wait(tensor, extra_key)
                    if ctrv >= 1:
                        wait(tensor, f"trev_{tg - 1}")
                    last = None
                    for dcq in range(8):
                        last = tensor.transpose(pt[:, dcq * 128:(dcq + 1) * 128],
                                                src_fn(grp * 8 + dcq), ident_sb[:])
                    sPE.inc(last)
                em("tensor", _tr)
                snap(f"trmm_{tg}", sPE)

                def _trev(scalar, grp=grp, tg=tg, pt=pt):
                    wait(scalar, f"trmm_{tg}")
                    scalar.copy(dst_fn(grp),
                                pt[:].rearrange("p (dc t) -> p dc t", dc=8))
                em("scalar", _trev)
                snap(f"trev_{tg}", sSC)

        def emit_A_piece(c, piece):
            if c >= NT:
                return
            if piece in (0, 2):
                si = piece // 2
                sig = c * 2 + si

                def _ld(gpsimd, sig=sig, c=c, si=si):
                    if sig >= 1:
                        wait(gpsimd, f"ln1tm_{sig - 1}")
                    r0 = c * TC + si * 128
                    sX0.inc(gpsimd.dma_start(out=x0c[:],
                                             in_=xb[r0:r0 + 128, :]), 16)
                em("gpsimd", _ld)
                snap(f"x0ld_{sig}", sX0)

                def _ws(scalar, k=f"x0ld_{sig}"):
                    wait(scalar, k)
                em("scalar", _ws)
                slot, li = emit_ln_stats(lambda: x0c[:])

                def _norm(scalar, slot=slot, li=li, sig=sig):
                    wait(scalar, f"lnstat_done_{li}")
                    if sig >= 1:
                        wait(scalar, f"ln1tr_{sig - 1}")
                    scalar.activation(out=ln1_tm[:], in_=x0c[:],
                                      func=AF.Identity,
                                      scale=srstd[:, slot:slot + 1],
                                      bias=snegmr[:, slot:slot + 1])
                em("scalar", _norm)
                snap(f"ln1tm_{sig}", sSC)
                ln_norm_key[li] = f"ln1tm_{sig}"

            elif piece in (1, 3):
                si = piece // 2
                sig = c * 2 + si
                emit_transposes(
                    (lambda dc: ln1_tm[:, dc * 128:(dc + 1) * 128]),
                    (lambda grp, si=si: ln1_fm[:, grp * 8:grp * 8 + 8,
                                               si * 128:(si + 1) * 128]),
                    f"ln1tm_{sig}",
                    f"udone_{c - 1}" if (si == 0 and c >= 1) else None)
                snap(f"ln1tr_{sig}", sPE)
                if si == 1:
                    snap(f"ln1fm_{c}", sSC)

            else:
                m0 = 2 * (piece - 4)
                for m in (m0, m0 + 1):
                    gdm = c * 8 + m

                    def _wdl(gpsimd, c=c, m=m, gdm=gdm):
                        if gdm >= 2:
                            wait(gpsimd, f"dtmm_{gdm - 2}")
                        sWdt[m % 2].inc(gpsimd.dma_start(
                            out=wdtb[m % 2][:].rearrange('p k c -> p (k c)'),
                            in_=wdt[m].rearrange('p k c -> p (k c)')), 16)
                    em("gpsimd", _wdl)
                    snap(f"wdtl_{gdm}", sWdt[m % 2])

                    hold = {}

                    def _mm(tensor, c=c, m=m, gdm=gdm, hold=hold):
                        wait(tensor, f"wdtl_{gdm}")
                        wait(tensor, f"ln1fm_{c}")
                        b = hold.setdefault("b", pool.take(tensor))
                        if pool.replay and "taken2" not in hold:
                            hold["taken2"] = True
                        last = None
                        for kk in range(16):
                            last = tensor.matmul(ps[b][:, 0:TC],
                                                 wdtb[m % 2][:, kk, :],
                                                 ln1_fm[:, kk, :],
                                                 start=(kk == 0), stop=(kk == 15))
                        sPE.inc(last)
                    em("tensor", _mm)
                    snap(f"dtmm_{gdm}", sPE)

                    def _sp(scalar, c=c, m=m, gdm=gdm, hold=hold):
                        wait(scalar, f"dtmm_{gdm}")
                        if c >= 1 and m == 0:
                            wait(scalar, f"udone_{c - 1}")
                        b = hold["b"]
                        e = scr[:, (m % 2) * 256:(m % 2) * 256 + 256]
                        scalar.activation(out=e, in_=ps[b][:, 0:TC], func=AF.Exp,
                                          bias=dtb_sb[:, m:m + 1])
                        scalar.activation(out=dt_sb[:, m, :], in_=e, func=AF.Ln,
                                          bias=ct[:, C_ONE:C_ONE + 1])
                    em("scalar", _sp)
                    snap(f"dtev_{gdm}", sSC)

                    def _give(hold=hold, key=f"dtev_{gdm}"):
                        pool.give(hold["b"], key)
                    emc(_give)

                if piece == 7:
                    holdx = {}

                    def _xp(tensor, c=c, hold=holdx):
                        if c == 0:
                            wait(tensor, "pre")
                        wait(tensor, f"ln1fm_{c}")
                        b = hold.setdefault("b", pool.take(tensor))
                        last = None
                        for kk in range(16):
                            last = tensor.matmul(ps[b][0:2 * NST, 0:TC],
                                                 wxp_sb[:, kk, :], ln1_fm[:, kk, :],
                                                 start=(kk == 0), stop=(kk == 15))
                        sPE.inc(last)
                    em("tensor", _xp)
                    snap(f"xpmm_{c}", sPE)

                    def _xpe(scalar, c=c, hold=holdx):
                        wait(scalar, f"xpmm_{c}")
                        if c >= 2:
                            wait(scalar, f"bcst_{c - 2}")
                        scalar.activation(out=bc_sb[c % 2][:],
                                          in_=ps[hold["b"]][0:2 * NST, 0:TC],
                                          func=AF.Identity, bias=xpb_sb[:])
                    em("scalar", _xpe)
                    snap(f"bcev_{c}", sSC)

                    def _givexp(hold=holdx, key=f"bcev_{c}"):
                        pool.give(hold["b"], key)
                    emc(_givexp)

                    def _u(vector, c=c):
                        vector.set_lag(1)
                        wait(vector, f"dtev_{c * 8 + 7}")
                        vector.tensor_tensor(
                            out=u_sb[:].rearrange("p m t -> p (m t)"),
                            in0=dt_sb[:].rearrange("p m t -> p (m t)"),
                            in1=ln1_fm[:, 0:8, :].rearrange("p m t -> p (m t)"),
                            op=ALU.mult)
                    em("vector", _u)
                    snap(f"udone_{c}", sVE)

                    def _bcst(gpsimd, c=c):
                        wait(gpsimd, f"bcev_{c}")
                        sBc.inc(gpsimd.dma_start(out=bc_d.ap()[c % 2],
                                                 in_=bc_sb[c % 2][:]), 16)
                    em("gpsimd", _bcst)
                    snap(f"bcst_{c}", sBc)

                    def _bcb(gpsimd, c=c):
                        wait(gpsimd, f"bcst_{c}")
                        if c >= 1:
                            wait(gpsimd, f"scanchunk_{c - 1}")
                        base_el = (c % 2) * 2 * NST * TC
                        repb = bass.AP(bc_d.ap().tensor, base_el,
                                       [[0, 128], [TC, NST], [1, TC]])
                        repc = bass.AP(bc_d.ap().tensor, base_el + NST * TC,
                                       [[0, 128], [TC, NST], [1, TC]])
                        sBcb.inc(gpsimd.dma_start(out=b_bc[:], in_=repb), 16)
                        sBcb.inc(gpsimd.dma_start(out=c_bc[:], in_=repc), 16)
                    em("gpsimd", _bcb)
                    snap(f"bcast_{c}", sBcb)

        def emit_A(c):
            for piece in range(8):
                emit_A_piece(c, piece)

        def emit_scan_m(c, m):
            gm = c * 8 + m

            def mk_da2(np_):
                n0 = 2 * np_
                i0 = c * 128 + m * 16 + n0

                def _da2(scalar, c=c, m=m, n0=n0, i0=i0):
                    if i0 == 0:
                        wait(scalar, "prep")
                    for j in range(2):
                        i = i0 + j
                        n = n0 + j
                        if i >= 6:
                            wait(scalar, f"scan_{i - 6}")
                        scalar.activation(
                            out=da_sb[i % 6][:], in_=dt_sb[:, m, :], func=AF.Exp,
                            scale=a_sb[:, m * 16 + n:m * 16 + n + 1])
                em("scalar", _da2)
                snap(f"da_{i0}", sSC)
                snap(f"da_{i0 + 1}", sSC)

            def mk_sc2(np_):
                n0 = 2 * np_
                i0 = c * 128 + m * 16 + n0

                def _sc2(vector, c=c, m=m, n0=n0, i0=i0):
                    vector.set_lag(1 if n0 == 0 else 2)
                    wait(vector, f"da_{i0 + 1}")
                    for j in range(2):
                        n = n0 + j
                        k = m * 16 + n
                        init = 0.0 if c == 0 else hstate[:, k:k + 1]
                        vector.tensor_tensor_scan(
                            out=h_all[:, n, :], data0=da_sb[(i0 + j) % 6][:],
                            data1=h_all[:, n, :], initial=init,
                            op0=ALU.mult, op1=ALU.add)
                em("vector", _sc2)
                snap(f"scan_{i0}", sVE)
                snap(f"scan_{i0 + 1}", sVE)

            for p in range(3):
                mk_da2(p)

            def _dbx(vector, c=c, m=m):
                vector.set_lag(1)
                if m == 0:
                    wait(vector, f"bcast_{c}")
                    wait(vector, f"udone_{c}")
                urep = bass.AP(u_sb[:].tensor, m * TC,
                               [[8 * TC, 128], [0, NST], [1, TC]])
                vector.tensor_tensor(out=h_all[:], in0=b_bc[:], in1=urep,
                                     op=ALU.mult)
            em("vector", _dbx)

            for p in range(8):
                mk_sc2(p)
                if p + 3 <= 7:
                    mk_da2(p + 3)

            def _tail(vector, c=c, m=m, gm=gm):
                vector.set_lag(1)
                vector.tensor_copy(
                    hstate[:, m * 16:(m + 1) * 16],
                    h_all[:, :, TC - 1:TC].rearrange("p n o -> p (n o)"))
                vector.tensor_tensor(out=h_all[:], in0=h_all[:], in1=c_bc[:],
                                     op=ALU.mult)
                if gm >= 1:
                    wait(vector, f"ycst_{gm - 1}")
                vector.tensor_tensor(out=h_all[:, 0:8, :], in0=h_all[:, 0:8, :],
                                     in1=h_all[:, 8:16, :], op=ALU.add)
                vector.tensor_tensor(out=h_all[:, 0:4, :], in0=h_all[:, 0:4, :],
                                     in1=h_all[:, 4:8, :], op=ALU.add)
                vector.tensor_tensor(out=h_all[:, 0:2, :], in0=h_all[:, 0:2, :],
                                     in1=h_all[:, 2:4, :], op=ALU.add)
                vector.tensor_tensor(out=y32[:], in0=h_all[:, 0, :],
                                     in1=h_all[:, 1, :], op=ALU.add)
            em("vector", _tail)
            snap(f"yred_{gm}", sVE)

            def _yc(scalar, c=c, m=m, gm=gm):
                wait(scalar, f"yred_{gm}")
                if c >= 2 and m == 0:
                    wait(scalar, f"opdone_{c - 2}")
                scalar.copy(y_bf[c % 2][:, m, :], y32[:])
            em("scalar", _yc)
            snap(f"ycst_{gm}", sSC)

        def emit_outproj(cc):
            for mo in range(8):
                def _wl(gpsimd, cc=cc, mo=mo):
                    gi4 = cc * 8 + mo
                    if gi4 >= 2:
                        wait(gpsimd, f"opmm_{cc}_{(mo - 2) * 2 + 1}" if mo >= 2
                             else f"opmm_{cc - 1}_{(mo + 6) * 2 + 1}")
                    sWo[mo % 2].inc(gpsimd.dma_start(
                        out=woutb[mo % 2][:].rearrange('p k c -> p (k c)'),
                        in_=wout[mo].rearrange('p k c -> p (k c)')), 16)
                em("gpsimd", _wl)
                snap(f"opwl_{cc}_{mo}", sWo[mo % 2])

                for tt in range(2):
                    gi = mo * 2 + tt

                    holdg = {}

                    def _g(tensor, cc=cc, tt=tt, mo=mo, gi=gi, hold=holdg):
                        if gi == 0:
                            wait(tensor, f"ycast_{cc}")
                            wait(tensor, "prep")
                        if tt == 0:
                            wait(tensor, f"opwl_{cc}_{mo}")
                        b = hold.setdefault("b", pool.take(tensor))
                        last = None
                        for kk in range(8):
                            last = tensor.matmul(
                                ps[b][:, 0:256],
                                y_bf[cc % 2][:, kk, tt * 128:(tt + 1) * 128],
                                woutb[mo % 2][:, kk, :],
                                start=(kk == 0), stop=(kk == 7))
                        sPE.inc(last)
                    em("tensor", _g)
                    snap(f"opmm_{cc}_{gi}", sPE)

                    def _e(vector, cc=cc, tt=tt, mo=mo, gi=gi, hold=holdg):
                        vector.set_lag(1)
                        wait(vector, f"opmm_{cc}_{gi}")
                        if gi == 0 and cc > 0:
                            wait(vector, f"mixdma_{cc - 1}")
                        vector.tensor_tensor(
                            out=mixp_sb[:, tt, mo * 256:(mo + 1) * 256],
                            in0=ps[hold["b"]][:, 0:256],
                            in1=outbh_sb[:, mo * 256:(mo + 1) * 256], op=ALU.add)
                    em("vector", _e)
                    snap(f"opev_{cc}_{gi}", sVE)

                    def _giveop(hold=holdg, key=f"opev_{cc}_{gi}"):
                        pool.give(hold["b"], key)
                    emc(_giveop)
                if mo < 7:
                    piece_break()
            snap(f"mixevac_{cc}", sVE)
            snap(f"opdone_{cc}", sPE)

            def _d(gpsimd, cc=cc):
                wait(gpsimd, f"mixevac_{cc}")
                dst = mixp_d.ap()[cc * TC:(cc + 1) * TC, :].rearrange(
                    "(tt p) m -> p tt m", p=128)
                sMixp.inc(gpsimd.dma_start(out=dst, in_=mixp_sb[:]), 16)
            em("gpsimd", _d)
            snap(f"mixdma_{cc}", sMixp)

            if cc % 2 == 1:
                gg = cc // 2

                def _rs(gpsimd, cc=cc, gg=gg):
                    wait(gpsimd, f"mixdma_{cc}")
                    sCC.inc(gpsimd.collective_compute(
                        "ReduceScatter", ALU.add,
                        replica_groups=[[0, 1], [2, 3], [4, 5], [6, 7]],
                        ins=[mixp_d.ap()[gg * 512:(gg + 1) * 512, :]],
                        outs=[mixg_d.ap()[gg]]))
                em("gpsimd", _rs)
                snap(f"rs_{gg}", sCC)

        def emit_C_light(g):
            for tt in range(2):
                ti = g * 2 + tt

                def _gldx(gpsimd, g=g, tt=tt, ti=ti):
                    if ti >= 1:
                        wait(gpsimd, f"x1_{(ti - 1) // 2}_{(ti - 1) % 2}")
                    sX0m.inc(gpsimd.dma_start(
                        out=x0m_t[:],
                        in_=x0m[g][tt * 128:(tt + 1) * 128, :]), 16)
                em("gpsimd", _gldx)
                snap(f"gldx_{ti}", sX0m)

                def _gld(gpsimd, g=g, tt=tt, ti=ti):
                    wait(gpsimd, f"rs_{g}")
                    if ti >= 1:
                        wait(gpsimd, f"mixused_{ti - 1}")
                    sMg.inc(gpsimd.dma_start(
                        out=mixg_t[:],
                        in_=mixg_d.ap()[g][tt * 128:(tt + 1) * 128, :]), 16)
                em("gpsimd", _gld)
                snap(f"gldm_{ti}", sMg)

                def _ws(scalar, k1=f"gldm_{ti}"):
                    wait(scalar, k1)
                em("scalar", _ws)
                slot, li = emit_ln_stats(lambda: mixg_t[:], fam=1)
                piece_break()

                for q in range(4):
                    def _nmQ(scalar, ti=ti, slot=slot, li=li, q=q):
                        if q == 0:
                            wait(scalar, f"lnstat_done_{li}")
                            if ti >= 1:
                                wait(scalar, f"x1q_{ti - 1}_3")
                        else:
                            wait(scalar, f"x1q_{ti}_{q - 1}")
                        scalar.activation(out=scr[:],
                                          in_=mixg_t[:, q * 512:(q + 1) * 512],
                                          func=AF.Identity,
                                          scale=srstd[:, slot:slot + 1],
                                          bias=snegmr[:, slot:slot + 1])
                    em("scalar", _nmQ)
                    snap(f"lnmQ_{ti}_{q}", sSC)
                    if q == 3:
                        ln_norm_key[li] = f"lnmQ_{ti}_{q}"

                    def _x1Q(vector, g=g, tt=tt, ti=ti, q=q):
                        vector.set_lag(1)
                        wait(vector, f"lnmQ_{ti}_{q}")
                        if q == 0:
                            wait(vector, f"gldx_{ti}")
                            if ti >= 1:
                                pg, pt = (ti - 1) // 2, (ti - 1) % 2
                                wait(vector, f"ln2tm_{pg}_{pt}")
                                wait(vector, f"x1st_{pg}_{pt}")
                        vector.tensor_tensor(
                            out=x1_sb[:, q * 512:(q + 1) * 512],
                            in0=scr[:], in1=x0m_t[:, q * 512:(q + 1) * 512],
                            op=ALU.add)
                    em("vector", _x1Q)
                    snap(f"x1q_{ti}_{q}", sVE)
                snap(f"x1_{g}_{tt}", sVE)

                def _x1st(gpsimd, g=g, tt=tt):
                    wait(gpsimd, f"x1_{g}_{tt}")
                    sX1s[tt].inc(gpsimd.dma_start(
                        out=out[g][tt * 128:(tt + 1) * 128, :],
                        in_=x1_sb[:]), 16)
                em("gpsimd", _x1st)
                snap(f"x1st_{g}_{tt}", sX1s[tt])

                def _ws2(scalar, k=f"x1_{g}_{tt}"):
                    wait(scalar, k)
                em("scalar", _ws2)
                slot2, li2 = emit_ln_stats(lambda: x1_sb[:], fam=1)

                def _nm2(scalar, g=g, tt=tt, slot=slot2, li=li2):
                    wait(scalar, f"lnstat_done_{li}")
                    if g >= 1:
                        wait(scalar, f"ln2tr_{g - 1}")
                    scalar.activation(out=ln2_tm[:, tt, :],
                                      in_=x1_sb[:],
                                      func=AF.Identity,
                                      scale=srstd[:, slot:slot + 1],
                                      bias=snegmr[:, slot:slot + 1])
                em("scalar", _nm2)
                snap(f"ln2tm_{g}_{tt}", sSC)
                ln_norm_key[li2] = f"ln2tm_{g}_{tt}"
                snap(f"mixused_{ti}", sSC)
                piece_break()

            for tt in range(2):
                emit_transposes(
                    (lambda dc, tt=tt: ln2_tm[:, tt, dc * 128:(dc + 1) * 128]),
                    (lambda grp, tt=tt: ln2_fm[:, grp * 8:grp * 8 + 8,
                                               tt * 128:(tt + 1) * 128]),
                    f"ln2tm_{g}_{tt}",
                    f"ffdone_{g - 1}" if (tt == 0 and g >= 1) else None,
                    fam=1)
            snap(f"ln2tr_{g}", sPE)
            snap(f"ln2fm_{g}", sSC)

            def _rsum(tensor, g=g):
                wait(tensor, f"ln2fm_{g}")
                if g >= 1:
                    wait(tensor, f"shnev_{g - 1}")
                last = None
                for kk in range(16):
                    last = tensor.matmul(psr[0:1, 0:TG], onesc_sb[:, 0:1],
                                         ln2_fm[:, kk, :],
                                         start=(kk == 0), stop=(kk == 15))
                sPE.inc(last)
            em("tensor", _rsum)
            snap(f"rsum_{g}", sPE)

            def _rsev(scalar, g=g):
                wait(scalar, f"rsum_{g}")
                scalar.activation(out=rsn_sb[:], in_=psr[0:1, 0:TG],
                                  func=AF.Identity,
                                  scale=ct[0:1, C_NEG128:C_NEG128 + 1])
            em("scalar", _rsev)
            snap(f"rsnev_{g}", sSC)

        def emit_C_heavy(g):
            for ic in range(64):
                gic = g * 64 + ic

                def _wld(sync, g=g, ic=ic, gic=gic):
                    if gic == 0:
                        wait(sync, "precast")
                    if gic >= 2:
                        wait(sync, f"gumm_{gic - 2}")
                    sGU[ic % 2].inc(sync.dma_start(
                        out=wgb[ic % 2][:].rearrange('p k c -> p (k c)'),
                        in_=wg_bf.ap()[ic].rearrange('p k c -> p (k c)')), 16)
                    sGU[ic % 2].inc(sync.dma_start(
                        out=wub[ic % 2][:].rearrange('p k c -> p (k c)'),
                        in_=wu_bf.ap()[ic].rearrange('p k c -> p (k c)')), 16)
                em("sync", _wld)
                snap(f"wld_{gic}", sGU[ic % 2])

                holdgu = {}

                def _gu(tensor, g=g, ic=ic, gic=gic, hold=holdgu):
                    wait(tensor, f"wld_{gic}")
                    if ic == 0:
                        wait(tensor, f"rsnev_{g}")
                    if "b" not in hold:
                        hold["b"] = (pool.take(tensor), pool.take(tensor))
                    else:
                        pool.take(tensor)
                        pool.take(tensor)
                    bg, bu = hold["b"]
                    if ic >= SHLAG:
                        wait(tensor, f"hh_{g}_{ic - SHLAG}")
                        tensor.matmul(psr[0:1, TG:2 * TG], onesc_sb[:, 0:1],
                                      hh_sb[:, ic - SHLAG, :],
                                      start=(ic - SHLAG == 0), stop=False)
                    for kk in range(16):
                        tensor.matmul(ps[bg][:, 0:TG], wgb[ic % 2][:, kk, :],
                                      ln2_fm[:, kk, :], start=(kk == 0), stop=False)
                    tensor.matmul(ps[bg][:, 0:TG], onesr_sb[0:1, 0:128],
                                  rsn_sb[0:1, :], start=False, stop=True)
                    for kk in range(16):
                        tensor.matmul(ps[bu][:, 0:TG], wub[ic % 2][:, kk, :],
                                      ln2_fm[:, kk, :], start=(kk == 0), stop=False)
                    sPE.inc(tensor.matmul(ps[bu][:, 0:TG], onesr_sb[0:1, 0:128],
                                          rsn_sb[0:1, :], start=False, stop=True))
                em("tensor", _gu)
                snap(f"gumm_{gic}", sPE)

                def _guev(scalar, g=g, ic=ic, gic=gic, hold=holdgu):
                    wait(scalar, f"gumm_{gic}")
                    if ic >= 2:
                        wait(scalar, f"hh_{g}_{ic - 2}")
                    elif g >= 1:
                        wait(scalar, f"hh_{g - 1}_{62 + ic}")
                    bg, bu = hold["b"]
                    scalar.activation(out=g_b[ic % 2][:], in_=ps[bg][:, 0:TG],
                                      func=AF.Silu, scale=gsc_sb[:, ic:ic + 1])
                em("scalar", _guev)
                snap(f"guev_{gic}", sSC)

                def _hh(vector, g=g, ic=ic, gic=gic, hold=holdgu):
                    vector.set_lag(1)
                    wait(vector, f"guev_{gic}")
                    if ic == 0 and g >= 1:
                        wait(vector, f"downdone_{g - 1}")
                    bg, bu = hold["b"]
                    vector.scalar_tensor_tensor(
                        out=hh_sb[:, ic, :], in0=ps[bu][:, 0:TG],
                        scalar=usc_sb[:, ic:ic + 1], in1=g_b[ic % 2][:],
                        op0=ALU.mult, op1=ALU.mult)
                em("vector", _hh)
                snap(f"hh_{g}_{ic}", sVE)

                def _givegu(hold=holdgu, keyg=f"guev_{gic}", keyu=f"hh_{g}_{ic}"):
                    bg, bu = hold["b"]
                    pool.give(bg, keyg)
                    pool.give(bu, keyu)
                emc(_givegu)
                if ic % 5 == 4 or ic == 63:
                    piece_break()

            def _shf(tensor, g=g):
                last = None
                for ic in range(64 - SHLAG, 64):
                    wait(tensor, f"hh_{g}_{ic}")
                    last = tensor.matmul(psr[0:1, TG:2 * TG], onesc_sb[:, 0:1],
                                         hh_sb[:, ic, :], start=False,
                                         stop=(ic == 63))
                sPE.inc(last)
            em("tensor", _shf)
            snap(f"shmm_{g}", sPE)
            snap(f"ffdone_{g}", sPE)

            def _shev(scalar, g=g):
                wait(scalar, f"shmm_{g}")
                scalar.activation(out=shn_sb[:], in_=psr[0:1, TG:2 * TG],
                                  func=AF.Identity,
                                  scale=ct[0:1, C_NEG128:C_NEG128 + 1])
            em("scalar", _shev)
            snap(f"shnev_{g}", sSC)

            bdown = []

            def _takedown(tensor, g=g, bd=bdown):
                if not pool.replay or len(bd) == 0:
                    bd.append(pool.take(tensor))
                    bd.append(pool.take(tensor))
                else:
                    pool.take(tensor)
                    pool.take(tensor)
            em("tensor", _takedown)
            for ho in range(4):
                gho = g * 4 + ho
                for icg in range(16):
                    gw = gho * 16 + icg

                    def _wdl(sync, ho=ho, icg=icg, gw=gw):
                        if gw >= 2:
                            wait(sync, f"wdmm_{gw - 2}")
                        if gw == 0:
                            wait(sync, "precast")
                        src = wd_bf.ap()[ho, icg * 4:(icg + 1) * 4].rearrange(
                            "u p c -> p u c")
                        sWd[icg % 2].inc(sync.dma_start(out=wdb[icg % 2][:],
                                                        in_=src), 16)
                    em("sync", _wdl)
                    snap(f"wdl_{gw}", sWd[icg % 2])

                    def _dmm(tensor, g=g, ho=ho, icg=icg, gw=gw, gho=gho, bd=bdown):
                        wait(tensor, f"wdl_{gw}")
                        if icg == 0:
                            wait(tensor, f"shnev_{g}")
                            wait(tensor, f"hh_{g}_{63}")
                            if ho >= 1:
                                wait(tensor, f"dnev_{gho - 1}")
                        last = None
                        for u in range(4):
                            ic = icg * 4 + u
                            for tt in range(2):
                                last = tensor.matmul(
                                    ps[bd[tt]][:],
                                    hh_sb[:, ic, tt * 128:(tt + 1) * 128],
                                    wdb[icg % 2][:, u, :],
                                    start=(ic == 0), stop=False)
                        if icg == 15:
                            for tt in range(2):
                                last = tensor.matmul(
                                    ps[bd[tt]][:],
                                    shn_sb[0:1, tt * 128:(tt + 1) * 128],
                                    onesr_sb[0:1, 0:512], start=False, stop=True)
                        sPE.inc(last)
                    em("tensor", _dmm)
                    snap(f"wdmm_{gw}", sPE)

                def _dnev(vector, g=g, ho=ho, gho=gho, bd=bdown):
                    vector.set_lag(1)
                    wait(vector, f"wdmm_{gho * 16 + 15}")
                    if g == 0 and ho == 0:
                        wait(vector, "pre")
                    if ho == 0 and g >= 1:
                        wait(vector, f"outdma_{g - 1}")
                    for tt in range(2):
                        vector.tensor_tensor(
                            out=ffn32[:, tt, ho * 512:(ho + 1) * 512],
                            in0=ps[bd[tt]][:],
                            in1=dsc_sb[:, ho * 512:(ho + 1) * 512], op=ALU.mult)
                em("vector", _dnev)
                snap(f"dnev_{gho}", sVE)
                if ho in (0, 2):
                    piece_break()

            def _givedown(bd=bdown, key=f"dnev_{g * 4 + 3}"):
                pool.give(bd[0], key)
                pool.give(bd[1], key)
            emc(_givedown)
            snap(f"downdone_{g}", sVE)
            snap(f"ffnout_{g}", sVE)

            for tt in range(2):
                def _ws3(scalar, k=f"ffnout_{g}"):
                    wait(scalar, k)
                em("scalar", _ws3)
                slot3, li3 = emit_ln_stats(lambda g=g, tt=tt: ffn32[:, tt, :],
                                           fam=2)

                def _nm3(scalar, g=g, tt=tt, slot=slot3, li=li3):
                    wait(scalar, f"lnstat_done_{li}")
                    scalar.activation(out=ffn32[:, tt, :], in_=ffn32[:, tt, :],
                                      func=AF.Identity,
                                      scale=srstd[:, slot:slot + 1],
                                      bias=snegmr[:, slot:slot + 1])
                em("scalar", _nm3)
                snap(f"ln3_{g}_{tt}", sSC)
                ln_norm_key[li3] = f"ln3_{g}_{tt}"

            def _ost(gpsimd, g=g):
                wait(gpsimd, f"ln3_{g}_0")
                wait(gpsimd, f"ln3_{g}_1")
                wait(gpsimd, f"x1st_{g}_0")
                wait(gpsimd, f"x1st_{g}_1")
                dst = out[g].rearrange("(tt p) m -> p tt m", p=128)
                sST.inc(gpsimd.dma_start(out=dst, in_=ffn32[:],
                                         accum_op=ALU.add), 16)
            em("gpsimd", _ost)
            snap(f"outdma_{g}", sST)

        emit_A(0)
        lq = []
        hq = []
        ldone = [-1]

        def pop_lanes():
            if lq:
                g, p = lq.pop(0)
                p()
                if not lq or lq[0][0] != g:
                    ldone[0] = g
            if hq and hq[0][0] <= ldone[0]:
                hq.pop(0)[1]()

        def enqueue_C(g):
            lq.extend((g, p) for p in record_pieces(
                lambda g=g: emit_C_light(g)))
            hq.extend((g, p) for p in record_pieces(
                lambda g=g: emit_C_heavy(g)))

        for c in range(NT):
            opp = (record_pieces(lambda cc=c - 1: emit_outproj(cc))
                   if c >= 1 else None)
            if c >= 3 and (c - 3) % 2 == 0:
                enqueue_C((c - 3) // 2)
            for m in range(8):
                emit_scan_m(c, m)
                if m == 7:
                    snap(f"scanchunk_{c}", sVE)
                    snap(f"ycast_{c}", sSC)
                emit_A_piece(c + 1, m)
                if opp is not None:
                    opp[m]()
                pop_lanes()
        opp = record_pieces(lambda: emit_outproj(NT - 1))
        for m in range(8):
            opp[m]()
            pop_lanes()
        enqueue_C(7)
        while lq or hq:
            pop_lanes()

        def _final(sync):
            sync.wait_ge(sST.h, sST.n)
        em("sync", _final)

        mock = _MockEng()
        mock_sc = ChainEng(mock, sSC)
        mock_ve = ChainEng(mock, sVE)
        for eng_name, payload in order:
            if eng_name == "__snap__":
                key, sem = payload
                ev[key] = (sem, sem.n)
            elif eng_name == "__call__":
                payload()
            elif eng_name == "scalar":
                payload(mock_sc)
            elif eng_name == "vector":
                payload(mock_ve)
            else:
                payload(mock)
        for s_ in ALL_SEMS:
            s_.n = 0
        pool.replay = True
        pool.ri = 0

        blk.sync(lambda eng: [f(eng) for f in sched["sync"]])
        blk.gpsimd(lambda eng: [f(eng) for f in sched["gpsimd"]])
        blk.scalar(lambda eng: [f(ChainEng(eng, sSC)) for f in sched["scalar"]])
        blk.vector(lambda eng: [f(ChainEng(eng, sVE)) for f in sched["vector"]])
        blk.tensor(lambda eng: [f(eng) for f in sched["tensor"]])

    return nc


def _prep_inputs(k, x, A_log, x_proj_w, dt_proj_w, dt_proj_b, out_proj_w, shared):
    b, j = k >> 1, k & 1
    d0, d1 = j * DH, (j + 1) * DH
    perm = np.concatenate([np.arange(d0, d1), np.arange((1 - j) * DH, (2 - j) * DH)])

    xbv = np.ascontiguousarray(x[b][:, perm], dtype=np.float32)
    x0mv = np.ascontiguousarray(x[b].reshape(NG, 2, TG, D)[:, j], dtype=np.float32)

    wdtT = dt_proj_w.T
    wdt_k = np.ascontiguousarray(
        wdtT[perm][:, d0:d1].reshape(16, 128, 8, 128).transpose(2, 1, 0, 3),
        dtype=np.float32)
    wxp_k = np.ascontiguousarray(
        x_proj_w.T[perm].reshape(16, 128, 2 * NST).transpose(1, 0, 2),
        dtype=np.float32)
    wout_k = np.ascontiguousarray(
        out_proj_w.T[d0:d1].reshape(8, 128, 8, 256).transpose(2, 1, 0, 3),
        dtype=np.float32)
    alog_k = np.ascontiguousarray(
        A_log[d0:d1].reshape(8, 128, NST).transpose(1, 0, 2).reshape(128, 128),
        dtype=np.float32)
    dtb_k = np.ascontiguousarray(dt_proj_b[d0:d1].reshape(8, 128).T,
                                 dtype=np.float32)

    return dict(xb=xbv, x0m=x0mv, wdt=wdt_k, wxp=wxp_k, wout=wout_k, alog=alog_k,
                dtb=dtb_k, **shared)


def kernel(L0_attn_x, L0_attn_A_log, L0_attn_x_proj_w, L0_attn_x_proj_b,
           L0_attn_dt_proj_w, L0_attn_dt_proj_b, L0_attn_out_proj_w,
           L0_attn_out_proj_b, L0_ffn_gate_w, L0_ffn_gate_scale, L0_ffn_up_w,
           L0_ffn_up_scale, L0_ffn_down_w, L0_ffn_down_scale):
    x = np.asarray(L0_attn_x, dtype=np.float32)
    A_log = np.asarray(L0_attn_A_log, dtype=np.float32)
    x_proj_w = np.asarray(L0_attn_x_proj_w, dtype=np.float32)
    x_proj_b = np.asarray(L0_attn_x_proj_b, dtype=np.float32)
    dt_proj_w = np.asarray(L0_attn_dt_proj_w, dtype=np.float32)
    dt_proj_b = np.asarray(L0_attn_dt_proj_b, dtype=np.float32)
    out_proj_w = np.asarray(L0_attn_out_proj_w, dtype=np.float32)
    out_proj_b = np.asarray(L0_attn_out_proj_b, dtype=np.float32)
    gate_w = np.asarray(L0_ffn_gate_w)
    up_w = np.asarray(L0_ffn_up_w)
    down_w = np.asarray(L0_ffn_down_w)
    gate_scale = np.asarray(L0_ffn_gate_scale, dtype=np.float32)
    up_scale = np.asarray(L0_ffn_up_scale, dtype=np.float32)
    down_scale = np.asarray(L0_ffn_down_scale, dtype=np.float32)

    wg8v = np.ascontiguousarray(
        gate_w.astype(np.uint8).T.reshape(16, 128, 64, 128).transpose(2, 1, 0, 3))
    wu8v = np.ascontiguousarray(
        up_w.astype(np.uint8).T.reshape(16, 128, 64, 128).transpose(2, 1, 0, 3))
    wd8v = np.ascontiguousarray(
        down_w.astype(np.uint8).T.reshape(64, 128, 4, 512).transpose(2, 0, 1, 3))
    shared = dict(
        xpb=np.ascontiguousarray(x_proj_b.reshape(2 * NST, 1)),
        outb_bc=np.ascontiguousarray(np.broadcast_to(out_proj_b, (128, D))),
        wg8=wg8v, wu8=wu8v, wd8=wd8v,
        gsc=np.ascontiguousarray(gate_scale.reshape(64, 128).T),
        usc=np.ascontiguousarray(up_scale.reshape(64, 128).T),
        dsc_bc=np.ascontiguousarray(np.broadcast_to(down_scale, (128, D))),
        ident=np.eye(128, dtype=BF),
        onesc=np.ones((128, 1), dtype=BF),
        onesr=np.ones((1, 512), dtype=BF),
        consts=np.ascontiguousarray(np.broadcast_to(np.array(
            [1e-5, 1.0 / D, -1.0, 1.0, -0.5, -128.0, 0.5, 0.0], np.float32),
            (128, NCONST))),
    )

    nc = build_nc()
    in_maps = [_prep_inputs(k, x, A_log, x_proj_w, dt_proj_w, dt_proj_b,
                            out_proj_w, shared) for k in range(8)]
    res = run_bass_kernel_spmd(nc, in_maps, core_ids=list(range(8)))

    outv = np.empty((B, L, D), dtype=np.float32)
    for k in range(8):
        b, j = k >> 1, k & 1
        outv[b].reshape(NG, 2, TG, D)[:, j] = res.results[k]["out"]
    return outv


if __name__ == "__main__":
    nc = build_nc()
    print("built; instructions:", nc.n_instructions(),
          "sbuf remaining:", nc.sbuf_bytes_remaining)


# version 10
# speedup vs baseline: 1.3740x; 1.0589x over previous
from contextlib import ExitStack

import numpy as np
import ml_dtypes

import concourse.bass as bass
import concourse.mybir as mybir
from concourse.bass_utils import run_bass_kernel_spmd

F32 = mybir.dt.float32
BF16 = mybir.dt.bfloat16
U8 = mybir.dt.uint8
AF = mybir.ActivationFunctionType
ALU = mybir.AluOpType
AX = mybir.AxisListType
BF = ml_dtypes.bfloat16

B, L, D = 4, 4096, 2048
DH = 1024
NST = 16
TC = 256
NT = L // TC
TG = 256
NG = 8
SHLAG = 16

C_EPS, C_INVD, C_NEG1, C_ONE, C_NEGH, C_NEG128, C_HALF = range(7)
NCONST = 8


class Sem:
    def __init__(self, es, nc, name):
        self.h = es.enter_context(nc.semaphore(name))
        self.n = 0

    def inc(self, instr, k=1):
        instr.then_inc(self.h, k)
        self.n += k
        return self.n


class _MockInstr:
    def then_inc(self, h, k):
        return self


class _MockEng:
    def __getattr__(self, name):
        return lambda *a, **kw: _MockInstr()


_UNCHAINED = {"wait_ge", "sem_inc", "dma_start", "collective_compute",
              "then_inc", "nop", "engine_nop"}


class ChainEng:

    def __init__(self, eng, sem):
        object.__setattr__(self, "_e", eng)
        object.__setattr__(self, "_s", sem)
        object.__setattr__(self, "_lag", 1)

    def set_lag(self, k):
        object.__setattr__(self, "_lag", k)

    def __getattr__(self, name):
        f = getattr(self._e, name)
        if name in _UNCHAINED or name.startswith("_"):
            return f

        def g(*a, **kw):
            w = self._s.n - (self._lag - 1)
            if w > 0:
                self._e.wait_ge(self._s.h, w)
            r = f(*a, **kw)
            self._s.inc(r, 1)
            return r
        return g


def build_nc():
    nc = bass.Bass("TRN2", target_bir_lowering=False)

    xb = nc.dram_tensor("xb", [L, D], F32, kind="ExternalInput").ap()
    x0m = nc.dram_tensor("x0m", [NG, TG, D], F32, kind="ExternalInput").ap()
    wdt = nc.dram_tensor("wdt", [8, 128, 16, 128], F32, kind="ExternalInput").ap()
    wxp = nc.dram_tensor("wxp", [128, 16, 2 * NST], F32, kind="ExternalInput").ap()
    wout = nc.dram_tensor("wout", [8, 128, 8, 256], F32, kind="ExternalInput").ap()
    alog = nc.dram_tensor("alog", [128, 128], F32, kind="ExternalInput").ap()
    dtb = nc.dram_tensor("dtb", [128, 8], F32, kind="ExternalInput").ap()
    xpb = nc.dram_tensor("xpb", [2 * NST, 1], F32, kind="ExternalInput").ap()
    outb_bc = nc.dram_tensor("outb_bc", [128, D], F32, kind="ExternalInput").ap()
    wg8 = nc.dram_tensor("wg8", [64, 128, 16, 128], U8, kind="ExternalInput").ap()
    wu8 = nc.dram_tensor("wu8", [64, 128, 16, 128], U8, kind="ExternalInput").ap()
    wd8 = nc.dram_tensor("wd8", [4, 64, 128, 512], U8, kind="ExternalInput").ap()
    gsc = nc.dram_tensor("gsc", [128, 64], F32, kind="ExternalInput").ap()
    usc = nc.dram_tensor("usc", [128, 64], F32, kind="ExternalInput").ap()
    dsc_bc = nc.dram_tensor("dsc_bc", [128, D], F32, kind="ExternalInput").ap()
    ident = nc.dram_tensor("ident", [128, 128], BF16, kind="ExternalInput").ap()
    onesc = nc.dram_tensor("onesc", [128, 1], BF16, kind="ExternalInput").ap()
    onesr = nc.dram_tensor("onesr", [1, 512], BF16, kind="ExternalInput").ap()
    consts = nc.dram_tensor("consts", [128, NCONST], F32, kind="ExternalInput").ap()

    out = nc.dram_tensor("out", [NG, TG, D], F32, kind="ExternalOutput").ap()

    wg_bf = nc.dram_tensor("wg_bf", [64, 128, 16, 128], BF16)
    wu_bf = nc.dram_tensor("wu_bf", [64, 128, 16, 128], BF16)
    wd_bf = nc.dram_tensor("wd_bf", [4, 64, 128, 512], BF16)
    bc_d = nc.dram_tensor("bc_d", [2, 2 * NST, TC], BF16)
    mixp_d = nc.dram_tensor("mixp_d", [L, D], BF16)
    mixg_d = nc.dram_tensor("mixg_d", [NG, TG, D], BF16)

    es = ExitStack()
    sb = lambda name, shape, dt: es.enter_context(nc.sbuf_tensor(name, shape, dt))

    wxp_sb = sb("wxp_sb", [128, 16, 2 * NST], BF16)
    a_sb = sb("a_sb", [128, 128], F32)
    dtb_sb = sb("dtb_sb", [128, 8], F32)
    xpb_sb = sb("xpb_sb", [2 * NST, 1], F32)
    gsc_sb = sb("gsc_sb", [128, 64], F32)
    usc_sb = sb("usc_sb", [128, 64], F32)
    ident_sb = sb("ident_sb", [128, 128], BF16)
    onesc_sb = sb("onesc_sb", [128, 1], BF16)
    onesr_sb = sb("onesr_sb", [1, 512], BF16)
    ct = sb("ct", [128, NCONST], F32)
    sx = sb("sx", [128, 6, 4], F32)
    sx2 = sb("sx2", [128, 6, 4], F32)
    smean = sb("smean", [128, 6], F32)
    snvar = sb("snvar", [128, 6], F32)
    sm2 = sb("sm2", [128, 6], F32)
    srstd = sb("srstd", [128, 6], F32)
    snegmr = sb("snegmr", [128, 6], F32)
    scr = sb("scr", [128, 512], F32)
    outbh_sb = sb("outbh_sb", [128, D], BF16)
    dsc_sb = sb("dsc_sb", [128, D], BF16)
    rsn_sb = sb("rsn_sb", [1, TG], BF16)
    shn_sb = sb("shn_sb", [1, TG], BF16)

    x0c = sb("x0c", [128, D], BF16)
    ln1_tm = sb("ln1tm", [128, D], BF16)
    ln1_fm = sb("ln1fm", [128, 16, TC], BF16)
    wdtb = [sb(f"wdtb{i}", [128, 16, 128], BF16) for i in range(2)]
    dt_sb = sb("dt_sb", [128, 8, TC], BF16)
    u_sb = sb("u_sb", [128, 8, TC], BF16)
    bc_sb = [sb(f"bcs{i}", [2 * NST, TC], BF16) for i in range(2)]
    b_bc = sb("b_bc", [128, NST, TC], BF16)
    c_bc = sb("c_bc", [128, NST, TC], BF16)
    da_sb = [sb(f"das{i}", [128, TC], F32) for i in range(6)]
    h_all = sb("h_all", [128, NST, TC], BF16)
    hstate = sb("hstate", [128, 128], F32)
    y32 = sb("y32", [128, TC], F32)
    y_bf = [sb(f"ybfs{i}", [128, 8, TC], BF16) for i in range(2)]
    woutb = [sb(f"woutbs{i}", [128, 8, 256], BF16) for i in range(2)]
    mixp_sb = sb("mixp_sb", [128, 2, D], BF16)
    mixg_t = sb("mixg_t", [128, D], BF16)
    x0m_t = sb("x0m_t", [128, D], BF16)
    x1_sb = sb("x1_sb", [128, D], BF16)
    ln2_tm = sb("ln2_tm", [128, 2, D], BF16)
    ln2_fm = sb("ln2_fm", [128, 16, TG], BF16)
    wgb = [sb(f"wgbs{i}", [128, 16, 128], BF16) for i in range(2)]
    wub = [sb(f"wubs{i}", [128, 16, 128], BF16) for i in range(2)]
    wdb = [sb(f"wdbs{i}", [128, 4, 512], BF16) for i in range(2)]
    g_b = [sb(f"g_bs{i}", [128, TG], BF16) for i in range(2)]
    hh_sb = sb("hh_sb", [128, 64, TG], BF16)
    ffn32 = sb("ffn32", [128, 2, D], F32)

    ps = [es.enter_context(nc.psum_tensor(f"ps{i}", [128, 512], F32)) for i in range(5)]
    pt0 = es.enter_context(nc.psum_tensor("pt0", [128, 1024], BF16))
    pt1 = es.enter_context(nc.psum_tensor("pt1", [128, 1024], BF16))
    psr = es.enter_context(nc.psum_tensor("psr", [128, 512], F32))

    sConst = Sem(es, nc, "sConst")
    sPre = Sem(es, nc, "sPre")
    sPc = Sem(es, nc, "sPc")
    sX0 = Sem(es, nc, "sX0")
    sWdt = [Sem(es, nc, f"sWdt{i}") for i in range(2)]
    sWo = [Sem(es, nc, f"sWo{i}") for i in range(2)]
    sMixp = Sem(es, nc, "sMixp")
    sBc = Sem(es, nc, "sBc")
    sBcb = Sem(es, nc, "sBcb")
    sMg = Sem(es, nc, "sMg")
    sX0m = Sem(es, nc, "sX0m")
    sGU = [Sem(es, nc, f"sGU{i}") for i in range(2)]
    sWd = [Sem(es, nc, f"sWd{i}") for i in range(2)]
    sX1s = [Sem(es, nc, f"sX1s{i}") for i in range(2)]
    sST = Sem(es, nc, "sST")
    sCC = Sem(es, nc, "sCC")
    sPE = Sem(es, nc, "sPE")
    sVE = Sem(es, nc, "sVE")
    sSC = Sem(es, nc, "sSC")
    ALL_SEMS = ([sConst, sPre, sPc, sMixp, sBc, sBcb, sX0m, sST, sCC,
                 sPE, sVE, sSC] + [sX0] + sWdt + sWo + [sMg] + sGU + sWd + sX1s)

    ev = {}
    sched = {"sync": [], "gpsimd": [], "scalar": [], "vector": [], "tensor": []}
    order = []
    rec = [None]

    def _emit(kind, payload):
        if rec[0] is not None:
            rec[0][-1].append((kind, payload))
        elif kind in ("__snap__", "__call__"):
            order.append((kind, payload))
        else:
            sched[kind].append(payload)
            order.append((kind, payload))

    def em(engine, fn):
        _emit(engine, fn)

    def snap(key, sem):
        _emit("__snap__", (key, sem))

    def emc(fn):
        _emit("__call__", fn)

    def piece_break():
        rec[0].append([])

    def record_pieces(emit_fn):
        buf = [[]]
        rec[0] = buf
        emit_fn()
        rec[0] = None
        pieces = []
        for plist in buf:
            if not plist:
                continue

            def piece(plist=plist):
                for kind, payload in plist:
                    _emit(kind, payload)
            pieces.append(piece)
        return pieces

    def wait(eng, key):
        sem, n = ev[key]
        if n > 0:
            eng.wait_ge(sem.h, n)

    class Pool:

        def __init__(self):
            self.q = [(b, None) for b in range(5)]
            self.log = []
            self.replay = False
            self.ri = 0

        def take(self, eng):
            if self.replay:
                b, key = self.log[self.ri]
                self.ri += 1
            else:
                b, key = self.q.pop(0)
                self.log.append((b, key))
            if key is not None:
                wait(eng, key)
            return b

        def give(self, b, key):
            if not self.replay:
                self.q.append((b, key))

    pool = Pool()

    with es:
        blk = es.enter_context(nc.Block())

        def _loads(sync):
            for ap_in, tile in ((alog, a_sb), (dtb, dtb_sb), (xpb, xpb_sb),
                                (gsc, gsc_sb), (usc, usc_sb), (ident, ident_sb),
                                (onesc, onesc_sb), (onesr, onesr_sb), (consts, ct)):
                sConst.inc(sync.dma_start(out=tile[:], in_=ap_in[:]), 16)
        em("sync", _loads)
        snap("loads0", sConst)

        def _pre(gpsimd):
            sPre.inc(gpsimd.dma_start(out=wxp_sb[:].rearrange('p k c -> p (k c)'),
                                      in_=wxp[:].rearrange('p k c -> p (k c)')), 16)
            sPre.inc(gpsimd.dma_start(out=outbh_sb[:], in_=outb_bc[:]), 16)
            sPre.inc(gpsimd.dma_start(out=dsc_sb[:], in_=dsc_bc[:]), 16)
        em("gpsimd", _pre)
        snap("pre", sPre)

        def _precast(gpsimd):
            def flat(ap, cols=32768):
                return ap.rearrange(
                    " ".join(f"d{i}" for i in range(len(ap.shape)))
                    + " -> (" + " ".join(f"d{i}" for i in range(len(ap.shape)))
                    + ")").rearrange("(x y) -> x y", y=cols)
            for srcw, dst in ((wg8, wg_bf), (wu8, wu_bf)):
                for q in range(4):
                    sPc.inc(gpsimd.dma_start(
                        out=flat(dst.ap()[q * 16:(q + 1) * 16]),
                        in_=flat(srcw[q * 16:(q + 1) * 16])), 16)
            for ho in range(4):
                sPc.inc(gpsimd.dma_start(out=flat(wd_bf.ap()[ho]),
                                         in_=flat(wd8[ho])), 16)
        em("gpsimd", _precast)
        snap("precast", sPc)

        def _prep_sc(scalar):
            wait(scalar, "loads0")
            scalar.activation(out=a_sb[:], in_=a_sb[:], func=AF.Exp)
        em("scalar", _prep_sc)
        snap("aexp", sSC)

        def _prep_ve(vector):
            vector.set_lag(1)
            wait(vector, "aexp")
            wait(vector, "pre")
            vector.tensor_scalar(out=a_sb[:], in0=a_sb[:],
                                 scalar1=ct[:, C_NEG1:C_NEG1 + 1],
                                 scalar2=None, op0=ALU.mult)
            vector.tensor_scalar(out=outbh_sb[:], in0=outbh_sb[:],
                                 scalar1=ct[:, C_HALF:C_HALF + 1],
                                 scalar2=None, op0=ALU.mult)
        em("vector", _prep_ve)
        snap("prep", sVE)

        ln_ctr = {0: 0, 1: 0, 2: 0}
        ln_norm_key = {}

        def emit_ln_stats(src_ap_fn, fam=0):
            ctrv = ln_ctr[fam]
            ln_ctr[fam] += 1
            s = 2 * fam + ctrv % 2
            i = fam * 1000 + ctrv

            def _s1(scalar, i=i, s=s):
                src = src_ap_fn()
                for q in range(4):
                    scalar.activation(out=scr[:], in_=src[:, q * 512:(q + 1) * 512],
                                      func=AF.Identity,
                                      accum_out=sx[:, s, q:q + 1])
                for q in range(4):
                    scalar.activation(out=scr[:], in_=src[:, q * 512:(q + 1) * 512],
                                      func=AF.Square, accum_out=sx2[:, s, q:q + 1])
                scalar.activation(out=scr[:, 0:4], in_=sx[:, s, :],
                                  func=AF.Identity, scale=ct[:, C_INVD:C_INVD + 1],
                                  accum_out=smean[:, s:s + 1])
                scalar.activation(out=scr[:, 0:4], in_=sx2[:, s, :],
                                  func=AF.Identity, scale=ct[:, C_INVD:C_INVD + 1],
                                  accum_out=snvar[:, s:s + 1])
                scalar.activation(out=sm2[:, s:s + 1], in_=smean[:, s:s + 1],
                                  func=AF.Square)
                scalar.activation(out=sm2[:, s:s + 1], in_=sm2[:, s:s + 1],
                                  func=AF.Identity, scale=ct[:, C_NEG1:C_NEG1 + 1])
                scalar.activation(out=snvar[:, s:s + 1], in_=snvar[:, s:s + 1],
                                  func=AF.Identity, bias=sm2[:, s:s + 1])
                scalar.activation(out=snvar[:, s:s + 1], in_=snvar[:, s:s + 1],
                                  func=AF.Ln, bias=ct[:, C_EPS:C_EPS + 1])
                scalar.activation(out=srstd[:, s:s + 1], in_=snvar[:, s:s + 1],
                                  func=AF.Exp, scale=ct[:, C_NEGH:C_NEGH + 1])
                scalar.activation(out=snegmr[:, s:s + 1], in_=smean[:, s:s + 1],
                                  func=AF.Identity, scale=srstd[:, s:s + 1])
                scalar.activation(out=snegmr[:, s:s + 1], in_=snegmr[:, s:s + 1],
                                  func=AF.Identity, scale=ct[:, C_NEG1:C_NEG1 + 1])
            em("scalar", _s1)
            snap(f"lnstat_done_{i}", sSC)
            return s, i

        tg_ctr = {0: 0, 1: 0}

        def emit_transposes(src_fn, dst_fn, ready_key, extra_key=None, fam=0):
            pt = pt0 if fam == 0 else pt1
            for grp in range(2):
                ctrv = tg_ctr[fam]
                tg_ctr[fam] += 1
                tg = fam * 1000 + ctrv

                def _tr(tensor, grp=grp, tg=tg, ctrv=ctrv, pt=pt):
                    if ctrv == 0:
                        wait(tensor, "loads0")
                    wait(tensor, ready_key)
                    if extra_key is not None and grp == 0:
                        wait(tensor, extra_key)
                    if ctrv >= 1:
                        wait(tensor, f"trev_{tg - 1}")
                    last = None
                    for dcq in range(8):
                        last = tensor.transpose(pt[:, dcq * 128:(dcq + 1) * 128],
                                                src_fn(grp * 8 + dcq), ident_sb[:])
                    sPE.inc(last)
                em("tensor", _tr)
                snap(f"trmm_{tg}", sPE)

                def _trev(vector, grp=grp, tg=tg, pt=pt):
                    vector.set_lag(1)
                    wait(vector, f"trmm_{tg}")
                    vector.tensor_copy(dst_fn(grp),
                                       pt[:].rearrange("p (dc t) -> p dc t", dc=8))
                em("vector", _trev)
                snap(f"trev_{tg}", sVE)

        def emit_A_piece(c, piece):
            if c >= NT:
                return
            if piece in (0, 2):
                si = piece // 2
                sig = c * 2 + si

                def _ld(gpsimd, sig=sig, c=c, si=si):
                    if sig >= 1:
                        wait(gpsimd, f"ln1tm_{sig - 1}")
                    r0 = c * TC + si * 128
                    sX0.inc(gpsimd.dma_start(out=x0c[:],
                                             in_=xb[r0:r0 + 128, :]), 16)
                em("gpsimd", _ld)
                snap(f"x0ld_{sig}", sX0)

                def _ws(scalar, k=f"x0ld_{sig}"):
                    wait(scalar, k)
                em("scalar", _ws)
                slot, li = emit_ln_stats(lambda: x0c[:])

                def _norm(scalar, slot=slot, li=li, sig=sig):
                    wait(scalar, f"lnstat_done_{li}")
                    if sig >= 1:
                        wait(scalar, f"ln1tr_{sig - 1}")
                    scalar.activation(out=ln1_tm[:], in_=x0c[:],
                                      func=AF.Identity,
                                      scale=srstd[:, slot:slot + 1],
                                      bias=snegmr[:, slot:slot + 1])
                em("scalar", _norm)
                snap(f"ln1tm_{sig}", sSC)
                ln_norm_key[li] = f"ln1tm_{sig}"

            elif piece in (1, 3):
                si = piece // 2
                sig = c * 2 + si
                emit_transposes(
                    (lambda dc: ln1_tm[:, dc * 128:(dc + 1) * 128]),
                    (lambda grp, si=si: ln1_fm[:, grp * 8:grp * 8 + 8,
                                               si * 128:(si + 1) * 128]),
                    f"ln1tm_{sig}",
                    f"udone_{c - 1}" if (si == 0 and c >= 1) else None)
                snap(f"ln1tr_{sig}", sPE)
                if si == 1:
                    snap(f"ln1fm_{c}", sVE)

            else:
                m0 = 2 * (piece - 4)
                for m in (m0, m0 + 1):
                    gdm = c * 8 + m

                    def _wdl(gpsimd, c=c, m=m, gdm=gdm):
                        if gdm >= 2:
                            wait(gpsimd, f"dtmm_{gdm - 2}")
                        sWdt[m % 2].inc(gpsimd.dma_start(
                            out=wdtb[m % 2][:].rearrange('p k c -> p (k c)'),
                            in_=wdt[m].rearrange('p k c -> p (k c)')), 16)
                    em("gpsimd", _wdl)
                    snap(f"wdtl_{gdm}", sWdt[m % 2])

                    hold = {}

                    def _mm(tensor, c=c, m=m, gdm=gdm, hold=hold):
                        wait(tensor, f"wdtl_{gdm}")
                        wait(tensor, f"ln1fm_{c}")
                        b = hold.setdefault("b", pool.take(tensor))
                        if pool.replay and "taken2" not in hold:
                            hold["taken2"] = True
                        last = None
                        for kk in range(16):
                            last = tensor.matmul(ps[b][:, 0:TC],
                                                 wdtb[m % 2][:, kk, :],
                                                 ln1_fm[:, kk, :],
                                                 start=(kk == 0), stop=(kk == 15))
                        sPE.inc(last)
                    em("tensor", _mm)
                    snap(f"dtmm_{gdm}", sPE)

                    def _sp(scalar, c=c, m=m, gdm=gdm, hold=hold):
                        wait(scalar, f"dtmm_{gdm}")
                        if c >= 1 and m == 0:
                            wait(scalar, f"udone_{c - 1}")
                        b = hold["b"]
                        e = scr[:, (m % 2) * 256:(m % 2) * 256 + 256]
                        scalar.activation(out=e, in_=ps[b][:, 0:TC], func=AF.Exp,
                                          bias=dtb_sb[:, m:m + 1])
                        scalar.activation(out=dt_sb[:, m, :], in_=e, func=AF.Ln,
                                          bias=ct[:, C_ONE:C_ONE + 1])
                    em("scalar", _sp)
                    snap(f"dtev_{gdm}", sSC)

                    def _give(hold=hold, key=f"dtev_{gdm}"):
                        pool.give(hold["b"], key)
                    emc(_give)

                if piece == 7:
                    holdx = {}

                    def _xp(tensor, c=c, hold=holdx):
                        if c == 0:
                            wait(tensor, "pre")
                        wait(tensor, f"ln1fm_{c}")
                        b = hold.setdefault("b", pool.take(tensor))
                        last = None
                        for kk in range(16):
                            last = tensor.matmul(ps[b][0:2 * NST, 0:TC],
                                                 wxp_sb[:, kk, :], ln1_fm[:, kk, :],
                                                 start=(kk == 0), stop=(kk == 15))
                        sPE.inc(last)
                    em("tensor", _xp)
                    snap(f"xpmm_{c}", sPE)

                    def _xpe(scalar, c=c, hold=holdx):
                        wait(scalar, f"xpmm_{c}")
                        if c >= 2:
                            wait(scalar, f"bcst_{c - 2}")
                        scalar.activation(out=bc_sb[c % 2][:],
                                          in_=ps[hold["b"]][0:2 * NST, 0:TC],
                                          func=AF.Identity, bias=xpb_sb[:])
                    em("scalar", _xpe)
                    snap(f"bcev_{c}", sSC)

                    def _givexp(hold=holdx, key=f"bcev_{c}"):
                        pool.give(hold["b"], key)
                    emc(_givexp)

                    def _u(vector, c=c):
                        vector.set_lag(1)
                        wait(vector, f"dtev_{c * 8 + 7}")
                        vector.tensor_tensor(
                            out=u_sb[:].rearrange("p m t -> p (m t)"),
                            in0=dt_sb[:].rearrange("p m t -> p (m t)"),
                            in1=ln1_fm[:, 0:8, :].rearrange("p m t -> p (m t)"),
                            op=ALU.mult)
                    em("vector", _u)
                    snap(f"udone_{c}", sVE)

                    def _bcst(gpsimd, c=c):
                        wait(gpsimd, f"bcev_{c}")
                        sBc.inc(gpsimd.dma_start(out=bc_d.ap()[c % 2],
                                                 in_=bc_sb[c % 2][:]), 16)
                    em("gpsimd", _bcst)
                    snap(f"bcst_{c}", sBc)

                    def _bcb(gpsimd, c=c):
                        wait(gpsimd, f"bcst_{c}")
                        if c >= 1:
                            wait(gpsimd, f"scanchunk_{c - 1}")
                        base_el = (c % 2) * 2 * NST * TC
                        repb = bass.AP(bc_d.ap().tensor, base_el,
                                       [[0, 128], [TC, NST], [1, TC]])
                        repc = bass.AP(bc_d.ap().tensor, base_el + NST * TC,
                                       [[0, 128], [TC, NST], [1, TC]])
                        sBcb.inc(gpsimd.dma_start(out=b_bc[:], in_=repb), 16)
                        sBcb.inc(gpsimd.dma_start(out=c_bc[:], in_=repc), 16)
                    em("gpsimd", _bcb)
                    snap(f"bcast_{c}", sBcb)

        def emit_A(c):
            for piece in range(8):
                emit_A_piece(c, piece)

        def emit_scan_m(c, m):
            gm = c * 8 + m

            def mk_da2(np_):
                n0 = 2 * np_
                i0 = c * 128 + m * 16 + n0

                def _da2(scalar, c=c, m=m, n0=n0, i0=i0):
                    if i0 == 0:
                        wait(scalar, "prep")
                    for j in range(2):
                        i = i0 + j
                        n = n0 + j
                        if i >= 6:
                            wait(scalar, f"scan_{i - 6}")
                        scalar.activation(
                            out=da_sb[i % 6][:], in_=dt_sb[:, m, :], func=AF.Exp,
                            scale=a_sb[:, m * 16 + n:m * 16 + n + 1])
                em("scalar", _da2)
                snap(f"da_{i0}", sSC)
                snap(f"da_{i0 + 1}", sSC)

            def mk_sc2(np_):
                n0 = 2 * np_
                i0 = c * 128 + m * 16 + n0

                def _sc2(vector, c=c, m=m, n0=n0, i0=i0):
                    vector.set_lag(1 if n0 == 0 else 2)
                    wait(vector, f"da_{i0 + 1}")
                    for j in range(2):
                        n = n0 + j
                        k = m * 16 + n
                        init = 0.0 if c == 0 else hstate[:, k:k + 1]
                        vector.tensor_tensor_scan(
                            out=h_all[:, n, :], data0=da_sb[(i0 + j) % 6][:],
                            data1=h_all[:, n, :], initial=init,
                            op0=ALU.mult, op1=ALU.add)
                em("vector", _sc2)
                snap(f"scan_{i0}", sVE)
                snap(f"scan_{i0 + 1}", sVE)

            for p in range(3):
                mk_da2(p)

            def _dbx(vector, c=c, m=m):
                vector.set_lag(1)
                if m == 0:
                    wait(vector, f"bcast_{c}")
                    wait(vector, f"udone_{c}")
                urep = bass.AP(u_sb[:].tensor, m * TC,
                               [[8 * TC, 128], [0, NST], [1, TC]])
                vector.tensor_tensor(out=h_all[:], in0=b_bc[:], in1=urep,
                                     op=ALU.mult)
            em("vector", _dbx)

            for p in range(8):
                mk_sc2(p)
                if p + 3 <= 7:
                    mk_da2(p + 3)

            def _tail(vector, c=c, m=m, gm=gm):
                vector.set_lag(1)
                vector.tensor_copy(
                    hstate[:, m * 16:(m + 1) * 16],
                    h_all[:, :, TC - 1:TC].rearrange("p n o -> p (n o)"))
                vector.tensor_tensor(out=h_all[:], in0=h_all[:], in1=c_bc[:],
                                     op=ALU.mult)
                if gm >= 1:
                    wait(vector, f"ycst_{gm - 1}")
                vector.tensor_tensor(out=h_all[:, 0:8, :], in0=h_all[:, 0:8, :],
                                     in1=h_all[:, 8:16, :], op=ALU.add)
                vector.tensor_tensor(out=h_all[:, 0:4, :], in0=h_all[:, 0:4, :],
                                     in1=h_all[:, 4:8, :], op=ALU.add)
                vector.tensor_tensor(out=h_all[:, 0:2, :], in0=h_all[:, 0:2, :],
                                     in1=h_all[:, 2:4, :], op=ALU.add)
                vector.tensor_tensor(out=y32[:], in0=h_all[:, 0, :],
                                     in1=h_all[:, 1, :], op=ALU.add)
            em("vector", _tail)
            snap(f"yred_{gm}", sVE)

            def _yc(scalar, c=c, m=m, gm=gm):
                wait(scalar, f"yred_{gm}")
                if c >= 2 and m == 0:
                    wait(scalar, f"opdone_{c - 2}")
                scalar.copy(y_bf[c % 2][:, m, :], y32[:])
            em("scalar", _yc)
            snap(f"ycst_{gm}", sSC)

        def emit_outproj(cc):
            for mo in range(8):
                def _wl(gpsimd, cc=cc, mo=mo):
                    gi4 = cc * 8 + mo
                    if gi4 >= 2:
                        wait(gpsimd, f"opmm_{cc}_{(mo - 2) * 2 + 1}" if mo >= 2
                             else f"opmm_{cc - 1}_{(mo + 6) * 2 + 1}")
                    sWo[mo % 2].inc(gpsimd.dma_start(
                        out=woutb[mo % 2][:].rearrange('p k c -> p (k c)'),
                        in_=wout[mo].rearrange('p k c -> p (k c)')), 16)
                em("gpsimd", _wl)
                snap(f"opwl_{cc}_{mo}", sWo[mo % 2])

                for tt in range(2):
                    gi = mo * 2 + tt

                    holdg = {}

                    def _g(tensor, cc=cc, tt=tt, mo=mo, gi=gi, hold=holdg):
                        if gi == 0:
                            wait(tensor, f"ycast_{cc}")
                            wait(tensor, "prep")
                        if tt == 0:
                            wait(tensor, f"opwl_{cc}_{mo}")
                        b = hold.setdefault("b", pool.take(tensor))
                        last = None
                        for kk in range(8):
                            last = tensor.matmul(
                                ps[b][:, 0:256],
                                y_bf[cc % 2][:, kk, tt * 128:(tt + 1) * 128],
                                woutb[mo % 2][:, kk, :],
                                start=(kk == 0), stop=(kk == 7))
                        sPE.inc(last)
                    em("tensor", _g)
                    snap(f"opmm_{cc}_{gi}", sPE)

                    def _e(vector, cc=cc, tt=tt, mo=mo, gi=gi, hold=holdg):
                        vector.set_lag(1)
                        wait(vector, f"opmm_{cc}_{gi}")
                        if gi == 0 and cc > 0:
                            wait(vector, f"mixdma_{cc - 1}")
                        vector.tensor_tensor(
                            out=mixp_sb[:, tt, mo * 256:(mo + 1) * 256],
                            in0=ps[hold["b"]][:, 0:256],
                            in1=outbh_sb[:, mo * 256:(mo + 1) * 256], op=ALU.add)
                    em("vector", _e)
                    snap(f"opev_{cc}_{gi}", sVE)

                    def _giveop(hold=holdg, key=f"opev_{cc}_{gi}"):
                        pool.give(hold["b"], key)
                    emc(_giveop)
                if mo < 7:
                    piece_break()
            snap(f"mixevac_{cc}", sVE)
            snap(f"opdone_{cc}", sPE)

            def _d(gpsimd, cc=cc):
                wait(gpsimd, f"mixevac_{cc}")
                dst = mixp_d.ap()[cc * TC:(cc + 1) * TC, :].rearrange(
                    "(tt p) m -> p tt m", p=128)
                sMixp.inc(gpsimd.dma_start(out=dst, in_=mixp_sb[:]), 16)
            em("gpsimd", _d)
            snap(f"mixdma_{cc}", sMixp)

            if cc % 2 == 1:
                gg = cc // 2

                def _rs(gpsimd, cc=cc, gg=gg):
                    wait(gpsimd, f"mixdma_{cc}")
                    sCC.inc(gpsimd.collective_compute(
                        "ReduceScatter", ALU.add,
                        replica_groups=[[0, 1], [2, 3], [4, 5], [6, 7]],
                        ins=[mixp_d.ap()[gg * 512:(gg + 1) * 512, :]],
                        outs=[mixg_d.ap()[gg]]))
                em("gpsimd", _rs)
                snap(f"rs_{gg}", sCC)

        def emit_C_light(g):
            for tt in range(2):
                ti = g * 2 + tt

                def _gldx(gpsimd, g=g, tt=tt, ti=ti):
                    if ti >= 1:
                        wait(gpsimd, f"x1_{(ti - 1) // 2}_{(ti - 1) % 2}")
                    sX0m.inc(gpsimd.dma_start(
                        out=x0m_t[:],
                        in_=x0m[g][tt * 128:(tt + 1) * 128, :]), 16)
                em("gpsimd", _gldx)
                snap(f"gldx_{ti}", sX0m)

                def _gld(gpsimd, g=g, tt=tt, ti=ti):
                    wait(gpsimd, f"rs_{g}")
                    if ti >= 1:
                        wait(gpsimd, f"mixused_{ti - 1}")
                    sMg.inc(gpsimd.dma_start(
                        out=mixg_t[:],
                        in_=mixg_d.ap()[g][tt * 128:(tt + 1) * 128, :]), 16)
                em("gpsimd", _gld)
                snap(f"gldm_{ti}", sMg)

                def _ws(scalar, k1=f"gldm_{ti}"):
                    wait(scalar, k1)
                em("scalar", _ws)
                slot, li = emit_ln_stats(lambda: mixg_t[:], fam=1)
                piece_break()

                for q in range(4):
                    def _nmQ(scalar, ti=ti, slot=slot, li=li, q=q):
                        if q == 0:
                            wait(scalar, f"lnstat_done_{li}")
                            if ti >= 1:
                                wait(scalar, f"x1q_{ti - 1}_3")
                        else:
                            wait(scalar, f"x1q_{ti}_{q - 1}")
                        scalar.activation(out=scr[:],
                                          in_=mixg_t[:, q * 512:(q + 1) * 512],
                                          func=AF.Identity,
                                          scale=srstd[:, slot:slot + 1],
                                          bias=snegmr[:, slot:slot + 1])
                    em("scalar", _nmQ)
                    snap(f"lnmQ_{ti}_{q}", sSC)
                    if q == 3:
                        ln_norm_key[li] = f"lnmQ_{ti}_{q}"

                    def _x1Q(vector, g=g, tt=tt, ti=ti, q=q):
                        vector.set_lag(1)
                        wait(vector, f"lnmQ_{ti}_{q}")
                        if q == 0:
                            wait(vector, f"gldx_{ti}")
                            if ti >= 1:
                                pg, pt = (ti - 1) // 2, (ti - 1) % 2
                                wait(vector, f"ln2tm_{pg}_{pt}")
                                wait(vector, f"x1st_{pg}_{pt}")
                        vector.tensor_tensor(
                            out=x1_sb[:, q * 512:(q + 1) * 512],
                            in0=scr[:], in1=x0m_t[:, q * 512:(q + 1) * 512],
                            op=ALU.add)
                    em("vector", _x1Q)
                    snap(f"x1q_{ti}_{q}", sVE)
                snap(f"x1_{g}_{tt}", sVE)

                def _x1st(gpsimd, g=g, tt=tt):
                    wait(gpsimd, f"x1_{g}_{tt}")
                    sX1s[tt].inc(gpsimd.dma_start(
                        out=out[g][tt * 128:(tt + 1) * 128, :],
                        in_=x1_sb[:]), 16)
                em("gpsimd", _x1st)
                snap(f"x1st_{g}_{tt}", sX1s[tt])

                def _ws2(scalar, k=f"x1_{g}_{tt}"):
                    wait(scalar, k)
                em("scalar", _ws2)
                slot2, li2 = emit_ln_stats(lambda: x1_sb[:], fam=1)

                def _nm2(scalar, g=g, tt=tt, slot=slot2, li=li2):
                    wait(scalar, f"lnstat_done_{li}")
                    if g >= 1:
                        wait(scalar, f"ln2tr_{g - 1}")
                    scalar.activation(out=ln2_tm[:, tt, :],
                                      in_=x1_sb[:],
                                      func=AF.Identity,
                                      scale=srstd[:, slot:slot + 1],
                                      bias=snegmr[:, slot:slot + 1])
                em("scalar", _nm2)
                snap(f"ln2tm_{g}_{tt}", sSC)
                ln_norm_key[li2] = f"ln2tm_{g}_{tt}"
                snap(f"mixused_{ti}", sSC)
                piece_break()

            for tt in range(2):
                emit_transposes(
                    (lambda dc, tt=tt: ln2_tm[:, tt, dc * 128:(dc + 1) * 128]),
                    (lambda grp, tt=tt: ln2_fm[:, grp * 8:grp * 8 + 8,
                                               tt * 128:(tt + 1) * 128]),
                    f"ln2tm_{g}_{tt}",
                    f"ffdone_{g - 1}" if (tt == 0 and g >= 1) else None,
                    fam=1)
            snap(f"ln2tr_{g}", sPE)
            snap(f"ln2fm_{g}", sVE)

            def _rsum(tensor, g=g):
                wait(tensor, f"ln2fm_{g}")
                if g >= 1:
                    wait(tensor, f"shnev_{g - 1}")
                last = None
                for kk in range(16):
                    last = tensor.matmul(psr[0:1, 0:TG], onesc_sb[:, 0:1],
                                         ln2_fm[:, kk, :],
                                         start=(kk == 0), stop=(kk == 15))
                sPE.inc(last)
            em("tensor", _rsum)
            snap(f"rsum_{g}", sPE)

            def _rsev(scalar, g=g):
                wait(scalar, f"rsum_{g}")
                scalar.activation(out=rsn_sb[:], in_=psr[0:1, 0:TG],
                                  func=AF.Identity,
                                  scale=ct[0:1, C_NEG128:C_NEG128 + 1])
            em("scalar", _rsev)
            snap(f"rsnev_{g}", sSC)

        def emit_C_heavy(g):
            def em_wld(ic):
                gic = g * 64 + ic

                def _wld(sync, g=g, ic=ic, gic=gic):
                    if gic == 0:
                        wait(sync, "precast")
                    if gic >= 2:
                        wait(sync, f"gumm_{gic - 2}")
                    sGU[ic % 2].inc(sync.dma_start(
                        out=wgb[ic % 2][:].rearrange('p k c -> p (k c)'),
                        in_=wg_bf.ap()[ic].rearrange('p k c -> p (k c)')), 16)
                    sGU[ic % 2].inc(sync.dma_start(
                        out=wub[ic % 2][:].rearrange('p k c -> p (k c)'),
                        in_=wu_bf.ap()[ic].rearrange('p k c -> p (k c)')), 16)
                em("sync", _wld)
                snap(f"wld_{gic}", sGU[ic % 2])

            def em_gu(ic, hold):
                gic = g * 64 + ic

                def _gu(tensor, g=g, ic=ic, gic=gic, hold=hold):
                    wait(tensor, f"wld_{gic}")
                    if ic == 0:
                        wait(tensor, f"rsnev_{g}")
                    b = hold.setdefault("b", pool.take(tensor))
                    if ic >= SHLAG:
                        wait(tensor, f"hh_{g}_{ic - SHLAG}")
                        tensor.matmul(psr[0:1, TG:2 * TG], onesc_sb[:, 0:1],
                                      hh_sb[:, ic - SHLAG, :],
                                      start=(ic - SHLAG == 0), stop=False)
                    for kk in range(16):
                        tensor.matmul(ps[b][:, 0:TG], wgb[ic % 2][:, kk, :],
                                      ln2_fm[:, kk, :], start=(kk == 0), stop=False)
                    tensor.matmul(ps[b][:, 0:TG], onesr_sb[0:1, 0:128],
                                  rsn_sb[0:1, :], start=False, stop=True)
                    for kk in range(16):
                        tensor.matmul(ps[b][:, TG:2 * TG], wub[ic % 2][:, kk, :],
                                      ln2_fm[:, kk, :], start=(kk == 0), stop=False)
                    sPE.inc(tensor.matmul(ps[b][:, TG:2 * TG], onesr_sb[0:1, 0:128],
                                          rsn_sb[0:1, :], start=False, stop=True))
                em("tensor", _gu)
                snap(f"gumm_{gic}", sPE)

            def em_guev(ic, hold):
                gic = g * 64 + ic

                def _guev(scalar, g=g, ic=ic, gic=gic, hold=hold):
                    wait(scalar, f"gumm_{gic}")
                    if ic >= 2:
                        wait(scalar, f"hh_{g}_{ic - 2}")
                    elif g >= 1:
                        wait(scalar, f"hh_{g - 1}_{62 + ic}")
                    b = hold["b"]
                    scalar.activation(out=g_b[ic % 2][:], in_=ps[b][:, 0:TG],
                                      func=AF.Silu, scale=gsc_sb[:, ic:ic + 1])
                em("scalar", _guev)
                snap(f"guev_{gic}", sSC)

            def em_hh(ic, hold):
                gic = g * 64 + ic

                def _hh(vector, g=g, ic=ic, gic=gic, hold=hold):
                    vector.set_lag(1)
                    wait(vector, f"guev_{gic}")
                    if ic == 0 and g >= 1:
                        wait(vector, f"downdone_{g - 1}")
                    b = hold["b"]
                    vector.scalar_tensor_tensor(
                        out=hh_sb[:, ic, :], in0=ps[b][:, TG:2 * TG],
                        scalar=usc_sb[:, ic:ic + 1], in1=g_b[ic % 2][:],
                        op0=ALU.mult, op1=ALU.mult)
                em("vector", _hh)
                snap(f"hh_{g}_{ic}", sVE)

                def _givegu(hold=hold, key=f"hh_{g}_{ic}"):
                    pool.give(hold["b"], key)
                emc(_givegu)

            for p0 in range(0, 64, 5):
                ics = list(range(p0, min(p0 + 5, 64)))
                for grp in (ics[0:2], ics[2:4], ics[4:5]):
                    if not grp:
                        continue
                    holds = {ic: {} for ic in grp}
                    for ic in grp:
                        em_wld(ic)
                    for ic in grp:
                        em_gu(ic, holds[ic])
                    for ic in grp:
                        em_guev(ic, holds[ic])
                    for ic in grp:
                        em_hh(ic, holds[ic])
                piece_break()

            def _shf(tensor, g=g):
                last = None
                for ic in range(64 - SHLAG, 64):
                    wait(tensor, f"hh_{g}_{ic}")
                    last = tensor.matmul(psr[0:1, TG:2 * TG], onesc_sb[:, 0:1],
                                         hh_sb[:, ic, :], start=False,
                                         stop=(ic == 63))
                sPE.inc(last)
            em("tensor", _shf)
            snap(f"shmm_{g}", sPE)
            snap(f"ffdone_{g}", sPE)

            def _shev(scalar, g=g):
                wait(scalar, f"shmm_{g}")
                scalar.activation(out=shn_sb[:], in_=psr[0:1, TG:2 * TG],
                                  func=AF.Identity,
                                  scale=ct[0:1, C_NEG128:C_NEG128 + 1])
            em("scalar", _shev)
            snap(f"shnev_{g}", sSC)

            bdown = []

            def _takedown(tensor, g=g, bd=bdown):
                if not pool.replay or len(bd) == 0:
                    bd.append(pool.take(tensor))
                    bd.append(pool.take(tensor))
                else:
                    pool.take(tensor)
                    pool.take(tensor)
            em("tensor", _takedown)
            for ho in range(4):
                gho = g * 4 + ho
                for icg in range(16):
                    gw = gho * 16 + icg

                    def _wdl(sync, ho=ho, icg=icg, gw=gw):
                        if gw >= 2:
                            wait(sync, f"wdmm_{gw - 2}")
                        if gw == 0:
                            wait(sync, "precast")
                        src = wd_bf.ap()[ho, icg * 4:(icg + 1) * 4].rearrange(
                            "u p c -> p u c")
                        sWd[icg % 2].inc(sync.dma_start(out=wdb[icg % 2][:],
                                                        in_=src), 16)
                    em("sync", _wdl)
                    snap(f"wdl_{gw}", sWd[icg % 2])

                    def _dmm(tensor, g=g, ho=ho, icg=icg, gw=gw, gho=gho, bd=bdown):
                        wait(tensor, f"wdl_{gw}")
                        if icg == 0:
                            wait(tensor, f"shnev_{g}")
                            wait(tensor, f"hh_{g}_{63}")
                            if ho >= 1:
                                wait(tensor, f"dnev_{gho - 1}")
                        last = None
                        for u in range(4):
                            ic = icg * 4 + u
                            for tt in range(2):
                                last = tensor.matmul(
                                    ps[bd[tt]][:],
                                    hh_sb[:, ic, tt * 128:(tt + 1) * 128],
                                    wdb[icg % 2][:, u, :],
                                    start=(ic == 0), stop=False)
                        if icg == 15:
                            for tt in range(2):
                                last = tensor.matmul(
                                    ps[bd[tt]][:],
                                    shn_sb[0:1, tt * 128:(tt + 1) * 128],
                                    onesr_sb[0:1, 0:512], start=False, stop=True)
                        sPE.inc(last)
                    em("tensor", _dmm)
                    snap(f"wdmm_{gw}", sPE)

                def _dnev(vector, g=g, ho=ho, gho=gho, bd=bdown):
                    vector.set_lag(1)
                    wait(vector, f"wdmm_{gho * 16 + 15}")
                    if g == 0 and ho == 0:
                        wait(vector, "pre")
                    if ho == 0 and g >= 1:
                        wait(vector, f"outdma_{g - 1}")
                    for tt in range(2):
                        vector.tensor_tensor(
                            out=ffn32[:, tt, ho * 512:(ho + 1) * 512],
                            in0=ps[bd[tt]][:],
                            in1=dsc_sb[:, ho * 512:(ho + 1) * 512], op=ALU.mult)
                em("vector", _dnev)
                snap(f"dnev_{gho}", sVE)
                if ho in (0, 2):
                    piece_break()

            def _givedown(bd=bdown, key=f"dnev_{g * 4 + 3}"):
                pool.give(bd[0], key)
                pool.give(bd[1], key)
            emc(_givedown)
            snap(f"downdone_{g}", sVE)
            snap(f"ffnout_{g}", sVE)

            for tt in range(2):
                def _ws3(scalar, k=f"ffnout_{g}"):
                    wait(scalar, k)
                em("scalar", _ws3)
                slot3, li3 = emit_ln_stats(lambda g=g, tt=tt: ffn32[:, tt, :],
                                           fam=2)

                def _nm3(scalar, g=g, tt=tt, slot=slot3, li=li3):
                    wait(scalar, f"lnstat_done_{li}")
                    scalar.activation(out=ffn32[:, tt, :], in_=ffn32[:, tt, :],
                                      func=AF.Identity,
                                      scale=srstd[:, slot:slot + 1],
                                      bias=snegmr[:, slot:slot + 1])
                em("scalar", _nm3)
                snap(f"ln3_{g}_{tt}", sSC)
                ln_norm_key[li3] = f"ln3_{g}_{tt}"

            def _ost(gpsimd, g=g):
                wait(gpsimd, f"ln3_{g}_0")
                wait(gpsimd, f"ln3_{g}_1")
                wait(gpsimd, f"x1st_{g}_0")
                wait(gpsimd, f"x1st_{g}_1")
                dst = out[g].rearrange("(tt p) m -> p tt m", p=128)
                sST.inc(gpsimd.dma_start(out=dst, in_=ffn32[:],
                                         accum_op=ALU.add), 16)
            em("gpsimd", _ost)
            snap(f"outdma_{g}", sST)

        emit_A(0)
        lq = []
        hq = []
        ldone = [-1]

        def pop_lanes():
            if lq:
                g, p = lq.pop(0)
                p()
                if not lq or lq[0][0] != g:
                    ldone[0] = g
            if hq and hq[0][0] <= ldone[0]:
                hq.pop(0)[1]()

        def enqueue_C(g):
            lq.extend((g, p) for p in record_pieces(
                lambda g=g: emit_C_light(g)))
            hq.extend((g, p) for p in record_pieces(
                lambda g=g: emit_C_heavy(g)))

        for c in range(NT):
            opp = (record_pieces(lambda cc=c - 1: emit_outproj(cc))
                   if c >= 1 else None)
            if c >= 3 and (c - 3) % 2 == 0:
                enqueue_C((c - 3) // 2)
            for m in range(8):
                emit_scan_m(c, m)
                if m == 7:
                    snap(f"scanchunk_{c}", sVE)
                    snap(f"ycast_{c}", sSC)
                emit_A_piece(c + 1, m)
                if opp is not None:
                    opp[m]()
                pop_lanes()
        opp = record_pieces(lambda: emit_outproj(NT - 1))
        for m in range(8):
            opp[m]()
            pop_lanes()
        enqueue_C(7)
        while lq or hq:
            pop_lanes()

        def _final(sync):
            sync.wait_ge(sST.h, sST.n)
        em("sync", _final)

        mock = _MockEng()
        mock_sc = ChainEng(mock, sSC)
        mock_ve = ChainEng(mock, sVE)
        for eng_name, payload in order:
            if eng_name == "__snap__":
                key, sem = payload
                ev[key] = (sem, sem.n)
            elif eng_name == "__call__":
                payload()
            elif eng_name == "scalar":
                payload(mock_sc)
            elif eng_name == "vector":
                payload(mock_ve)
            else:
                payload(mock)
        for s_ in ALL_SEMS:
            s_.n = 0
        pool.replay = True
        pool.ri = 0

        blk.sync(lambda eng: [f(eng) for f in sched["sync"]])
        blk.gpsimd(lambda eng: [f(eng) for f in sched["gpsimd"]])
        blk.scalar(lambda eng: [f(ChainEng(eng, sSC)) for f in sched["scalar"]])
        blk.vector(lambda eng: [f(ChainEng(eng, sVE)) for f in sched["vector"]])
        blk.tensor(lambda eng: [f(eng) for f in sched["tensor"]])

    return nc


def _prep_inputs(k, x, A_log, x_proj_w, dt_proj_w, dt_proj_b, out_proj_w, shared):
    b, j = k >> 1, k & 1
    d0, d1 = j * DH, (j + 1) * DH
    perm = np.concatenate([np.arange(d0, d1), np.arange((1 - j) * DH, (2 - j) * DH)])

    xbv = np.ascontiguousarray(x[b][:, perm], dtype=np.float32)
    x0mv = np.ascontiguousarray(x[b].reshape(NG, 2, TG, D)[:, j], dtype=np.float32)

    wdtT = dt_proj_w.T
    wdt_k = np.ascontiguousarray(
        wdtT[perm][:, d0:d1].reshape(16, 128, 8, 128).transpose(2, 1, 0, 3),
        dtype=np.float32)
    wxp_k = np.ascontiguousarray(
        x_proj_w.T[perm].reshape(16, 128, 2 * NST).transpose(1, 0, 2),
        dtype=np.float32)
    wout_k = np.ascontiguousarray(
        out_proj_w.T[d0:d1].reshape(8, 128, 8, 256).transpose(2, 1, 0, 3),
        dtype=np.float32)
    alog_k = np.ascontiguousarray(
        A_log[d0:d1].reshape(8, 128, NST).transpose(1, 0, 2).reshape(128, 128),
        dtype=np.float32)
    dtb_k = np.ascontiguousarray(dt_proj_b[d0:d1].reshape(8, 128).T,
                                 dtype=np.float32)

    return dict(xb=xbv, x0m=x0mv, wdt=wdt_k, wxp=wxp_k, wout=wout_k, alog=alog_k,
                dtb=dtb_k, **shared)


def kernel(L0_attn_x, L0_attn_A_log, L0_attn_x_proj_w, L0_attn_x_proj_b,
           L0_attn_dt_proj_w, L0_attn_dt_proj_b, L0_attn_out_proj_w,
           L0_attn_out_proj_b, L0_ffn_gate_w, L0_ffn_gate_scale, L0_ffn_up_w,
           L0_ffn_up_scale, L0_ffn_down_w, L0_ffn_down_scale):
    x = np.asarray(L0_attn_x, dtype=np.float32)
    A_log = np.asarray(L0_attn_A_log, dtype=np.float32)
    x_proj_w = np.asarray(L0_attn_x_proj_w, dtype=np.float32)
    x_proj_b = np.asarray(L0_attn_x_proj_b, dtype=np.float32)
    dt_proj_w = np.asarray(L0_attn_dt_proj_w, dtype=np.float32)
    dt_proj_b = np.asarray(L0_attn_dt_proj_b, dtype=np.float32)
    out_proj_w = np.asarray(L0_attn_out_proj_w, dtype=np.float32)
    out_proj_b = np.asarray(L0_attn_out_proj_b, dtype=np.float32)
    gate_w = np.asarray(L0_ffn_gate_w)
    up_w = np.asarray(L0_ffn_up_w)
    down_w = np.asarray(L0_ffn_down_w)
    gate_scale = np.asarray(L0_ffn_gate_scale, dtype=np.float32)
    up_scale = np.asarray(L0_ffn_up_scale, dtype=np.float32)
    down_scale = np.asarray(L0_ffn_down_scale, dtype=np.float32)

    wg8v = np.ascontiguousarray(
        gate_w.astype(np.uint8).T.reshape(16, 128, 64, 128).transpose(2, 1, 0, 3))
    wu8v = np.ascontiguousarray(
        up_w.astype(np.uint8).T.reshape(16, 128, 64, 128).transpose(2, 1, 0, 3))
    wd8v = np.ascontiguousarray(
        down_w.astype(np.uint8).T.reshape(64, 128, 4, 512).transpose(2, 0, 1, 3))
    shared = dict(
        xpb=np.ascontiguousarray(x_proj_b.reshape(2 * NST, 1)),
        outb_bc=np.ascontiguousarray(np.broadcast_to(out_proj_b, (128, D))),
        wg8=wg8v, wu8=wu8v, wd8=wd8v,
        gsc=np.ascontiguousarray(gate_scale.reshape(64, 128).T),
        usc=np.ascontiguousarray(up_scale.reshape(64, 128).T),
        dsc_bc=np.ascontiguousarray(np.broadcast_to(down_scale, (128, D))),
        ident=np.eye(128, dtype=BF),
        onesc=np.ones((128, 1), dtype=BF),
        onesr=np.ones((1, 512), dtype=BF),
        consts=np.ascontiguousarray(np.broadcast_to(np.array(
            [1e-5, 1.0 / D, -1.0, 1.0, -0.5, -128.0, 0.5, 0.0], np.float32),
            (128, NCONST))),
    )

    nc = build_nc()
    in_maps = [_prep_inputs(k, x, A_log, x_proj_w, dt_proj_w, dt_proj_b,
                            out_proj_w, shared) for k in range(8)]
    res = run_bass_kernel_spmd(nc, in_maps, core_ids=list(range(8)))

    outv = np.empty((B, L, D), dtype=np.float32)
    for k in range(8):
        b, j = k >> 1, k & 1
        outv[b].reshape(NG, 2, TG, D)[:, j] = res.results[k]["out"]
    return outv


if __name__ == "__main__":
    nc = build_nc()
    print("built; instructions:", nc.n_instructions(),
          "sbuf remaining:", nc.sbuf_bytes_remaining)


# version 11 (final)
# speedup vs baseline: 1.3808x; 1.0050x over previous
from contextlib import ExitStack

import numpy as np
import ml_dtypes

import concourse.bass as bass
import concourse.mybir as mybir
from concourse.bass_utils import run_bass_kernel_spmd

F32 = mybir.dt.float32
BF16 = mybir.dt.bfloat16
U8 = mybir.dt.uint8
AF = mybir.ActivationFunctionType
ALU = mybir.AluOpType
AX = mybir.AxisListType
BF = ml_dtypes.bfloat16

B, L, D = 4, 4096, 2048
DH = 1024
NST = 16
TC = 256
NT = L // TC
TG = 256
NG = 8
SHLAG = 20

C_EPS, C_INVD, C_NEG1, C_ONE, C_NEGH, C_NEG128, C_HALF = range(7)
NCONST = 8


class Sem:
    def __init__(self, es, nc, name):
        self.h = es.enter_context(nc.semaphore(name))
        self.n = 0

    def inc(self, instr, k=1):
        instr.then_inc(self.h, k)
        self.n += k
        return self.n


class _MockInstr:
    def then_inc(self, h, k):
        return self


class _MockEng:
    def __getattr__(self, name):
        return lambda *a, **kw: _MockInstr()


_UNCHAINED = {"wait_ge", "sem_inc", "dma_start", "collective_compute",
              "then_inc", "nop", "engine_nop"}


class ChainEng:

    def __init__(self, eng, sem):
        object.__setattr__(self, "_e", eng)
        object.__setattr__(self, "_s", sem)
        object.__setattr__(self, "_lag", 1)

    def set_lag(self, k):
        object.__setattr__(self, "_lag", k)

    def __getattr__(self, name):
        f = getattr(self._e, name)
        if name in _UNCHAINED or name.startswith("_"):
            return f

        def g(*a, **kw):
            w = self._s.n - (self._lag - 1)
            if w > 0:
                self._e.wait_ge(self._s.h, w)
            r = f(*a, **kw)
            self._s.inc(r, 1)
            return r
        return g


def build_nc():
    nc = bass.Bass("TRN2", target_bir_lowering=False)

    xb = nc.dram_tensor("xb", [L, D], F32, kind="ExternalInput").ap()
    x0m = nc.dram_tensor("x0m", [NG, TG, D], F32, kind="ExternalInput").ap()
    wdt = nc.dram_tensor("wdt", [8, 128, 16, 128], F32, kind="ExternalInput").ap()
    wxp = nc.dram_tensor("wxp", [128, 16, 2 * NST], F32, kind="ExternalInput").ap()
    wout = nc.dram_tensor("wout", [8, 128, 8, 256], F32, kind="ExternalInput").ap()
    alog = nc.dram_tensor("alog", [128, 128], F32, kind="ExternalInput").ap()
    dtb = nc.dram_tensor("dtb", [128, 8], F32, kind="ExternalInput").ap()
    xpb = nc.dram_tensor("xpb", [2 * NST, 1], F32, kind="ExternalInput").ap()
    outb_bc = nc.dram_tensor("outb_bc", [128, D], F32, kind="ExternalInput").ap()
    wg8 = nc.dram_tensor("wg8", [64, 128, 16, 128], U8, kind="ExternalInput").ap()
    wu8 = nc.dram_tensor("wu8", [64, 128, 16, 128], U8, kind="ExternalInput").ap()
    wd8 = nc.dram_tensor("wd8", [4, 64, 128, 512], U8, kind="ExternalInput").ap()
    gsc = nc.dram_tensor("gsc", [128, 64], F32, kind="ExternalInput").ap()
    usc = nc.dram_tensor("usc", [128, 64], F32, kind="ExternalInput").ap()
    dsc_bc = nc.dram_tensor("dsc_bc", [128, D], F32, kind="ExternalInput").ap()
    ident = nc.dram_tensor("ident", [128, 128], BF16, kind="ExternalInput").ap()
    onesc = nc.dram_tensor("onesc", [128, 1], BF16, kind="ExternalInput").ap()
    onesr = nc.dram_tensor("onesr", [1, 512], BF16, kind="ExternalInput").ap()
    consts = nc.dram_tensor("consts", [128, NCONST], F32, kind="ExternalInput").ap()

    out = nc.dram_tensor("out", [NG, TG, D], F32, kind="ExternalOutput").ap()

    wg_bf = nc.dram_tensor("wg_bf", [64, 128, 16, 128], BF16)
    wu_bf = nc.dram_tensor("wu_bf", [64, 128, 16, 128], BF16)
    wd_bf = nc.dram_tensor("wd_bf", [4, 64, 128, 512], BF16)
    bc_d = nc.dram_tensor("bc_d", [2, 2 * NST, TC], BF16)
    mixp_d = nc.dram_tensor("mixp_d", [L, D], BF16)
    mixg_d = nc.dram_tensor("mixg_d", [NG, TG, D], BF16)

    es = ExitStack()
    sb = lambda name, shape, dt: es.enter_context(nc.sbuf_tensor(name, shape, dt))

    wxp_sb = sb("wxp_sb", [128, 16, 2 * NST], BF16)
    a_sb = sb("a_sb", [128, 128], F32)
    dtb_sb = sb("dtb_sb", [128, 8], F32)
    xpb_sb = sb("xpb_sb", [2 * NST, 1], F32)
    gsc_sb = sb("gsc_sb", [128, 64], F32)
    usc_sb = sb("usc_sb", [128, 64], F32)
    ident_sb = sb("ident_sb", [128, 128], BF16)
    onesc_sb = sb("onesc_sb", [128, 1], BF16)
    onesr_sb = sb("onesr_sb", [1, 512], BF16)
    ct = sb("ct", [128, NCONST], F32)
    sx = sb("sx", [128, 6, 4], F32)
    sx2 = sb("sx2", [128, 6, 4], F32)
    smean = sb("smean", [128, 6], F32)
    snvar = sb("snvar", [128, 6], F32)
    sm2 = sb("sm2", [128, 6], F32)
    srstd = sb("srstd", [128, 6], F32)
    snegmr = sb("snegmr", [128, 6], F32)
    scr = sb("scr", [128, 512], F32)
    outbh_sb = sb("outbh_sb", [128, D], BF16)
    dsc_sb = sb("dsc_sb", [128, D], BF16)
    rsn_sb = sb("rsn_sb", [1, TG], BF16)
    shn_sb = sb("shn_sb", [1, TG], BF16)

    x0c = sb("x0c", [128, D], BF16)
    ln1_tm = sb("ln1tm", [128, D], BF16)
    ln1_fm = sb("ln1fm", [128, 16, TC], BF16)
    wdtb = [sb(f"wdtb{i}", [128, 16, 128], BF16) for i in range(2)]
    dt_sb = sb("dt_sb", [128, 8, TC], BF16)
    u_sb = sb("u_sb", [128, 8, TC], BF16)
    bc_sb = [sb(f"bcs{i}", [2 * NST, TC], BF16) for i in range(2)]
    b_bc = sb("b_bc", [128, NST, TC], BF16)
    c_bc = sb("c_bc", [128, NST, TC], BF16)
    da_sb = [sb(f"das{i}", [128, TC], F32) for i in range(6)]
    h_all = sb("h_all", [128, NST, TC], BF16)
    hstate = sb("hstate", [128, 128], F32)
    y32 = sb("y32", [128, TC], F32)
    y_bf = [sb(f"ybfs{i}", [128, 8, TC], BF16) for i in range(2)]
    woutb = [sb(f"woutbs{i}", [128, 8, 256], BF16) for i in range(2)]
    mixp_sb = sb("mixp_sb", [128, 2, D], BF16)
    mixg_t = sb("mixg_t", [128, D], BF16)
    x0m_t = sb("x0m_t", [128, D], BF16)
    x1_sb = sb("x1_sb", [128, D], BF16)
    ln2_tm = sb("ln2_tm", [128, 2, D], BF16)
    ln2_fm = sb("ln2_fm", [128, 16, TG], BF16)
    wgb = [sb(f"wgbs{i}", [128, 16, 128], BF16) for i in range(2)]
    wub = [sb(f"wubs{i}", [128, 16, 128], BF16) for i in range(2)]
    wdb = [sb(f"wdbs{i}", [128, 4, 512], BF16) for i in range(2)]
    g_b = [sb(f"g_bs{i}", [128, TG], BF16) for i in range(2)]
    hh_sb = sb("hh_sb", [128, 64, TG], BF16)
    ffn32 = sb("ffn32", [128, 2, D], F32)

    ps = [es.enter_context(nc.psum_tensor(f"ps{i}", [128, 512], F32)) for i in range(5)]
    pt0 = es.enter_context(nc.psum_tensor("pt0", [128, 1024], BF16))
    pt1 = es.enter_context(nc.psum_tensor("pt1", [128, 1024], BF16))
    psr = es.enter_context(nc.psum_tensor("psr", [128, 512], F32))

    sConst = Sem(es, nc, "sConst")
    sPre = Sem(es, nc, "sPre")
    sPc = Sem(es, nc, "sPc")
    sX0 = Sem(es, nc, "sX0")
    sWdt = [Sem(es, nc, f"sWdt{i}") for i in range(2)]
    sWo = [Sem(es, nc, f"sWo{i}") for i in range(2)]
    sMixp = Sem(es, nc, "sMixp")
    sBc = Sem(es, nc, "sBc")
    sBcb = Sem(es, nc, "sBcb")
    sMg = Sem(es, nc, "sMg")
    sX0m = Sem(es, nc, "sX0m")
    sGU = [Sem(es, nc, f"sGU{i}") for i in range(2)]
    sWd = [Sem(es, nc, f"sWd{i}") for i in range(2)]
    sX1s = [Sem(es, nc, f"sX1s{i}") for i in range(2)]
    sST = Sem(es, nc, "sST")
    sCC = Sem(es, nc, "sCC")
    sPE = Sem(es, nc, "sPE")
    sVE = Sem(es, nc, "sVE")
    sSC = Sem(es, nc, "sSC")
    ALL_SEMS = ([sConst, sPre, sPc, sMixp, sBc, sBcb, sX0m, sST, sCC,
                 sPE, sVE, sSC] + [sX0] + sWdt + sWo + [sMg] + sGU + sWd + sX1s)

    ev = {}
    sched = {"sync": [], "gpsimd": [], "scalar": [], "vector": [], "tensor": []}
    order = []
    rec = [None]

    def _emit(kind, payload):
        if rec[0] is not None:
            rec[0][-1].append((kind, payload))
        elif kind in ("__snap__", "__call__"):
            order.append((kind, payload))
        else:
            sched[kind].append(payload)
            order.append((kind, payload))

    def em(engine, fn):
        _emit(engine, fn)

    def snap(key, sem):
        _emit("__snap__", (key, sem))

    def emc(fn):
        _emit("__call__", fn)

    def piece_break():
        rec[0].append([])

    def record_pieces(emit_fn):
        buf = [[]]
        rec[0] = buf
        emit_fn()
        rec[0] = None
        pieces = []
        for plist in buf:
            if not plist:
                continue

            def piece(plist=plist):
                for kind, payload in plist:
                    _emit(kind, payload)
            pieces.append(piece)
        return pieces

    def wait(eng, key):
        sem, n = ev[key]
        if n > 0:
            eng.wait_ge(sem.h, n)

    class Pool:

        def __init__(self):
            self.q = [(b, None) for b in range(5)]
            self.log = []
            self.replay = False
            self.ri = 0

        def take(self, eng):
            if self.replay:
                b, key = self.log[self.ri]
                self.ri += 1
            else:
                b, key = self.q.pop(0)
                self.log.append((b, key))
            if key is not None:
                wait(eng, key)
            return b

        def give(self, b, key):
            if not self.replay:
                self.q.append((b, key))

    pool = Pool()

    with es:
        blk = es.enter_context(nc.Block())

        def _loads(sync):
            for ap_in, tile in ((alog, a_sb), (dtb, dtb_sb), (xpb, xpb_sb),
                                (gsc, gsc_sb), (usc, usc_sb), (ident, ident_sb),
                                (onesc, onesc_sb), (onesr, onesr_sb), (consts, ct)):
                sConst.inc(sync.dma_start(out=tile[:], in_=ap_in[:]), 16)
        em("sync", _loads)
        snap("loads0", sConst)

        def _pre(gpsimd):
            sPre.inc(gpsimd.dma_start(out=wxp_sb[:].rearrange('p k c -> p (k c)'),
                                      in_=wxp[:].rearrange('p k c -> p (k c)')), 16)
            sPre.inc(gpsimd.dma_start(out=outbh_sb[:], in_=outb_bc[:]), 16)
            sPre.inc(gpsimd.dma_start(out=dsc_sb[:], in_=dsc_bc[:]), 16)
        em("gpsimd", _pre)
        snap("pre", sPre)

        def _precast(gpsimd):
            def flat(ap, cols=32768):
                return ap.rearrange(
                    " ".join(f"d{i}" for i in range(len(ap.shape)))
                    + " -> (" + " ".join(f"d{i}" for i in range(len(ap.shape)))
                    + ")").rearrange("(x y) -> x y", y=cols)
            for srcw, dst in ((wg8, wg_bf), (wu8, wu_bf)):
                for q in range(4):
                    sPc.inc(gpsimd.dma_start(
                        out=flat(dst.ap()[q * 16:(q + 1) * 16]),
                        in_=flat(srcw[q * 16:(q + 1) * 16])), 16)
            for ho in range(4):
                sPc.inc(gpsimd.dma_start(out=flat(wd_bf.ap()[ho]),
                                         in_=flat(wd8[ho])), 16)
        em("gpsimd", _precast)
        snap("precast", sPc)

        def _prep_sc(scalar):
            wait(scalar, "loads0")
            scalar.activation(out=a_sb[:], in_=a_sb[:], func=AF.Exp)
        em("scalar", _prep_sc)
        snap("aexp", sSC)

        def _prep_ve(vector):
            vector.set_lag(1)
            wait(vector, "aexp")
            wait(vector, "pre")
            vector.tensor_scalar(out=a_sb[:], in0=a_sb[:],
                                 scalar1=ct[:, C_NEG1:C_NEG1 + 1],
                                 scalar2=None, op0=ALU.mult)
            vector.tensor_scalar(out=outbh_sb[:], in0=outbh_sb[:],
                                 scalar1=ct[:, C_HALF:C_HALF + 1],
                                 scalar2=None, op0=ALU.mult)
        em("vector", _prep_ve)
        snap("prep", sVE)

        ln_ctr = {0: 0, 1: 0, 2: 0}
        ln_norm_key = {}

        def emit_ln_stats(src_ap_fn, fam=0):
            ctrv = ln_ctr[fam]
            ln_ctr[fam] += 1
            s = 2 * fam + ctrv % 2
            i = fam * 1000 + ctrv

            def _s1(scalar, i=i, s=s):
                src = src_ap_fn()
                for q in range(4):
                    scalar.activation(out=scr[:], in_=src[:, q * 512:(q + 1) * 512],
                                      func=AF.Identity,
                                      accum_out=sx[:, s, q:q + 1])
                for q in range(4):
                    scalar.activation(out=scr[:], in_=src[:, q * 512:(q + 1) * 512],
                                      func=AF.Square, accum_out=sx2[:, s, q:q + 1])
                scalar.activation(out=scr[:, 0:4], in_=sx[:, s, :],
                                  func=AF.Identity, scale=ct[:, C_INVD:C_INVD + 1],
                                  accum_out=smean[:, s:s + 1])
                scalar.activation(out=scr[:, 0:4], in_=sx2[:, s, :],
                                  func=AF.Identity, scale=ct[:, C_INVD:C_INVD + 1],
                                  accum_out=snvar[:, s:s + 1])
                scalar.activation(out=sm2[:, s:s + 1], in_=smean[:, s:s + 1],
                                  func=AF.Square)
                scalar.activation(out=sm2[:, s:s + 1], in_=sm2[:, s:s + 1],
                                  func=AF.Identity, scale=ct[:, C_NEG1:C_NEG1 + 1])
                scalar.activation(out=snvar[:, s:s + 1], in_=snvar[:, s:s + 1],
                                  func=AF.Identity, bias=sm2[:, s:s + 1])
                scalar.activation(out=snvar[:, s:s + 1], in_=snvar[:, s:s + 1],
                                  func=AF.Ln, bias=ct[:, C_EPS:C_EPS + 1])
                scalar.activation(out=srstd[:, s:s + 1], in_=snvar[:, s:s + 1],
                                  func=AF.Exp, scale=ct[:, C_NEGH:C_NEGH + 1])
                scalar.activation(out=snegmr[:, s:s + 1], in_=smean[:, s:s + 1],
                                  func=AF.Identity, scale=srstd[:, s:s + 1])
                scalar.activation(out=snegmr[:, s:s + 1], in_=snegmr[:, s:s + 1],
                                  func=AF.Identity, scale=ct[:, C_NEG1:C_NEG1 + 1])
            em("scalar", _s1)
            snap(f"lnstat_done_{i}", sSC)
            return s, i

        tg_ctr = {0: 0, 1: 0}

        def emit_transposes(src_fn, dst_fn, ready_key, extra_key=None, fam=0):
            pt = pt0 if fam == 0 else pt1
            for grp in range(2):
                ctrv = tg_ctr[fam]
                tg_ctr[fam] += 1
                tg = fam * 1000 + ctrv

                def _tr(tensor, grp=grp, tg=tg, ctrv=ctrv, pt=pt):
                    if ctrv == 0:
                        wait(tensor, "loads0")
                    wait(tensor, ready_key)
                    if extra_key is not None and grp == 0:
                        wait(tensor, extra_key)
                    if ctrv >= 1:
                        wait(tensor, f"trev_{tg - 1}")
                    last = None
                    for dcq in range(8):
                        last = tensor.transpose(pt[:, dcq * 128:(dcq + 1) * 128],
                                                src_fn(grp * 8 + dcq), ident_sb[:])
                    sPE.inc(last)
                em("tensor", _tr)
                snap(f"trmm_{tg}", sPE)

                def _trev(vector, grp=grp, tg=tg, pt=pt):
                    vector.set_lag(1)
                    wait(vector, f"trmm_{tg}")
                    vector.tensor_copy(dst_fn(grp),
                                       pt[:].rearrange("p (dc t) -> p dc t", dc=8))
                em("vector", _trev)
                snap(f"trev_{tg}", sVE)

        def emit_A_piece(c, piece):
            if c >= NT:
                return
            if piece in (0, 2):
                si = piece // 2
                sig = c * 2 + si

                def _ld(gpsimd, sig=sig, c=c, si=si):
                    if sig >= 1:
                        wait(gpsimd, f"ln1tm_{sig - 1}")
                    r0 = c * TC + si * 128
                    sX0.inc(gpsimd.dma_start(out=x0c[:],
                                             in_=xb[r0:r0 + 128, :]), 16)
                em("gpsimd", _ld)
                snap(f"x0ld_{sig}", sX0)

                def _ws(scalar, k=f"x0ld_{sig}"):
                    wait(scalar, k)
                em("scalar", _ws)
                slot, li = emit_ln_stats(lambda: x0c[:])

                def _norm(scalar, slot=slot, li=li, sig=sig):
                    wait(scalar, f"lnstat_done_{li}")
                    if sig >= 1:
                        wait(scalar, f"ln1tr_{sig - 1}")
                    scalar.activation(out=ln1_tm[:], in_=x0c[:],
                                      func=AF.Identity,
                                      scale=srstd[:, slot:slot + 1],
                                      bias=snegmr[:, slot:slot + 1])
                em("scalar", _norm)
                snap(f"ln1tm_{sig}", sSC)
                ln_norm_key[li] = f"ln1tm_{sig}"

            elif piece in (1, 3):
                si = piece // 2
                sig = c * 2 + si
                emit_transposes(
                    (lambda dc: ln1_tm[:, dc * 128:(dc + 1) * 128]),
                    (lambda grp, si=si: ln1_fm[:, grp * 8:grp * 8 + 8,
                                               si * 128:(si + 1) * 128]),
                    f"ln1tm_{sig}",
                    f"udone_b_{c - 1}" if (si == 0 and c >= 1) else None)
                snap(f"ln1tr_{sig}", sPE)
                if si == 1:
                    snap(f"ln1fm_{c}", sVE)

            else:
                if piece == 4:
                    holdx = {}

                    def _xp(tensor, c=c, hold=holdx):
                        if c == 0:
                            wait(tensor, "pre")
                        wait(tensor, f"ln1fm_{c}")
                        b = hold.setdefault("b", pool.take(tensor))
                        last = None
                        for kk in range(16):
                            last = tensor.matmul(ps[b][0:2 * NST, 0:TC],
                                                 wxp_sb[:, kk, :], ln1_fm[:, kk, :],
                                                 start=(kk == 0), stop=(kk == 15))
                        sPE.inc(last)
                    em("tensor", _xp)
                    snap(f"xpmm_{c}", sPE)

                    def _xpe(scalar, c=c, hold=holdx):
                        wait(scalar, f"xpmm_{c}")
                        if c >= 2:
                            wait(scalar, f"bcst_{c - 2}")
                        scalar.activation(out=bc_sb[c % 2][:],
                                          in_=ps[hold["b"]][0:2 * NST, 0:TC],
                                          func=AF.Identity, bias=xpb_sb[:])
                    em("scalar", _xpe)
                    snap(f"bcev_{c}", sSC)

                    def _givexp(hold=holdx, key=f"bcev_{c}"):
                        pool.give(hold["b"], key)
                    emc(_givexp)

                    def _bcst(gpsimd, c=c):
                        wait(gpsimd, f"bcev_{c}")
                        sBc.inc(gpsimd.dma_start(out=bc_d.ap()[c % 2],
                                                 in_=bc_sb[c % 2][:]), 16)
                    em("gpsimd", _bcst)
                    snap(f"bcst_{c}", sBc)

                m0 = 2 * (piece - 4)
                for m in (m0, m0 + 1):
                    gdm = c * 8 + m

                    def _wdl(gpsimd, c=c, m=m, gdm=gdm):
                        if gdm >= 2:
                            wait(gpsimd, f"dtmm_{gdm - 2}")
                        sWdt[m % 2].inc(gpsimd.dma_start(
                            out=wdtb[m % 2][:].rearrange('p k c -> p (k c)'),
                            in_=wdt[m].rearrange('p k c -> p (k c)')), 16)
                    em("gpsimd", _wdl)
                    snap(f"wdtl_{gdm}", sWdt[m % 2])

                    hold = {}

                    def _mm(tensor, c=c, m=m, gdm=gdm, hold=hold):
                        wait(tensor, f"wdtl_{gdm}")
                        wait(tensor, f"ln1fm_{c}")
                        b = hold.setdefault("b", pool.take(tensor))
                        if pool.replay and "taken2" not in hold:
                            hold["taken2"] = True
                        last = None
                        for kk in range(16):
                            last = tensor.matmul(ps[b][:, 0:TC],
                                                 wdtb[m % 2][:, kk, :],
                                                 ln1_fm[:, kk, :],
                                                 start=(kk == 0), stop=(kk == 15))
                        sPE.inc(last)
                    em("tensor", _mm)
                    snap(f"dtmm_{gdm}", sPE)

                    def _sp(scalar, c=c, m=m, gdm=gdm, hold=hold):
                        wait(scalar, f"dtmm_{gdm}")
                        if c >= 1 and m == 0:
                            wait(scalar, f"udone_a_{c - 1}")
                        if c >= 1 and m == 4:
                            wait(scalar, f"udone_b_{c - 1}")
                        b = hold["b"]
                        e = scr[:, (m % 2) * 256:(m % 2) * 256 + 256]
                        scalar.activation(out=e, in_=ps[b][:, 0:TC], func=AF.Exp,
                                          bias=dtb_sb[:, m:m + 1])
                        scalar.activation(out=dt_sb[:, m, :], in_=e, func=AF.Ln,
                                          bias=ct[:, C_ONE:C_ONE + 1])
                    em("scalar", _sp)
                    snap(f"dtev_{gdm}", sSC)

                    def _give(hold=hold, key=f"dtev_{gdm}"):
                        pool.give(hold["b"], key)
                    emc(_give)

                if piece == 6:
                    def _ua(vector, c=c):
                        vector.set_lag(1)
                        wait(vector, f"dtev_{c * 8 + 3}")
                        vector.tensor_tensor(
                            out=u_sb[:, 0:4, :].rearrange("p m t -> p (m t)"),
                            in0=dt_sb[:, 0:4, :].rearrange("p m t -> p (m t)"),
                            in1=ln1_fm[:, 0:4, :].rearrange("p m t -> p (m t)"),
                            op=ALU.mult)
                    em("vector", _ua)
                    snap(f"udone_a_{c}", sVE)

                if piece == 7:
                    def _ub(vector, c=c):
                        vector.set_lag(1)
                        wait(vector, f"dtev_{c * 8 + 7}")
                        vector.tensor_tensor(
                            out=u_sb[:, 4:8, :].rearrange("p m t -> p (m t)"),
                            in0=dt_sb[:, 4:8, :].rearrange("p m t -> p (m t)"),
                            in1=ln1_fm[:, 4:8, :].rearrange("p m t -> p (m t)"),
                            op=ALU.mult)
                    em("vector", _ub)
                    snap(f"udone_b_{c}", sVE)

                    def _bcb(gpsimd, c=c):
                        wait(gpsimd, f"bcst_{c}")
                        if c >= 1:
                            wait(gpsimd, f"scanchunk_{c - 1}")
                        base_el = (c % 2) * 2 * NST * TC
                        repb = bass.AP(bc_d.ap().tensor, base_el,
                                       [[0, 128], [TC, NST], [1, TC]])
                        repc = bass.AP(bc_d.ap().tensor, base_el + NST * TC,
                                       [[0, 128], [TC, NST], [1, TC]])
                        sBcb.inc(gpsimd.dma_start(out=b_bc[:], in_=repb), 16)
                        sBcb.inc(gpsimd.dma_start(out=c_bc[:], in_=repc), 16)
                    em("gpsimd", _bcb)
                    snap(f"bcast_{c}", sBcb)

        def emit_A(c):
            for piece in range(8):
                emit_A_piece(c, piece)

        def emit_scan_m(c, m):
            gm = c * 8 + m

            def mk_da2(np_):
                n0 = 2 * np_
                i0 = c * 128 + m * 16 + n0

                def _da2(scalar, c=c, m=m, n0=n0, i0=i0):
                    if i0 == 0:
                        wait(scalar, "prep")
                    for j in range(2):
                        i = i0 + j
                        n = n0 + j
                        if i >= 6:
                            wait(scalar, f"scan_{i - 6}")
                        scalar.activation(
                            out=da_sb[i % 6][:], in_=dt_sb[:, m, :], func=AF.Exp,
                            scale=a_sb[:, m * 16 + n:m * 16 + n + 1])
                em("scalar", _da2)
                snap(f"da_{i0}", sSC)
                snap(f"da_{i0 + 1}", sSC)

            def mk_sc2(np_):
                n0 = 2 * np_
                i0 = c * 128 + m * 16 + n0

                def _sc2(vector, c=c, m=m, n0=n0, i0=i0):
                    vector.set_lag(1 if n0 == 0 else 2)
                    wait(vector, f"da_{i0 + 1}")
                    for j in range(2):
                        n = n0 + j
                        k = m * 16 + n
                        init = 0.0 if c == 0 else hstate[:, k:k + 1]
                        vector.tensor_tensor_scan(
                            out=h_all[:, n, :], data0=da_sb[(i0 + j) % 6][:],
                            data1=h_all[:, n, :], initial=init,
                            op0=ALU.mult, op1=ALU.add)
                em("vector", _sc2)
                snap(f"scan_{i0}", sVE)
                snap(f"scan_{i0 + 1}", sVE)

            for p in range(3):
                mk_da2(p)

            def _dbx(vector, c=c, m=m):
                vector.set_lag(1)
                if m == 0:
                    wait(vector, f"bcast_{c}")
                    wait(vector, f"udone_a_{c}")
                elif m == 4:
                    wait(vector, f"udone_b_{c}")
                urep = bass.AP(u_sb[:].tensor, m * TC,
                               [[8 * TC, 128], [0, NST], [1, TC]])
                vector.tensor_tensor(out=h_all[:], in0=b_bc[:], in1=urep,
                                     op=ALU.mult)
            em("vector", _dbx)

            for p in range(8):
                mk_sc2(p)
                if p + 3 <= 7:
                    mk_da2(p + 3)

            def _tail(vector, c=c, m=m, gm=gm):
                vector.set_lag(1)
                vector.tensor_copy(
                    hstate[:, m * 16:(m + 1) * 16],
                    h_all[:, :, TC - 1:TC].rearrange("p n o -> p (n o)"))
                vector.tensor_tensor(out=h_all[:], in0=h_all[:], in1=c_bc[:],
                                     op=ALU.mult)
                if gm >= 1:
                    wait(vector, f"ycst_{gm - 1}")
                vector.tensor_tensor(out=h_all[:, 0:8, :], in0=h_all[:, 0:8, :],
                                     in1=h_all[:, 8:16, :], op=ALU.add)
                vector.tensor_tensor(out=h_all[:, 0:4, :], in0=h_all[:, 0:4, :],
                                     in1=h_all[:, 4:8, :], op=ALU.add)
                vector.tensor_tensor(out=h_all[:, 0:2, :], in0=h_all[:, 0:2, :],
                                     in1=h_all[:, 2:4, :], op=ALU.add)
                vector.tensor_tensor(out=y32[:], in0=h_all[:, 0, :],
                                     in1=h_all[:, 1, :], op=ALU.add)
            em("vector", _tail)
            snap(f"yred_{gm}", sVE)

            def _yc(scalar, c=c, m=m, gm=gm):
                wait(scalar, f"yred_{gm}")
                if c >= 2 and m == 0:
                    wait(scalar, f"opdone_{c - 2}")
                scalar.copy(y_bf[c % 2][:, m, :], y32[:])
            em("scalar", _yc)
            snap(f"ycst_{gm}", sSC)

        def emit_outproj(cc):
            for mo in range(8):
                def _wl(gpsimd, cc=cc, mo=mo):
                    gi4 = cc * 8 + mo
                    if gi4 >= 2:
                        wait(gpsimd, f"opmm_{cc}_{(mo - 2) * 2 + 1}" if mo >= 2
                             else f"opmm_{cc - 1}_{(mo + 6) * 2 + 1}")
                    sWo[mo % 2].inc(gpsimd.dma_start(
                        out=woutb[mo % 2][:].rearrange('p k c -> p (k c)'),
                        in_=wout[mo].rearrange('p k c -> p (k c)')), 16)
                em("gpsimd", _wl)
                snap(f"opwl_{cc}_{mo}", sWo[mo % 2])

                for tt in range(2):
                    gi = mo * 2 + tt

                    holdg = {}

                    def _g(tensor, cc=cc, tt=tt, mo=mo, gi=gi, hold=holdg):
                        if gi == 0:
                            wait(tensor, f"ycast_{cc}")
                            wait(tensor, "prep")
                        if tt == 0:
                            wait(tensor, f"opwl_{cc}_{mo}")
                        b = hold.setdefault("b", pool.take(tensor))
                        last = None
                        for kk in range(8):
                            last = tensor.matmul(
                                ps[b][:, 0:256],
                                y_bf[cc % 2][:, kk, tt * 128:(tt + 1) * 128],
                                woutb[mo % 2][:, kk, :],
                                start=(kk == 0), stop=(kk == 7))
                        sPE.inc(last)
                    em("tensor", _g)
                    snap(f"opmm_{cc}_{gi}", sPE)

                    def _e(vector, cc=cc, tt=tt, mo=mo, gi=gi, hold=holdg):
                        vector.set_lag(1)
                        wait(vector, f"opmm_{cc}_{gi}")
                        if gi == 0 and cc > 0:
                            wait(vector, f"mixdma_{cc - 1}")
                        vector.tensor_tensor(
                            out=mixp_sb[:, tt, mo * 256:(mo + 1) * 256],
                            in0=ps[hold["b"]][:, 0:256],
                            in1=outbh_sb[:, mo * 256:(mo + 1) * 256], op=ALU.add)
                    em("vector", _e)
                    snap(f"opev_{cc}_{gi}", sVE)

                    def _giveop(hold=holdg, key=f"opev_{cc}_{gi}"):
                        pool.give(hold["b"], key)
                    emc(_giveop)
                if mo < 7:
                    piece_break()
            snap(f"mixevac_{cc}", sVE)
            snap(f"opdone_{cc}", sPE)

            def _d(gpsimd, cc=cc):
                wait(gpsimd, f"mixevac_{cc}")
                dst = mixp_d.ap()[cc * TC:(cc + 1) * TC, :].rearrange(
                    "(tt p) m -> p tt m", p=128)
                sMixp.inc(gpsimd.dma_start(out=dst, in_=mixp_sb[:]), 16)
            em("gpsimd", _d)
            snap(f"mixdma_{cc}", sMixp)

            if cc % 2 == 1:
                gg = cc // 2

                def _rs(gpsimd, cc=cc, gg=gg):
                    wait(gpsimd, f"mixdma_{cc}")
                    sCC.inc(gpsimd.collective_compute(
                        "ReduceScatter", ALU.add,
                        replica_groups=[[0, 1], [2, 3], [4, 5], [6, 7]],
                        ins=[mixp_d.ap()[gg * 512:(gg + 1) * 512, :]],
                        outs=[mixg_d.ap()[gg]]))
                em("gpsimd", _rs)
                snap(f"rs_{gg}", sCC)

        def emit_C_light(g):
            for tt in range(2):
                ti = g * 2 + tt

                def _gldx(gpsimd, g=g, tt=tt, ti=ti):
                    if ti >= 1:
                        wait(gpsimd, f"x1_{(ti - 1) // 2}_{(ti - 1) % 2}")
                    sX0m.inc(gpsimd.dma_start(
                        out=x0m_t[:],
                        in_=x0m[g][tt * 128:(tt + 1) * 128, :]), 16)
                em("gpsimd", _gldx)
                snap(f"gldx_{ti}", sX0m)

                def _gld(gpsimd, g=g, tt=tt, ti=ti):
                    wait(gpsimd, f"rs_{g}")
                    if ti >= 1:
                        wait(gpsimd, f"x1_{(ti - 1) // 2}_{(ti - 1) % 2}")
                    sMg.inc(gpsimd.dma_start(
                        out=mixg_t[:],
                        in_=mixg_d.ap()[g][tt * 128:(tt + 1) * 128, :]), 16)
                em("gpsimd", _gld)
                snap(f"gldm_{ti}", sMg)

                def _ws(scalar, ti=ti, k1=f"gldm_{ti}"):
                    wait(scalar, k1)
                    if ti >= 1:
                        wait(scalar, f"x1_{(ti - 1) // 2}_{(ti - 1) % 2}")
                em("scalar", _ws)
                slot, li = emit_ln_stats(lambda: mixg_t[:], fam=1)
                piece_break()

                def _x0p(scalar, ti=ti, slot=slot):
                    wait(scalar, f"gldx_{ti}")
                    scalar.activation(out=x0m_t[:], in_=x0m_t[:],
                                      func=AF.Identity,
                                      bias=snegmr[:, slot:slot + 1])
                em("scalar", _x0p)
                snap(f"x0p_{ti}", sSC)

                def _x1v(vector, g=g, tt=tt, ti=ti, slot=slot):
                    vector.set_lag(1)
                    wait(vector, f"x0p_{ti}")
                    if ti >= 1:
                        pg, pt = (ti - 1) // 2, (ti - 1) % 2
                        wait(vector, f"ln2tm_{pg}_{pt}")
                        wait(vector, f"x1st_{pg}_{pt}")
                    vector.scalar_tensor_tensor(
                        out=x1_sb[:], in0=mixg_t[:],
                        scalar=srstd[:, slot:slot + 1], in1=x0m_t[:],
                        op0=ALU.mult, op1=ALU.add)
                em("vector", _x1v)
                snap(f"x1_{g}_{tt}", sVE)

                def _x1st(gpsimd, g=g, tt=tt):
                    wait(gpsimd, f"x1_{g}_{tt}")
                    sX1s[tt].inc(gpsimd.dma_start(
                        out=out[g][tt * 128:(tt + 1) * 128, :],
                        in_=x1_sb[:]), 16)
                em("gpsimd", _x1st)
                snap(f"x1st_{g}_{tt}", sX1s[tt])

                def _ws2(scalar, k=f"x1_{g}_{tt}"):
                    wait(scalar, k)
                em("scalar", _ws2)
                slot2, li2 = emit_ln_stats(lambda: x1_sb[:], fam=1)

                def _nm2(scalar, g=g, tt=tt, slot=slot2, li=li2):
                    wait(scalar, f"lnstat_done_{li}")
                    if g >= 1:
                        wait(scalar, f"ln2tr_{g - 1}")
                    scalar.activation(out=ln2_tm[:, tt, :],
                                      in_=x1_sb[:],
                                      func=AF.Identity,
                                      scale=srstd[:, slot:slot + 1],
                                      bias=snegmr[:, slot:slot + 1])
                em("scalar", _nm2)
                snap(f"ln2tm_{g}_{tt}", sSC)
                ln_norm_key[li2] = f"ln2tm_{g}_{tt}"
                piece_break()

            for tt in range(2):
                emit_transposes(
                    (lambda dc, tt=tt: ln2_tm[:, tt, dc * 128:(dc + 1) * 128]),
                    (lambda grp, tt=tt: ln2_fm[:, grp * 8:grp * 8 + 8,
                                               tt * 128:(tt + 1) * 128]),
                    f"ln2tm_{g}_{tt}",
                    f"ffdone_{g - 1}" if (tt == 0 and g >= 1) else None,
                    fam=1)
            snap(f"ln2tr_{g}", sPE)
            snap(f"ln2fm_{g}", sVE)

            def _rsum(tensor, g=g):
                wait(tensor, f"ln2fm_{g}")
                if g >= 1:
                    wait(tensor, f"shnev_{g - 1}")
                last = None
                for kk in range(16):
                    last = tensor.matmul(psr[0:1, 0:TG], onesc_sb[:, 0:1],
                                         ln2_fm[:, kk, :],
                                         start=(kk == 0), stop=(kk == 15))
                sPE.inc(last)
            em("tensor", _rsum)
            snap(f"rsum_{g}", sPE)

            def _rsev(scalar, g=g):
                wait(scalar, f"rsum_{g}")
                scalar.activation(out=rsn_sb[:], in_=psr[0:1, 0:TG],
                                  func=AF.Identity,
                                  scale=ct[0:1, C_NEG128:C_NEG128 + 1])
            em("scalar", _rsev)
            snap(f"rsnev_{g}", sSC)

        def emit_C_heavy(g):
            def em_wld(ic):
                gic = g * 64 + ic

                def _wld(sync, g=g, ic=ic, gic=gic):
                    if gic == 0:
                        wait(sync, "precast")
                    if gic >= 2:
                        wait(sync, f"gumm_{gic - 2}")
                    sGU[ic % 2].inc(sync.dma_start(
                        out=wgb[ic % 2][:].rearrange('p k c -> p (k c)'),
                        in_=wg_bf.ap()[ic].rearrange('p k c -> p (k c)')), 16)
                    sGU[ic % 2].inc(sync.dma_start(
                        out=wub[ic % 2][:].rearrange('p k c -> p (k c)'),
                        in_=wu_bf.ap()[ic].rearrange('p k c -> p (k c)')), 16)
                em("sync", _wld)
                snap(f"wld_{gic}", sGU[ic % 2])

            def em_gu(ic, hold):
                gic = g * 64 + ic

                def _gu(tensor, g=g, ic=ic, gic=gic, hold=hold):
                    wait(tensor, f"wld_{gic}")
                    if ic == 0:
                        wait(tensor, f"rsnev_{g}")
                    b = hold.setdefault("b", pool.take(tensor))
                    if ic >= SHLAG:
                        wait(tensor, f"hh_{g}_{ic - SHLAG}")
                        tensor.matmul(psr[0:1, TG:2 * TG], onesc_sb[:, 0:1],
                                      hh_sb[:, ic - SHLAG, :],
                                      start=(ic - SHLAG == 0), stop=False)
                    for kk in range(16):
                        tensor.matmul(ps[b][:, 0:TG], wgb[ic % 2][:, kk, :],
                                      ln2_fm[:, kk, :], start=(kk == 0), stop=False)
                    tensor.matmul(ps[b][:, 0:TG], onesr_sb[0:1, 0:128],
                                  rsn_sb[0:1, :], start=False, stop=True)
                    for kk in range(16):
                        tensor.matmul(ps[b][:, TG:2 * TG], wub[ic % 2][:, kk, :],
                                      ln2_fm[:, kk, :], start=(kk == 0), stop=False)
                    sPE.inc(tensor.matmul(ps[b][:, TG:2 * TG], onesr_sb[0:1, 0:128],
                                          rsn_sb[0:1, :], start=False, stop=True))
                em("tensor", _gu)
                snap(f"gumm_{gic}", sPE)

            def em_guev(ic, hold):
                gic = g * 64 + ic

                def _guev(scalar, g=g, ic=ic, gic=gic, hold=hold):
                    wait(scalar, f"gumm_{gic}")
                    if ic >= 2:
                        wait(scalar, f"hh_{g}_{ic - 2}")
                    elif g >= 1:
                        wait(scalar, f"hh_{g - 1}_{62 + ic}")
                    b = hold["b"]
                    scalar.activation(out=g_b[ic % 2][:], in_=ps[b][:, 0:TG],
                                      func=AF.Silu, scale=gsc_sb[:, ic:ic + 1])
                em("scalar", _guev)
                snap(f"guev_{gic}", sSC)

            def em_hh(ic, hold):
                gic = g * 64 + ic

                def _hh(vector, g=g, ic=ic, gic=gic, hold=hold):
                    vector.set_lag(1)
                    wait(vector, f"guev_{gic}")
                    if ic == 0 and g >= 1:
                        wait(vector, f"downdone_{g - 1}")
                    b = hold["b"]
                    vector.scalar_tensor_tensor(
                        out=hh_sb[:, ic, :], in0=ps[b][:, TG:2 * TG],
                        scalar=usc_sb[:, ic:ic + 1], in1=g_b[ic % 2][:],
                        op0=ALU.mult, op1=ALU.mult)
                em("vector", _hh)
                snap(f"hh_{g}_{ic}", sVE)

                def _givegu(hold=hold, key=f"hh_{g}_{ic}"):
                    pool.give(hold["b"], key)
                emc(_givegu)

            for p0 in range(0, 64, 5):
                ics = list(range(p0, min(p0 + 5, 64)))
                for grp in (ics[0:2], ics[2:4], ics[4:5]):
                    if not grp:
                        continue
                    holds = {ic: {} for ic in grp}
                    for ic in grp:
                        em_wld(ic)
                    for ic in grp:
                        em_gu(ic, holds[ic])
                    for ic in grp:
                        em_guev(ic, holds[ic])
                    for ic in grp:
                        em_hh(ic, holds[ic])
                piece_break()

            def _shf(tensor, g=g):
                last = None
                for ic in range(64 - SHLAG, 64):
                    wait(tensor, f"hh_{g}_{ic}")
                    last = tensor.matmul(psr[0:1, TG:2 * TG], onesc_sb[:, 0:1],
                                         hh_sb[:, ic, :], start=False,
                                         stop=(ic == 63))
                sPE.inc(last)
            em("tensor", _shf)
            snap(f"shmm_{g}", sPE)
            snap(f"ffdone_{g}", sPE)

            def _shev(scalar, g=g):
                wait(scalar, f"shmm_{g}")
                scalar.activation(out=shn_sb[:], in_=psr[0:1, TG:2 * TG],
                                  func=AF.Identity,
                                  scale=ct[0:1, C_NEG128:C_NEG128 + 1])
            em("scalar", _shev)
            snap(f"shnev_{g}", sSC)

            bdown = []

            def _takedown(tensor, g=g, bd=bdown):
                if not pool.replay or len(bd) == 0:
                    bd.append(pool.take(tensor))
                    bd.append(pool.take(tensor))
                else:
                    pool.take(tensor)
                    pool.take(tensor)
            em("tensor", _takedown)
            for ho in range(4):
                gho = g * 4 + ho
                for icg in range(16):
                    gw = gho * 16 + icg

                    def _wdl(sync, ho=ho, icg=icg, gw=gw):
                        if gw >= 2:
                            wait(sync, f"wdmm_{gw - 2}")
                        if gw == 0:
                            wait(sync, "precast")
                        src = wd_bf.ap()[ho, icg * 4:(icg + 1) * 4].rearrange(
                            "u p c -> p u c")
                        sWd[icg % 2].inc(sync.dma_start(out=wdb[icg % 2][:],
                                                        in_=src), 16)
                    em("sync", _wdl)
                    snap(f"wdl_{gw}", sWd[icg % 2])

                    def _dmm(tensor, g=g, ho=ho, icg=icg, gw=gw, gho=gho, bd=bdown):
                        wait(tensor, f"wdl_{gw}")
                        if icg == 0:
                            wait(tensor, f"shnev_{g}")
                            wait(tensor, f"hh_{g}_{63}")
                            if ho >= 1:
                                wait(tensor, f"dnev_{gho - 1}")
                        last = None
                        for u in range(4):
                            ic = icg * 4 + u
                            for tt in range(2):
                                last = tensor.matmul(
                                    ps[bd[tt]][:],
                                    hh_sb[:, ic, tt * 128:(tt + 1) * 128],
                                    wdb[icg % 2][:, u, :],
                                    start=(ic == 0), stop=False)
                        if icg == 15:
                            for tt in range(2):
                                last = tensor.matmul(
                                    ps[bd[tt]][:],
                                    shn_sb[0:1, tt * 128:(tt + 1) * 128],
                                    onesr_sb[0:1, 0:512], start=False, stop=True)
                        sPE.inc(last)
                    em("tensor", _dmm)
                    snap(f"wdmm_{gw}", sPE)

                def _dnev(vector, g=g, ho=ho, gho=gho, bd=bdown):
                    vector.set_lag(1)
                    wait(vector, f"wdmm_{gho * 16 + 15}")
                    if g == 0 and ho == 0:
                        wait(vector, "pre")
                    if ho == 0 and g >= 1:
                        wait(vector, f"outdma_{g - 1}")
                    for tt in range(2):
                        vector.tensor_tensor(
                            out=ffn32[:, tt, ho * 512:(ho + 1) * 512],
                            in0=ps[bd[tt]][:],
                            in1=dsc_sb[:, ho * 512:(ho + 1) * 512], op=ALU.mult)
                em("vector", _dnev)
                snap(f"dnev_{gho}", sVE)
                if ho in (0, 2):
                    piece_break()

            def _givedown(bd=bdown, key=f"dnev_{g * 4 + 3}"):
                pool.give(bd[0], key)
                pool.give(bd[1], key)
            emc(_givedown)
            snap(f"downdone_{g}", sVE)
            snap(f"ffnout_{g}", sVE)

            for tt in range(2):
                def _ws3(scalar, k=f"ffnout_{g}"):
                    wait(scalar, k)
                em("scalar", _ws3)
                slot3, li3 = emit_ln_stats(lambda g=g, tt=tt: ffn32[:, tt, :],
                                           fam=2)

                def _nm3(scalar, g=g, tt=tt, slot=slot3, li=li3):
                    wait(scalar, f"lnstat_done_{li}")
                    scalar.activation(out=ffn32[:, tt, :], in_=ffn32[:, tt, :],
                                      func=AF.Identity,
                                      scale=srstd[:, slot:slot + 1],
                                      bias=snegmr[:, slot:slot + 1])
                em("scalar", _nm3)
                snap(f"ln3_{g}_{tt}", sSC)
                ln_norm_key[li3] = f"ln3_{g}_{tt}"

            def _ost(gpsimd, g=g):
                wait(gpsimd, f"ln3_{g}_0")
                wait(gpsimd, f"ln3_{g}_1")
                wait(gpsimd, f"x1st_{g}_0")
                wait(gpsimd, f"x1st_{g}_1")
                dst = out[g].rearrange("(tt p) m -> p tt m", p=128)
                sST.inc(gpsimd.dma_start(out=dst, in_=ffn32[:],
                                         accum_op=ALU.add), 16)
            em("gpsimd", _ost)
            snap(f"outdma_{g}", sST)

        emit_A(0)
        lq = []
        hq = []
        ldone = [-1]

        def pop_lanes():
            if lq:
                g, p = lq.pop(0)
                p()
                if not lq or lq[0][0] != g:
                    ldone[0] = g
            if hq and hq[0][0] <= ldone[0]:
                hq.pop(0)[1]()

        def enqueue_C(g):
            lq.extend((g, p) for p in record_pieces(
                lambda g=g: emit_C_light(g)))
            hq.extend((g, p) for p in record_pieces(
                lambda g=g: emit_C_heavy(g)))

        for c in range(NT):
            opp = (record_pieces(lambda cc=c - 1: emit_outproj(cc))
                   if c >= 1 else None)
            if c >= 3 and (c - 3) % 2 == 0:
                enqueue_C((c - 3) // 2)
            for m in range(8):
                emit_scan_m(c, m)
                if m == 7:
                    snap(f"scanchunk_{c}", sVE)
                    snap(f"ycast_{c}", sSC)
                emit_A_piece(c + 1, m)
                if opp is not None:
                    opp[m]()
                pop_lanes()
        opp = record_pieces(lambda: emit_outproj(NT - 1))
        for m in range(8):
            opp[m]()
            pop_lanes()
        enqueue_C(7)
        while lq or hq:
            pop_lanes()

        def _final(sync):
            sync.wait_ge(sST.h, sST.n)
        em("sync", _final)

        mock = _MockEng()
        mock_sc = ChainEng(mock, sSC)
        mock_ve = ChainEng(mock, sVE)
        for eng_name, payload in order:
            if eng_name == "__snap__":
                key, sem = payload
                ev[key] = (sem, sem.n)
            elif eng_name == "__call__":
                payload()
            elif eng_name == "scalar":
                payload(mock_sc)
            elif eng_name == "vector":
                payload(mock_ve)
            else:
                payload(mock)
        for s_ in ALL_SEMS:
            s_.n = 0
        pool.replay = True
        pool.ri = 0

        blk.sync(lambda eng: [f(eng) for f in sched["sync"]])
        blk.gpsimd(lambda eng: [f(eng) for f in sched["gpsimd"]])
        blk.scalar(lambda eng: [f(ChainEng(eng, sSC)) for f in sched["scalar"]])
        blk.vector(lambda eng: [f(ChainEng(eng, sVE)) for f in sched["vector"]])
        blk.tensor(lambda eng: [f(eng) for f in sched["tensor"]])

    return nc


def _prep_inputs(k, x, A_log, x_proj_w, dt_proj_w, dt_proj_b, out_proj_w, shared):
    b, j = k >> 1, k & 1
    d0, d1 = j * DH, (j + 1) * DH
    perm = np.concatenate([np.arange(d0, d1), np.arange((1 - j) * DH, (2 - j) * DH)])

    xbv = np.ascontiguousarray(x[b][:, perm], dtype=np.float32)
    x0mv = np.ascontiguousarray(x[b].reshape(NG, 2, TG, D)[:, j], dtype=np.float32)

    wdtT = dt_proj_w.T
    wdt_k = np.ascontiguousarray(
        wdtT[perm][:, d0:d1].reshape(16, 128, 8, 128).transpose(2, 1, 0, 3),
        dtype=np.float32)
    wxp_k = np.ascontiguousarray(
        x_proj_w.T[perm].reshape(16, 128, 2 * NST).transpose(1, 0, 2),
        dtype=np.float32)
    wout_k = np.ascontiguousarray(
        out_proj_w.T[d0:d1].reshape(8, 128, 8, 256).transpose(2, 1, 0, 3),
        dtype=np.float32)
    alog_k = np.ascontiguousarray(
        A_log[d0:d1].reshape(8, 128, NST).transpose(1, 0, 2).reshape(128, 128),
        dtype=np.float32)
    dtb_k = np.ascontiguousarray(dt_proj_b[d0:d1].reshape(8, 128).T,
                                 dtype=np.float32)

    return dict(xb=xbv, x0m=x0mv, wdt=wdt_k, wxp=wxp_k, wout=wout_k, alog=alog_k,
                dtb=dtb_k, **shared)


def kernel(L0_attn_x, L0_attn_A_log, L0_attn_x_proj_w, L0_attn_x_proj_b,
           L0_attn_dt_proj_w, L0_attn_dt_proj_b, L0_attn_out_proj_w,
           L0_attn_out_proj_b, L0_ffn_gate_w, L0_ffn_gate_scale, L0_ffn_up_w,
           L0_ffn_up_scale, L0_ffn_down_w, L0_ffn_down_scale):
    x = np.asarray(L0_attn_x, dtype=np.float32)
    A_log = np.asarray(L0_attn_A_log, dtype=np.float32)
    x_proj_w = np.asarray(L0_attn_x_proj_w, dtype=np.float32)
    x_proj_b = np.asarray(L0_attn_x_proj_b, dtype=np.float32)
    dt_proj_w = np.asarray(L0_attn_dt_proj_w, dtype=np.float32)
    dt_proj_b = np.asarray(L0_attn_dt_proj_b, dtype=np.float32)
    out_proj_w = np.asarray(L0_attn_out_proj_w, dtype=np.float32)
    out_proj_b = np.asarray(L0_attn_out_proj_b, dtype=np.float32)
    gate_w = np.asarray(L0_ffn_gate_w)
    up_w = np.asarray(L0_ffn_up_w)
    down_w = np.asarray(L0_ffn_down_w)
    gate_scale = np.asarray(L0_ffn_gate_scale, dtype=np.float32)
    up_scale = np.asarray(L0_ffn_up_scale, dtype=np.float32)
    down_scale = np.asarray(L0_ffn_down_scale, dtype=np.float32)

    wg8v = np.ascontiguousarray(
        gate_w.astype(np.uint8).T.reshape(16, 128, 64, 128).transpose(2, 1, 0, 3))
    wu8v = np.ascontiguousarray(
        up_w.astype(np.uint8).T.reshape(16, 128, 64, 128).transpose(2, 1, 0, 3))
    wd8v = np.ascontiguousarray(
        down_w.astype(np.uint8).T.reshape(64, 128, 4, 512).transpose(2, 0, 1, 3))
    shared = dict(
        xpb=np.ascontiguousarray(x_proj_b.reshape(2 * NST, 1)),
        outb_bc=np.ascontiguousarray(np.broadcast_to(out_proj_b, (128, D))),
        wg8=wg8v, wu8=wu8v, wd8=wd8v,
        gsc=np.ascontiguousarray(gate_scale.reshape(64, 128).T),
        usc=np.ascontiguousarray(up_scale.reshape(64, 128).T),
        dsc_bc=np.ascontiguousarray(np.broadcast_to(down_scale, (128, D))),
        ident=np.eye(128, dtype=BF),
        onesc=np.ones((128, 1), dtype=BF),
        onesr=np.ones((1, 512), dtype=BF),
        consts=np.ascontiguousarray(np.broadcast_to(np.array(
            [1e-5, 1.0 / D, -1.0, 1.0, -0.5, -128.0, 0.5, 0.0], np.float32),
            (128, NCONST))),
    )

    nc = build_nc()
    in_maps = [_prep_inputs(k, x, A_log, x_proj_w, dt_proj_w, dt_proj_b,
                            out_proj_w, shared) for k in range(8)]
    res = run_bass_kernel_spmd(nc, in_maps, core_ids=list(range(8)))

    outv = np.empty((B, L, D), dtype=np.float32)
    for k in range(8):
        b, j = k >> 1, k & 1
        outv[b].reshape(NG, 2, TG, D)[:, j] = res.results[k]["out"]
    return outv


if __name__ == "__main__":
    nc = build_nc()
    print("built; instructions:", nc.n_instructions(),
          "sbuf remaining:", nc.sbuf_bytes_remaining)
